# version 25
# baseline (speedup 1.0000x reference)
import contextlib
import numpy as np
import ml_dtypes

import concourse.bass as bass
import concourse.mybir as mybir
from concourse.bass_utils import run_bass_kernel_spmd

F32 = mybir.dt.float32
BF16 = mybir.dt.bfloat16
AF = mybir.ActivationFunctionType
ALU = mybir.AluOpType
AX = mybir.AxisListType

B, S, D = 4, 2048, 2048
TPC = 1024
NCH = D // 128
NEG = -1.0e30

BF = ml_dtypes.bfloat16


class Tok:
    __slots__ = ("sem", "val")

    def __init__(self, sem, val):
        self.sem, self.val = sem, val


class Emitter:

    ENGINES = ("tensor", "vector", "scalar", "gpsimd", "sync")

    def __init__(self, nc):
        self.nc = nc
        self.ctx = contextlib.ExitStack()
        self.items = {e: [] for e in self.ENGINES}
        self.sems = {}
        self.counts = {}
        self.hw = {e: {} for e in self.ENGINES}
        for e in ("tensor", "vector", "scalar", "gpsimd"):
            self._mksem("c_" + e)
        self.bank_w = [None] * 8
        self.bank_r = [[] for _ in range(8)]

    def _mksem(self, name):
        h = self.ctx.enter_context(self.nc.semaphore(name))
        self.sems[name] = h
        self.counts[name] = 0
        return name

    def dma_slot(self, name):
        if name not in self.sems:
            self._mksem(name)
        return name

    def emit(self, eng, fn, deps=(), inc="auto", inc_by=1):
        waits = []
        hw = self.hw[eng]
        for t in deps:
            if t is None:
                continue
            if hw.get(t.sem, 0) < t.val:
                waits.append((t.sem, t.val))
                hw[t.sem] = t.val
        semname = "c_" + eng if inc == "auto" else inc
        if semname is not None:
            self.counts[semname] += inc_by
            tok = Tok(semname, self.counts[semname])
            inc_pair = (semname, inc_by)
        else:
            tok, inc_pair = None, None
        self.items[eng].append((waits, fn, inc_pair))
        return tok

    def wait(self, eng, deps):
        hw = self.hw[eng]
        for t in deps:
            if t is None:
                continue
            if hw.get(t.sem, 0) < t.val:
                self.items[eng].append(([(t.sem, t.val)], None, None))
                hw[t.sem] = t.val

    def bank_write_deps(self, banks):
        deps = []
        for b in banks:
            deps.extend(self.bank_r[b])
            if self.bank_w[b] is not None:
                deps.append(self.bank_w[b])
        return deps

    def bank_set_writer(self, banks, tok):
        for b in banks:
            self.bank_w[b] = tok
            self.bank_r[b] = []

    def bank_read(self, banks, tok):
        for b in banks:
            self.bank_r[b].append(tok)

    def finalize(self, final_deps):
        self.wait("sync", list(final_deps))
        nc = self.nc
        sems = self.sems
        with nc.Block() as blk:
            for engname in self.ENGINES:
                items = self.items[engname]
                if not items:
                    continue
                section = getattr(blk, engname)

                def body(e, items=items):
                    for waits, fn, inc_pair in items:
                        for semname, val in waits:
                            e.wait_ge(sems[semname], val)
                        if fn is not None:
                            ins = fn(e)
                            if inc_pair is not None:
                                ins.then_inc(sems[inc_pair[0]], inc_pair[1])

                section(body)
        self.ctx.close()


class K:
    def __init__(self, nc):
        self.nc = nc
        self.em = Emitter(nc)
        self.ctx = self.em.ctx
        self._names = set()
        self.ps = self.sb_raw("psum32", [128, 4096], F32, psum=True)
        self._dma_ctr = 0
        self._slot_last_use = {}
        self.consts = {}

    def sb_raw(self, name, shape, dtype, psum=False):
        assert name not in self._names, name
        self._names.add(name)
        f = self.nc.psum_tensor if psum else self.nc.sbuf_tensor
        return self.ctx.enter_context(f(name, shape, dtype))

    def mk_arenas(self, bf_cols, f32_cols):
        self.abf = self.sb_raw("arena_bf", [128, bf_cols], BF16).ap()
        self.af32 = self.sb_raw("arena_f32", [128, f32_cols], F32).ap()
        self.fused = True
        self._aoff = {"bf": 0, "f32": 0}
        self._amax = {"bf": 0, "f32": 0}

    def phase_reset(self):
        for kk in self._aoff:
            self._amax[kk] = max(self._amax[kk], self._aoff[kk])
            self._aoff[kk] = 0

    def phase_barrier(self):
        em = self.em
        toks = [Tok(name, em.counts[name]) for name in em.sems
                if em.counts[name] > 0]
        for eng in Emitter.ENGINES:
            em.wait(eng, toks)
        em.bank_w = [None] * 8
        em.bank_r = [[] for _ in range(8)]
        self.phase_reset()

    def sb(self, name, shape, dtype):
        if not getattr(self, "fused", False):
            return self.sb_raw(name, shape, dtype).ap()
        akey = "bf" if dtype == BF16 else "f32"
        arena = self.abf if akey == "bf" else self.af32
        off = self._aoff[akey]
        n = shape[1]
        na = (n + 15) & ~15
        self._aoff[akey] = off + na
        assert off + na <= arena.shape[-1], (name, akey, off + na)
        return arena[0:shape[0], off:off + n]

    def dram(self, name, shape, dtype, kind="Internal"):
        assert name not in self._names, name
        self._names.add(name)
        return self.nc.dram_tensor(name, shape, dtype, kind=kind).ap()

    def bank(self, i, n=1, rows=128):
        return self.ps[0:rows, i * 512:(i + n) * 512]

    def bank_cols(self, i, cols, rows=128):
        return self.ps[0:rows, i * 512:i * 512 + cols]

    def dma(self, queue, out, in_, slot=None, deps=(), **kw):
        if slot is None:
            self._dma_ctr += 1
            slot = f"d{queue[0]}{self._dma_ctr}"
        slot = self.em.dma_slot(slot)
        deps = list(deps) + self._slot_last_use.pop(slot, [])
        prev = self.em.counts[slot]
        if prev:
            deps.append(Tok(slot, prev))
        return self.em.emit(queue, lambda e: e.dma_start(out=out, in_=in_, **kw),
                            deps=deps, inc=slot, inc_by=16)

    def slot_use(self, slot_tok, consumer_tok):
        self._slot_last_use.setdefault(slot_tok.sem, []).append(consumer_tok)

    def mm_group(self, parts, banks, extra_deps=(), cont=False,
                 open_end=False):
        n = len(parts)
        tok = None
        bdeps = (self.em.bank_write_deps(banks) if not cont else []) \
            + list(extra_deps)
        for i, (out, lhsT, rhs, deps) in enumerate(parts):
            d = list(deps) + (bdeps if i == 0 else [])
            start = (i == 0 and not cont)
            stop = (i == n - 1) and not open_end
            last = i == n - 1
            tok = self.em.emit(
                "tensor",
                (lambda e, o=out, l=lhsT, r=rhs, st=start, sp=stop:
                 e.matmul(o, l, r, start=st, stop=sp)),
                deps=d,
                inc="auto" if last else None)
        self.em.bank_set_writer(banks, tok)
        return tok

    def transpose(self, out, in_, identity, banks, deps=()):
        deps = list(deps) + self.em.bank_write_deps(banks)
        tok = self.em.emit(
            "tensor", lambda e: e.transpose(out, in_, identity), deps=deps)
        self.em.bank_set_writer(banks, tok)
        return tok

    def v(self, fn, deps=(), rb=(), wb=()):
        return self._comp("vector", fn, deps, rb, wb)

    def s(self, fn, deps=(), rb=(), wb=()):
        return self._comp("scalar", fn, deps, rb, wb)

    def g(self, fn, deps=(), rb=(), wb=()):
        return self._comp("gpsimd", fn, deps, rb, wb)

    def _comp(self, eng, fn, deps, rb, wb):
        deps = list(deps)
        for b in rb:
            if self.em.bank_w[b] is not None:
                deps.append(self.em.bank_w[b])
        if wb:
            deps += self.em.bank_write_deps(wb)
        tok = self.em.emit(eng, fn, deps=deps)
        self.em.bank_read(rb, tok)
        if wb:
            self.em.bank_set_writer(wb, tok)
        return tok

    def setup_consts(self, ident32_ap):
        c = {}
        self.onesc = self.sb_raw("ones_col", [128, 3], F32).ap()
        self.onesr = self.sb_raw("ones_row", [1, 128], F32).ap()
        self.onesrb = self.sb_raw("ones_rowb", [1, 128], BF16).ap()
        self.id32 = self.sb_raw("ident32", [128, 128], F32).ap()
        t1 = self.v(lambda e: e.memset(self.onesc[:, 0:1], 1.0 / D))
        t2 = self.v(lambda e: e.memset(self.onesc[:, 1:2], 1.0))
        t2b = self.v(lambda e: e.memset(self.onesc[:, 2:3], 1.0 / 128.0))
        t3 = self.v(lambda e: e.memset(self.onesr[:, :], 1.0))
        t3b = self.v(lambda e: e.memset(self.onesrb[:, :], 1.0))
        t4 = self.dma("gpsimd", self.id32[:, :], ident32_ap, slot="ident")
        c["ones"] = [t1, t2, t2b, t3, t3b]
        c["ident"] = [t4]
        self.consts = c


def prep_fm(Wc):
    KD, FO = Wc.shape
    assert KD % 128 == 0 and FO % 128 == 0
    return np.ascontiguousarray(
        Wc.reshape(KD // 128, 128, FO // 128, 128).transpose(2, 1, 0, 3))


def prep_rhs(Wc):
    KD, FO = Wc.shape
    assert KD % 128 == 0
    return np.ascontiguousarray(Wc.reshape(KD // 128, 128, FO))


def emit_ln(k: K, xres, xres_toks, xn, scratch, rows):
    toks = []
    sq_tok = [None] * 16
    rows_war = []
    for tg in range(2):
        c0 = tg * 512
        mu_ps = k.bank_cols(0, 512, rows=1)
        sq_ps = k.bank_cols(1, 512, rows=1)
        k.mm_group(
            [(mu_ps, k.onesc[:, 0:1], xres[:, ch * TPC + c0: ch * TPC + c0 + 512],
              list(xres_toks)) for ch in range(NCH)],
            banks=[0], extra_deps=k.consts["ones"])
        for half in range(2):
            parts = []
            for ch in range(half * 8, half * 8 + 8):
                sl = ch % 8
                st = k.s(lambda e, ch=ch, sl=sl, c0=c0: e.activation(
                    out=scratch[:, sl * 512:(sl + 1) * 512],
                    in_=xres[:, ch * TPC + c0: ch * TPC + c0 + 512],
                    func=AF.Square),
                    deps=list(xres_toks) + rows_war
                    + ([sq_tok[sl]] if sq_tok[sl] else []))
                parts.append((sq_ps, k.onesc[:, 0:1],
                              scratch[:, sl * 512:(sl + 1) * 512], [st]))
            mmt = k.mm_group(parts, banks=[1], extra_deps=k.consts["ones"],
                             cont=(half == 1), open_end=(half == 0))
            for sl in range(8):
                sq_tok[sl] = mmt
        r_mu, r_msq, r_t, r_A, r_B = (rows[0:1, i * 512:(i + 1) * 512]
                                      for i in range(5))
        t1 = k.v(lambda e: e.tensor_copy(out=r_mu, in_=mu_ps), rb=[0],
                 deps=rows_war + [sq_tok[0]])
        t2 = k.v(lambda e: e.tensor_copy(out=r_msq, in_=sq_ps), rb=[1])
        t3 = k.v(lambda e: e.tensor_tensor(out=r_t, in0=r_mu, in1=r_mu,
                                           op=ALU.mult), deps=[t1])
        t4 = k.v(lambda e: e.scalar_tensor_tensor(
            out=r_t, in0=r_msq, scalar=1.0e-5, in1=r_t,
            op0=ALU.add, op1=ALU.subtract), deps=[t2, t3])
        t5 = k.s(lambda e: e.activation(out=r_t, in_=r_t, func=AF.Ln),
                 deps=[t4])
        t5b = k.v(lambda e: e.tensor_scalar(out=r_t, in0=r_t, scalar1=-0.5,
                                            scalar2=None, op0=ALU.mult),
                  deps=[t5])
        t6 = k.s(lambda e: e.activation(out=r_A, in_=r_t, func=AF.Exp),
                 deps=[t5b])
        t7 = k.v(lambda e: e.scalar_tensor_tensor(
            out=r_B, in0=r_mu, scalar=-1.0, in1=r_A,
            op0=ALU.mult, op1=ALU.mult), deps=[t3, t6])
        bcA = k.bank_cols(2, 512)
        bcB = k.bank_cols(3, 512)
        mmA = k.mm_group([(bcA, k.onesr[:, :], r_A, [t6])], banks=[2],
                         extra_deps=k.consts["ones"])
        mmB = k.mm_group([(bcB, k.onesr[:, :], r_B, [t7])], banks=[3],
                         extra_deps=k.consts["ones"])
        rows_war = [mmA, mmB, t6, t7]
        for ch in range(NCH):
            sl = ch % 8
            tmp = scratch[:, sl * 512:(sl + 1) * 512]
            ta = k.v(lambda e, ch=ch, tmp=tmp, c0=c0: e.tensor_tensor(
                out=tmp, in0=xres[:, ch * TPC + c0: ch * TPC + c0 + 512],
                in1=bcA, op=ALU.mult),
                deps=list(xres_toks) + [sq_tok[sl]] + rows_war, rb=[2])
            tb = k.v(lambda e, ch=ch, tmp=tmp, c0=c0: e.tensor_tensor(
                out=xn[:, ch * TPC + c0: ch * TPC + c0 + 512], in0=tmp,
                in1=bcB, op=ALU.add), deps=[ta], rb=[3])
            sq_tok[sl] = tb
            toks.append(tb)
    free = [Tok(sn, k.em.counts[sn])
            for sn in ("c_tensor", "c_vector", "c_scalar")]
    return toks, free


def emit_ffn(k: K, tag, xn, xn_toks, xres, xres_toks, W, I, renorm,
             scratch=None, sf_deps=()):
    nI = I // 128
    has_w3 = "w3" in W
    em = k.em

    rw = k.sb(tag + "_rw", [128, NCH * 8], BF16)
    scores = k.sb(tag + "_scores", [128, 8], F32)
    srows = k.sb(tag + "_srows", [128, 16], F32)
    comb = k.sb(tag + "_comb", [128, 16], F32)
    combf = k.sb(tag + "_combf", [16, TPC], BF16)
    combrow = k.sb(tag + "_combrow", [1, 2 * 512], BF16)
    comb_d = k.dram(tag + "_comb_d", [16, TPC], BF16)
    probs = k.sb(tag + "_probs", [128, 8], F32)
    wsl_cols = (2 if has_w3 else 1) * NCH * 128
    w1buf = k.sb(tag + "_w1buf", [128, 2 * wsl_cols], BF16)
    if has_w3:
        w2h = [(0, nI)]
        h1s = k.sb(tag + "_h1s", [128, 4 * 512], BF16)
    else:
        icA = (nI + 1) // 2
        w2h = [(0, icA), (icA, nI - icA)]
        h1s = None
    w2cols = 3 * w2h[0][1] * 128
    w2buf = k.sb(tag + "_w2buf", [128, 2 * w2cols], BF16)
    g_all = k.sb(tag + "_gall", [128, 3 * nI * 2 * 512], BF16)
    cbc = scratch[:, 0:1024]

    def gsl_of(el, fi, tg):
        o = ((el * nI + fi) * 2 + tg) * 512
        return g_all[:, o:o + 512]

    rwt = k.dma("gpsimd", rw[:, :].rearrange("p (kc e) -> p kc e", kc=NCH),
                W["router"].rearrange("kc p e -> p kc e"), slot="rw")

    combf_toks = []
    iter_war = []
    for tt in range(8):
        ps_r = k.bank_cols(4, 8)
        mmt = k.mm_group(
            [(ps_r, xn[:, kc * TPC + tt * 128:kc * TPC + (tt + 1) * 128],
              rw[:, kc * 8:(kc + 1) * 8], list(xn_toks) + [rwt])
             for kc in range(NCH)], banks=[4])
        m1n = srows[:, 0:1]
        sums = srows[:, 1:2]
        rec = srows[:, 2:3]
        m1 = srows[:, 3:4]
        m2 = srows[:, 4:5]
        den = srows[:, 5:6]
        t0 = k.v(lambda e: e.reduce_max(out=m1n, in_=ps_r, axis=AX.X,
                                        negate=True), rb=[4], deps=iter_war)
        t1 = k.s(lambda e: e.activation(out=scores, in_=ps_r, func=AF.Exp,
                                        bias=m1n, accum_out=sums),
                 deps=[t0] + iter_war, rb=[4])
        t2 = k.v(lambda e: e.reciprocal(out=rec, in_=sums), deps=[t1])
        t3 = k.v(lambda e: e.tensor_scalar(out=probs, in0=scores, scalar1=rec,
                                           scalar2=None, op0=ALU.mult),
                 deps=[t1, t2])
        t4 = k.v(lambda e: e.reduce_max(out=m1, in_=probs, axis=AX.X), deps=[t3])
        t6 = k.v(lambda e: e.scalar_tensor_tensor(
            out=comb[:, 8:16], in0=probs, scalar=m1, in1=probs,
            op0=ALU.is_ge, op1=ALU.bypass), deps=[t4])
        t7 = k.v(lambda e: e.scalar_tensor_tensor(
            out=comb[:, 8:16], in0=comb[:, 8:16], scalar=-1.0e9, in1=probs,
            op0=ALU.mult, op1=ALU.add), deps=[t6, t3])
        t8 = k.v(lambda e: e.reduce_max(out=m2, in_=comb[:, 8:16], axis=AX.X),
                 deps=[t7])
        t9 = k.v(lambda e: e.tensor_scalar(out=comb[:, 0:8], in0=probs,
                                           scalar1=m2, scalar2=None,
                                           op0=ALU.is_ge), deps=[t8])
        t10 = k.v(lambda e: e.tensor_tensor(out=comb[:, 0:8], in0=comb[:, 0:8],
                                            in1=probs, op=ALU.mult), deps=[t9])
        last = t10
        if renorm:
            t11 = k.v(lambda e: e.tensor_tensor(out=den, in0=m1, in1=m2,
                                                op=ALU.add), deps=[t4, t8])
            t12 = k.v(lambda e: e.reciprocal(out=den, in_=den), deps=[t11])
            last = k.v(lambda e: e.tensor_scalar(out=comb[:, 0:8],
                                                 in0=comb[:, 0:8], scalar1=den,
                                                 scalar2=None, op0=ALU.mult),
                       deps=[t10, t12])
        tz = k.v(lambda e: e.memset(comb[:, 8:9], 1.0), deps=[last])
        tz2 = k.v(lambda e: e.memset(comb[:, 9:16], 0.0), deps=[tz])
        ctp = k.ps[0:16, 5 * 512:5 * 512 + 128]
        tt_t = k.transpose(ctp, comb[:, 0:16], k.id32[:, :], banks=[5],
                           deps=[last, tz, tz2] + k.consts["ident"])
        cv = k.v(lambda e, tt=tt: e.tensor_copy(
            out=combf[:, tt * 128:(tt + 1) * 128], in_=ctp), deps=[tt_t], rb=[5])
        combf_toks.append(cv)
        iter_war = [tt_t, cv, t1]
    cd = k.dma("sync", comb_d[:, :], combf[:, :], slot="cd",
               deps=combf_toks)

    groups = [(0, 1, 2), (3, 4, 5), (6, 7, 8)]
    upd_toks = []
    g_war = []
    h_war = {}
    cbc_war = list(sf_deps)
    ya_tok = {}
    w1ctr = [0]
    for gi, grp in enumerate(groups):
        gdeps = []
        for el, e_idx in enumerate(grp):
            cbts = []
            for tg in range(2):
                crv = combrow[0:1, tg * 512:(tg + 1) * 512]
                crt = k.dma("sync", crv,
                            comb_d[e_idx:e_idx + 1, tg * 512:(tg + 1) * 512],
                            slot=f"cr{tg}", deps=[cd])
                bc = k.bank_cols(6, 512)
                bmt = k.mm_group([(bc, k.onesrb[:, :], crv, [crt])],
                                 banks=[6], extra_deps=k.consts["ones"])
                k.slot_use(crt, bmt)
                cbt = k.s(lambda e, tg=tg, bc=bc: e.copy(
                    out=cbc[:, tg * 512:(tg + 1) * 512], in_=bc),
                    rb=[6], deps=cbc_war)
                cbc_war = []
                cbts.append(cbt)
            for fi in range(nI):
                sl = w1ctr[0] % 2
                w1q = "gpsimd" if sl == 0 else "sync"
                w1ctr[0] += 1
                wslab = w1buf[:, sl * wsl_cols:(sl + 1) * wsl_cols]
                w1v = wslab[:, 0:NCH * 128]
                if has_w3:
                    lt = k.dma(w1q,
                               wslab.rearrange("p (w kc j) -> p w kc j",
                                               w=2, kc=NCH),
                               W["w1"][e_idx, fi].rearrange(
                                   "w p kc j -> p w kc j"),
                               slot=f"w1{sl}")
                else:
                    lt = k.dma(w1q, w1v,
                               W["w1"][e_idx, fi].rearrange(
                                   "p kc j -> p (kc j)"),
                               slot=f"w1{sl}")
                for tg in range(2):
                    c0 = tg * 512
                    b1 = tg
                    ps_h1 = k.bank_cols(b1, 512)
                    mm1 = k.mm_group(
                        [(ps_h1, w1v[:, kc * 128:(kc + 1) * 128],
                          xn[:, kc * TPC + c0:kc * TPC + c0 + 512],
                          list(xn_toks) + [lt]) for kc in range(NCH)],
                        banks=[b1])
                    k.slot_use(lt, mm1)
                    gsl = gsl_of(el, fi, tg)
                    if has_w3:
                        hsl = h1s[:, (sl * 2 + tg) * 512:
                                  (sl * 2 + tg + 1) * 512]
                        sil = k.s(lambda e, hsl=hsl, ps_h1=ps_h1:
                                  e.activation(out=hsl, in_=ps_h1,
                                               func=AF.Silu),
                                  deps=[mm1] + h_war.pop((sl, tg), []),
                                  rb=[b1])
                        w3v = wslab[:, NCH * 128:2 * NCH * 128]
                        b3 = 2 + tg
                        ps_h3 = k.bank_cols(b3, 512)
                        mm3 = k.mm_group(
                            [(ps_h3, w3v[:, kc * 128:(kc + 1) * 128],
                              xn[:, kc * TPC + c0:kc * TPC + c0 + 512],
                              list(xn_toks) + [lt]) for kc in range(NCH)],
                            banks=[b3])
                        k.slot_use(lt, mm3)
                        hc = k.v(lambda e, gsl=gsl, ps_h3=ps_h3, tg=tg:
                                 e.tensor_tensor(
                                     out=gsl, in0=ps_h3,
                                     in1=cbc[:, tg * 512:(tg + 1) * 512],
                                     op=ALU.mult),
                                 deps=[mm3, cbts[tg]] + g_war, rb=[b3])
                        gt = k.v(lambda e, gsl=gsl, hsl=hsl: e.tensor_tensor(
                            out=gsl, in0=gsl, in1=hsl, op=ALU.mult),
                            deps=[sil, hc])
                        cbc_war.append(hc)
                        h_war[(sl, tg)] = [gt]
                    else:
                        sil = k.s(lambda e, gsl=gsl, ps_h1=ps_h1:
                                  e.activation(out=gsl, in_=ps_h1,
                                               func=AF.Silu),
                                  deps=[mm1] + g_war + h_war.pop((0, tg), []),
                                  rb=[b1])
                        gt = k.v(lambda e, gsl=gsl, tg=tg: e.tensor_tensor(
                            out=gsl, in0=gsl,
                            in1=cbc[:, tg * 512:(tg + 1) * 512],
                            op=ALU.mult), deps=[sil, cbts[tg]])
                        cbc_war.append(gt)
                        h_war[(0, tg)] = [gt]
                    gdeps.append(gt)
            g_war = []
        for dch in range(NCH):
            lts = []
            wvs = []
            for hi, (ic0, icn) in enumerate(w2h):
                sl = (dch * len(w2h) + hi) % 2
                w2q = "sync" if sl == 0 else "gpsimd"
                wv = w2buf[:, sl * w2cols: sl * w2cols + 3 * icn * 128]
                lt = k.dma(w2q,
                           wv.rearrange("p (el ic j) -> p el ic j",
                                        el=3, ic=icn),
                           W["w2"][gi, dch][:, :, ic0:ic0 + icn, :]
                           .rearrange("el p ic j -> p el ic j"),
                           slot=f"w2{sl}")
                lts.append(lt)
                wvs.append((wv, ic0, icn))
            for tg in range(2):
                b_y = 4 + tg
                ps_y = k.bank_cols(b_y, 512)
                parts = []
                for hi, (wv, ic0, icn) in enumerate(wvs):
                    for el in range(3):
                        for ici in range(icn):
                            parts.append(
                                (ps_y,
                                 wv[:, (el * icn + ici) * 128:
                                    (el * icn + ici + 1) * 128],
                                 gsl_of(el, ic0 + ici, tg),
                                 gdeps + [lts[hi]]))
                mmy = k.mm_group(parts, banks=[b_y])
                for lt in lts:
                    k.slot_use(lt, mmy)
                g_war.append(mmy)
                xs = xres[:, dch * TPC + tg * 512:dch * TPC + tg * 512 + 512]
                yd = [ya_tok[(dch, tg)]] if (dch, tg) in ya_tok else []
                ut = k.v(lambda e, xs=xs, ps_y=ps_y: e.tensor_tensor(
                    out=xs, in0=xs, in1=ps_y, op=ALU.add),
                    deps=[mmy] + yd + list(xres_toks), rb=[b_y])
                ya_tok[(dch, tg)] = ut
                if gi == 2:
                    upd_toks.append(ut)
    return upd_toks


def emit_load_res_fm(k: K, xr_ap, xres):
    t = k.dma("sync", xres[:, :].rearrange("p (ch t) -> p ch t", ch=NCH),
              xr_ap.rearrange("(ch p) t -> p ch t", p=128), slot="xload")
    return [t]


def emit_store_res_fm(k: K, xres, yr_ap, deps):
    t = k.dma("sync", yr_ap.rearrange("(ch p) t -> p ch t", p=128),
              xres[:, :].rearrange("p (ch t) -> p ch t", ch=NCH),
              slot="xstore", deps=deps)
    return t


def host_prep_ffn(L, inputs):
    if L == 0:
        gw = inputs["L0_ffn_gate_w"]
        w1 = inputs["L0_ffn_w1"]
        w2 = inputs["L0_ffn_w2"]
        w3 = inputs["L0_ffn_w3"]
        ws1, ws2, ws3 = (inputs["L0_ffn_ws1"], inputs["L0_ffn_ws2"],
                         inputs["L0_ffn_ws3"])
        I = 1024
        W1 = np.stack([prep_fm(w1[e].T) for e in range(8)] + [prep_fm(ws1.T)])
        W3 = np.stack([prep_fm(w3[e].T) for e in range(8)] + [prep_fm(ws3.T)])
        W13 = np.ascontiguousarray(np.stack([W1, W3], axis=2)).astype(BF)
        W2 = np.stack([prep_fm(w2[e].T) for e in range(8)] + [prep_fm(ws2.T)])
        W2g = np.ascontiguousarray(
            W2.reshape(3, 3, NCH, 128, I // 128, 128)
            .transpose(0, 2, 1, 3, 4, 5)).astype(BF)
        router = prep_rhs(gw.T)
        return dict(router=router, w1=W13, w2=W2g, I=I, renorm=False)
    else:
        w1 = inputs["L1_ffn_w1"]
        w2 = inputs["L1_ffn_w2"]
        ws1, ws2 = inputs["L1_ffn_w_shared1"], inputs["L1_ffn_w_shared2"]
        router = prep_rhs(inputs["L1_ffn_w_router"])
        I = 1408
        W1 = np.stack([prep_fm(w1[e]) for e in range(8)]
                      + [prep_fm(ws1)]).astype(BF)
        W2 = np.stack([prep_fm(w2[e]) for e in range(8)] + [prep_fm(ws2)])
        W2g = np.ascontiguousarray(
            W2.reshape(3, 3, NCH, 128, I // 128, 128)
            .transpose(0, 2, 1, 3, 4, 5)).astype(BF)
        return dict(router=router, w1=W1, w2=W2g, I=I, renorm=True)


def build_ffn_nc(L, I, renorm, has_w3):
    nc = bass.Bass("TRN2", target_bir_lowering=False, debug=False)
    nI = I // 128
    xr = nc.dram_tensor("xr", [D, TPC], F32, kind="ExternalInput").ap()
    yr = nc.dram_tensor("yr", [D, TPC], F32, kind="ExternalOutput").ap()
    ident = nc.dram_tensor("ident", [128, 128], F32, kind="ExternalInput").ap()
    router = nc.dram_tensor("w_router", [NCH, 128, 8], F32,
                            kind="ExternalInput").ap()
    w2 = nc.dram_tensor("w_w2", [3, NCH, 3, 128, nI, 128], F32,
                        kind="ExternalInput").ap()
    if has_w3:
        w1 = nc.dram_tensor("w_w1", [9, nI, 2, 128, NCH, 128], F32,
                            kind="ExternalInput").ap()
        W = dict(router=router, w1=w1, w2=w2, w3=w1)
    else:
        w1 = nc.dram_tensor("w_w1", [9, nI, 128, NCH, 128], F32,
                            kind="ExternalInput").ap()
        W = dict(router=router, w1=w1, w2=w2)
    k = K(nc)
    k.setup_consts(ident)
    xres = k.sb_raw("xres", [128, NCH * TPC], F32).ap()
    xn = k.sb_raw("xn", [128, NCH * TPC], BF16).ap()
    scratch = k.sb_raw("lnscr", [128, 8 * 512], F32).ap()
    rows = scratch[0:1, 0:2560]
    xt = emit_load_res_fm(k, xr, xres)
    xnt, _scrfree = emit_ln(k, xres, xt, xn, scratch, rows)
    upd = emit_ffn(k, "f", xn, xnt, xres, xt, W, I, renorm, scratch,
                   sf_deps=_scrfree)
    st = emit_store_res_fm(k, xres, yr, upd)
    k.em.finalize([st])
    return nc


_IDENT = np.eye(128, dtype=np.float32)


def run_ffn_launch(L, inputs, xres_list):
    Wp = host_prep_ffn(L, inputs)
    nc = build_ffn_nc(L, Wp["I"], Wp["renorm"], L == 0)
    in_maps = []
    for c in range(8):
        m = {"xr": xres_list[c], "ident": _IDENT, "w_router": Wp["router"],
             "w_w1": Wp["w1"], "w_w2": Wp["w2"]}
        in_maps.append(m)
    res = run_bass_kernel_spmd(nc, in_maps, core_ids=list(range(8)))
    return [res.results[c]["yr"] for c in range(8)]


def _to_fm_shards(x):
    out = []
    for c in range(8):
        b, h = c // 2, c % 2
        out.append(np.ascontiguousarray(x[b, h * TPC:(h + 1) * TPC, :].T))
    return out


def _from_fm_shards(shards):
    x = np.empty((B, S, D), np.float32)
    for c in range(8):
        b, h = c // 2, c % 2
        x[b, h * TPC:(h + 1) * TPC, :] = shards[c].T
    return x


def kernel_unfused(**inputs):
    xres = _to_fm_shards(np.ascontiguousarray(inputs["L0_attn_a_x"]))
    xres, _ = run_attn_launch("l0a", inputs, xres)
    xres = run_ffn_launch(0, inputs, xres)
    xres, _ = run_mamba_launch(inputs, xres)
    xres = run_ffn_launch(0, inputs, xres)
    xres, _ = run_attn_launch("l1a", inputs, xres)
    xres = run_ffn_launch(1, inputs, xres)
    xres, _ = run_attn_launch("l1b", inputs, xres)
    xres = run_ffn_launch(1, inputs, xres)
    return _from_fm_shards(xres)


RG_PAIRS = [[0, 1], [2, 3], [4, 5], [6, 7]]


def emit_allgather(k: K, tag, src, dst, deps):
    sem = k.em.dma_slot("cc")
    return k.em.emit(
        "gpsimd",
        lambda e: e.collective_compute(
            "AllGather", ALU.bypass, replica_groups=RG_PAIRS,
            ins=[src.opt()], outs=[dst.opt()]),
        deps=deps, inc=sem, inc_by=1)


def emit_attn(k: K, tag, cfg, xn, xn_toks, xres, xres_toks, W,
              spare_bf=None, spare_f32=None, sf_deps=()):
    nqh, nkvh = cfg["nqh"], cfg["nkvh"]
    gq, gkv = cfg["gq"], cfg["gkv"]
    ngrp = nqh // gq
    KV = cfg.get("kvw", 2 * TPC)
    full = cfg["kw"] is None
    KW = 2048 if full else cfg["kw"]
    NKC = KW // 128
    rope = cfg["rope"]
    qknorm = cfg.get("qknorm", False)
    bias = cfg.get("bias", False)
    L0SCALE = 0.08838834764831845

    xnw = k.sb(tag + "_xnw", [128, NCH * KV], BF16)
    kf = k.sb(tag + "_kf", [128, gkv * KV], BF16)
    wbuf = k.sb(tag + "_wbuf", [128, 2 * NCH * 128], BF16)
    wobuf = k.sb(tag + "_wobuf", [128, 2 * 2 * gq * 128], BF16)
    srows = k.sb(tag + "_srows", [128, 16], F32)
    ob = [0]

    def carve_bf(n):
        v = spare_bf[:, ob[0]:ob[0] + n]
        ob[0] += n
        return v

    of = [0]

    def carve_f(n):
        v = spare_f32[:, of[0]:of[0] + n]
        of[0] += n
        return v

    if full:
        pr = carve_bf(2 * KW)
        ptg = carve_bf(16 * 256)
        scrr = spare_f32[:, 0:2 * 2048]
    elif KW <= 1024:
        pr = carve_bf(2 * 2 * KW)
        ptg = carve_bf(2 * 2 * KW)
        scrr = None
    else:
        pr = carve_bf(2 * KW)
        ptg = carve_bf(2 * KW)
        scrr = k.sb(tag + "_scrr", [128, 2 * 2 * KW], BF16)
    vfm = carve_bf(KV)
    wo_every = cfg.get("wo_every", 2)
    outp = carve_bf(wo_every * gq * TPC)
    qf = carve_bf(gq * TPC)
    spare_total = NCH * TPC
    mcols = (2 * KW) if (not full) else 0
    if spare_total - ob[0] - mcols >= gkv * KV:
        vt = carve_bf(gkv * KV)
    else:
        vt = k.sb(tag + "_vt", [128, gkv * KV], BF16)
    if rope:
        tabq = k.sb(tag + "_tabq", [128, 2048], BF16)
        tabk = tabq
        ptm = k.sb(tag + "_ptm", [128, 128], BF16)
        rsc = carve_f(2048)
    if not full:
        msk = carve_bf(2 * KW)
    if qknorm:
        nsc = carve_f(1024)
        ncols = k.sb(tag + "_ncols", [128, 6], F32)
        nrows = nsc[0:1, 0:1024]
        bcsb = carve_f(1024)
    if bias:
        bcols = k.sb(tag + "_bcols", [128, 4 * NCH], F32)
    assert ob[0] <= spare_bf.shape[-1], ob[0]
    assert spare_f32 is None or of[0] <= spare_f32.shape[-1], of[0]

    ccs = k.dram(tag + "_ccs", [D, TPC], BF16)
    ccd = k.dram(tag + "_ccd", [2 * D, TPC], BF16)
    t_src = k.dma("sync", ccs.rearrange("(ch p) t -> p ch t", p=128),
                  xn.rearrange("p (ch t) -> p ch t", ch=NCH),
                  slot="src", deps=list(xn_toks))
    t_cc = emit_allgather(k, tag, ccs, ccd, [t_src])
    xnw_toks = []
    xnw_own = []
    PREV = KV - TPC
    for ch in range(NCH):
        tp = k.dma("sync", xnw[:, ch * KV: ch * KV + PREV],
                   ccd[ch * 128:(ch + 1) * 128, TPC - PREV:TPC],
                   slot=f"xw{ch % 4}", deps=[t_cc])
        to = k.v(lambda e, ch=ch: e.tensor_copy(
            out=xnw[:, ch * KV + PREV: (ch + 1) * KV],
            in_=xn[:, ch * TPC:(ch + 1) * TPC]), deps=list(xn_toks))
        xnw_toks += [tp, to]
        xnw_own.append(to)

    consts = []
    ptt = None
    sfw = list(sf_deps)
    if rope:
        ptt = k.dma("sync", ptm[:, :], W["rot"], slot="pt")
        consts.append(ptt)
    if bias:
        consts.append(k.dma("gpsimd", bcols[:, :], W["bcols"], slot="bb"))
    nconst_toks = []
    if qknorm:
        tq = k.dma("gpsimd", ncols[:, 0:2], W["nscales"], slot="ns")
        t1 = k.v(lambda e: e.tensor_scalar(
            out=ncols[:, 0:1], in0=ncols[:, 0:1], scalar1=L0SCALE,
            scalar2=None, op0=ALU.mult), deps=[tq])
        nconst_toks = [t1]

    rope_war = {}
    qn_war = []

    def rope_head(buf, c0, length, wtab, ring, ring_name, deps):
        toks = list(deps)
        out_toks = []
        blocks = [(i * 1024, 1024) for i in range(length // 1024)]
        if length % 1024:
            blocks.append((length - length % 1024, length % 1024))
        for off, wd in blocks:
            war = rope_war.pop(ring_name, [])
            rw2 = rope_war.pop("rsc", [])
            rslot = ring[:, 0:2 * wd]
            tl = k.dma("sync", rslot, wtab[:, 2 * off: 2 * off + 2 * wd],
                       slot="tb" + ring_name, deps=sfw)
            mms = []
            for sub in range(wd // 512):
                mm_ = k.mm_group(
                    [(k.bank_cols(2 + sub, 512), ptm[:, :],
                      buf[:, c0 + off + sub * 512:
                          c0 + off + sub * 512 + 512],
                      toks + [ptt])], banks=[2 + sub])
                mms.append(mm_)
            r1 = rsc[:, 0:wd]
            r2 = rsc[:, 1024:1024 + wd]
            v1 = k.v(lambda e, rslot=rslot, wd=wd, r1=r1: e.tensor_tensor(
                out=r1, in0=k.ps[:, 2 * 512: 2 * 512 + wd],
                in1=rslot[:, wd:2 * wd], op=ALU.mult),
                deps=mms + [tl] + war + rw2,
                rb=list(range(2, 2 + wd // 512)))
            g1 = k.g(lambda e, off=off, wd=wd, rslot=rslot, r2=r2:
                     e.tensor_tensor(
                         out=r2, in0=buf[:, c0 + off: c0 + off + wd],
                         in1=rslot[:, 0:wd], op=ALU.mult),
                     deps=toks + [tl] + war + rw2)
            v2 = k.v(lambda e, off=off, wd=wd, r1=r1, r2=r2: e.tensor_tensor(
                out=buf[:, c0 + off: c0 + off + wd],
                in0=r1, in1=r2, op=ALU.add), deps=[v1, g1])
            rope_war[ring_name] = [v1, g1]
            rope_war["rsc"] = [v2]
            k.slot_use(tl, v1)
            k.slot_use(tl, g1)
            out_toks.append(v2)
        return out_toks

    def qk_norm_head(buf, c0, length, is_q, deps):
        nonlocal qn_war
        scol = ncols[:, 0:1] if is_q else ncols[:, 1:2]
        out_toks = []
        for b in range(length // 1024):
            bb = c0 + b * 1024
            war = qn_war
            st = k.s(lambda e, bb=bb: e.activation(
                out=nsc[:, 0:1024], in_=buf[:, bb:bb + 1024],
                func=AF.Square), deps=list(deps) + nconst_toks + war)
            rr = nrows[0:1, 0:1024]
            mms = []
            for hh2 in range(2):
                rp = k.ps[0:1, (2 + hh2) * 512:(2 + hh2) * 512 + 512]
                mm = k.mm_group(
                    [(rp, k.onesc[:, 2:3], nsc[:, hh2 * 512:(hh2 + 1) * 512],
                      [st])], banks=[2 + hh2], extra_deps=k.consts["ones"])
                mms.append((mm, rp, hh2))
            c1a = k.v(lambda e: e.tensor_scalar(
                out=rr[0:1, 0:512], in0=mms[0][1], scalar1=1e-6, scalar2=None,
                op0=ALU.add), deps=[mms[0][0]] + war, rb=[2])
            c1b = k.v(lambda e: e.tensor_scalar(
                out=rr[0:1, 512:1024], in0=mms[1][1], scalar1=1e-6,
                scalar2=None, op0=ALU.add), deps=[mms[1][0]] + war, rb=[3])
            c2 = k.s(lambda e: e.activation(out=rr, in_=rr, func=AF.Ln),
                     deps=[c1a, c1b])
            c3 = k.v(lambda e: e.tensor_scalar(
                out=rr, in0=rr, scalar1=-0.5, scalar2=None, op0=ALU.mult),
                deps=[c2])
            c4 = k.s(lambda e: e.activation(out=rr, in_=rr, func=AF.Exp),
                     deps=[c3])
            bt = []
            for hh2 in range(2):
                bp = k.ps[:, (2 + hh2) * 512:(2 + hh2) * 512 + 512]
                mmb = k.mm_group(
                    [(bp, k.onesr[:, :], rr[0:1, hh2 * 512:(hh2 + 1) * 512],
                      [c4])], banks=[2 + hh2], extra_deps=k.consts["ones"])
                cp = k.s(lambda e, hh2=hh2, bp=bp: e.copy(
                    out=bcsb[:, hh2 * 512:(hh2 + 1) * 512], in_=bp),
                    deps=[mmb] + war, rb=[2 + hh2])
                bt.append(cp)
            c6 = k.v(lambda e, bb=bb: e.scalar_tensor_tensor(
                out=buf[:, bb:bb + 1024], in0=buf[:, bb:bb + 1024],
                scalar=scol, in1=bcsb[:, 0:1024], op0=ALU.mult,
                op1=ALU.mult), deps=bt + list(deps))
            qn_war = [c6, st]
            out_toks.append(c6)
        return out_toks

    slab_ctr = [0]

    def proj_head(wkey, fo, rhs_c0, rhs_len, dst, dst_c0, evac, war,
                  rdeps=None):
        rdeps = xnw_toks if rdeps is None else rdeps
        sl = slab_ctr[0] % 2
        slab_ctr[0] += 1
        wv = wbuf[:, sl * NCH * 128:(sl + 1) * NCH * 128]
        lt = k.dma("gpsimd", wv,
                   W[wkey][fo].rearrange("p kc j -> p (kc j)"),
                   slot=f"w{sl}")
        toks = []
        for tg in range(rhs_len // 512):
            pb = tg % 2
            ps = k.bank_cols(pb, 512)
            mm = k.mm_group(
                [(ps, wv[:, kc * 128:(kc + 1) * 128],
                  xnw[:, kc * KV + rhs_c0 + tg * 512:
                      kc * KV + rhs_c0 + tg * 512 + 512],
                  rdeps + [lt]) for kc in range(NCH)], banks=[pb])
            toks.append(evac(ps, dst, dst_c0 + tg * 512, 512, mm, pb,
                             tg % 2, war))
        k.slot_use(lt, toks[-1])
        return toks

    def evac_plain(ps, dst, c0, width, mm_tok, pbank, sl, war):
        return k.s(lambda e: e.copy(out=dst[:, c0:c0 + width], in_=ps),
                   deps=[mm_tok] + war, rb=[pbank])

    def mk_evac_bias(colidx, scale):
        def ev(ps, dst, c0, width, mm_tok, pbank, sl, war):
            col = bcols[:, colidx:colidx + 1]
            if scale is None:
                return k.v(lambda e: e.tensor_scalar(
                    out=dst[:, c0:c0 + width], in0=ps, scalar1=col,
                    scalar2=None, op0=ALU.add),
                    deps=[mm_tok] + consts + war, rb=[pbank])
            return k.v(lambda e: e.tensor_scalar(
                out=dst[:, c0:c0 + width], in0=ps, scalar1=col, scalar2=scale,
                op0=ALU.add, op1=ALU.mult),
                deps=[mm_tok] + consts + war, rb=[pbank])
        return ev

    def ev_qn_q(ps, dst, c0, width, mm_tok, pbank, sl, war):
        return qknorm_evac(ps, dst[:, c0:c0 + width], width, mm_tok, pbank,
                           True, sl, war)

    def ev_qn_k(ps, dst, c0, width, mm_tok, pbank, sl, war):
        return qknorm_evac(ps, dst[:, c0:c0 + width], width, mm_tok, pbank,
                           False, sl, war)

    def tensor_now():
        return Tok("c_tensor", k.em.counts["c_tensor"])

    donor = list(xn_toks) + xnw_toks + list(sf_deps)
    upd_toks = []
    wo_pair_deps = []
    outp_war = list(donor)
    msk_tok = {}
    msk_tt = {}
    msk_war = {0: list(donor), 1: list(donor)}
    pr_war = {0: list(donor), 1: list(donor)}
    scr_war = {0: list(donor) + list(sf_deps), 1: list(donor) + list(sf_deps)}
    ptT_war = {0: list(donor), 1: list(donor)}
    ptg_war = list(donor)
    nm_war = {}
    qf_war = [list(donor) for _ in range(gq)]
    kv_war = [[] for _ in range(gkv)]
    vt_war = [list(donor) for _ in range(gkv)]
    vfm_war = list(donor)
    bo_done = [False]
    pend = [None]

    def flush_pend():
        p = pend[0]
        if p is None:
            return
        pend[0] = None
        for hl in range(gq):
            hkl = hl // (gq // gkv)
            ps_av = k.ps[:, p["offs"][hl]:p["offs"][hl] + 128]
            avb = [p["offs"][hl] // 512]
            parts = [(ps_av,
                      vt[:, hkl * KV + (p["kstart"] // 128 + j) * 128:
                         hkl * KV + (p["kstart"] // 128 + j + 1) * 128],
                      p["dstT"][:, (hl * NKC + j) * 128:
                                (hl * NKC + j + 1) * 128],
                      p["tT"] + list(vt_war[hkl]))
                     for j in range(NKC)]
            mav = k.mm_group(parts, banks=avb)
            oc = ((p["grp"] % wo_every) * gq + hl) * TPC + p["tt"] * 128
            ot = k.s(lambda e, oc=oc, ps_av=ps_av: e.copy(
                out=outp[:, oc:oc + 128], in_=ps_av),
                deps=[mav] + outp_war, rb=avb)
            wo_pair_deps.append(ot)
        ptT_war[p["ptkey"]] = \
            [Tok("c_tensor", k.em.counts["c_tensor"])]

    pend_f = [None]
    nzh0 = [None]

    def flush_pend_f():
        nonlocal ptg_war
        p = pend_f[0]
        if p is None:
            return
        pend_f[0] = None
        ps_av = k.ps[:, 0:256]
        parts = [(ps_av,
                  vt[:, p["hkl"] * KV + j * 128:
                     p["hkl"] * KV + (j + 1) * 128],
                  ptg[:, j * 256:(j + 1) * 256],
                  p["tT"] + list(vt_war[p["hkl"]]))
                 for j in range(16)]
        mav = k.mm_group(parts, banks=[0])
        ptg_war = [mav]
        oc = ((p["grp"] % wo_every) * gq + p["hl"]) * TPC \
            + (p["tt"] - 1) * 128
        ot = k.s(lambda e, oc=oc, ps_av=ps_av: e.copy(
            out=outp[:, oc:oc + 256], in_=ps_av),
            deps=[mav] + outp_war, rb=[0])
        wo_pair_deps.append(ot)

    for grp in range(ngrp):
        pair = grp // 2
        pl = grp % 2
        for hl in range(gq):
            h = grp * gq + hl
            dst_c0 = hl * TPC
            if bias:
                toks = proj_head("wq", h, KV - TPC, TPC, qf, dst_c0,
                                 mk_evac_bias(0 * NCH + h, L0SCALE),
                                 qf_war[hl], rdeps=xnw_own)
            else:
                toks = proj_head("wq", h, KV - TPC, TPC, qf, dst_c0,
                                 evac_plain, qf_war[hl], rdeps=xnw_own)
            if qknorm:
                toks = qk_norm_head(qf, dst_c0, TPC, True, toks)
            if rope:
                toks = rope_head(qf, dst_c0, TPC, W["tabq"], tabq, "t", toks)
            qf_war[hl] = toks
        for kl in range(gkv):
            hk = grp * gkv + kl
            if bias:
                ktoks = proj_head("wk", hk, 0, KV, kf, kl * KV,
                                  mk_evac_bias(1 * NCH + hk, None), kv_war[kl])
            else:
                ktoks = proj_head("wk", hk, 0, KV, kf, kl * KV, evac_plain,
                                  kv_war[kl])
            if qknorm:
                ktoks = qk_norm_head(kf, kl * KV, KV, False, ktoks)
            if rope:
                ktoks = rope_head(kf, kl * KV, KV, W["tabk"], tabk, "t",
                                  ktoks)
            kv_war[kl] = ktoks
            if bias:
                vtoks = proj_head("wv", hk, 0, KV, vfm, 0,
                                  mk_evac_bias(2 * NCH + hk, None),
                                  vfm_war + vt_war[kl])
            else:
                vtoks = proj_head("wv", hk, 0, KV, vfm, 0, evac_plain,
                                  vfm_war + vt_war[kl])
            tv = k.dma("sync",
                       vt[:, kl * KV:(kl + 1) * KV].rearrange(
                           "p (j q) -> p j q", j=KV // 128),
                       vfm[:, 0:KV].rearrange("p (j q) -> p j q",
                                              j=KV // 128),
                       transpose=True, slot="vtp", deps=vtoks)
            vfm_war = [tv]
            vt_war[kl] = [tv]
        if full:
            tile_iter = [(hl, tt) for hl in range(gq) for tt in range(8)]
        else:
            tile_iter = []
        tctr = 0
        if not full:
            for tt in range(8):
                par = tt % 2
                tctr += 1
                if KW <= 1024:
                    s_off = par * 2048
                    s_banks = (0, 1, 2) if par == 0 else (4, 5, 6)
                    kstart = tt * 128
                    use_ev = False
                else:
                    s_off = 0
                    s_banks = (0, 1, 2, 3, 4)
                    kstart = tt * 128
                    use_ev = True

                def ensure_mask(tti):
                    sl_ = tti % 2
                    if msk_tt.get(sl_) != tti:
                        mt_ = k.dma("sync", msk[:, sl_ * KW:(sl_ + 1) * KW],
                                    W["masks"][tti], slot=f"mk{sl_}",
                                    deps=msk_war.pop(sl_, []))
                        msk_tok[sl_] = mt_
                        msk_tt[sl_] = tti
                mslot = tt % 2
                mv = msk[:, mslot * KW:(mslot + 1) * KW]
                ensure_mask(tt)
                if tt + 1 < 8:
                    ensure_mask(tt + 1)
                sc_toks = []
                for hl in range(gq):
                    hkl = hl // (gq // gkv)
                    qtile = qf[:, hl * TPC + tt * 128:
                               hl * TPC + (tt + 1) * 128]
                    qdeps = list(qf_war[hl]) + list(kv_war[hkl])
                    p0 = s_off + hl * KW
                    c0 = 0
                    while c0 < KW:
                        w = min(512 - (p0 + c0) % 512, KW - c0)
                        ps_j = k.ps[:, p0 + c0: p0 + c0 + w]
                        mmt = k.mm_group(
                            [(ps_j, qtile,
                              kf[:, hkl * KV + kstart + c0:
                                 hkl * KV + kstart + c0 + w], qdeps)],
                            banks=list(s_banks) if (c0 == 0 and hl == 0)
                            else [])
                        sc_toks.append(mmt)
                        c0 += w
                if use_ev:
                    base = scrr[:, par * 2 * KW:(par + 1) * 2 * KW]
                    ev = k.s(lambda e, base=base: e.copy(
                        out=base, in_=k.ps[:, s_off:s_off + 2 * KW]),
                        deps=sc_toks + scr_war.pop(par, []),
                        rb=list(s_banks))
                    bdeps = [ev]
                else:
                    base = k.ps[:, s_off:s_off + 2 * KW]
                    bdeps = sc_toks
                am0 = k.v(lambda e, base=base, mv=mv: e.tensor_tensor(
                    out=base[:, 0:KW], in0=base[:, 0:KW], in1=mv,
                    op=ALU.add),
                    deps=bdeps + [msk_tok[mslot]],
                    **({} if use_ev else dict(rb=list(s_banks),
                                              wb=list(s_banks))))
                am1 = k.v(lambda e, base=base, mv=mv: e.tensor_tensor(
                    out=base[:, KW:2 * KW], in0=base[:, KW:2 * KW], in1=mv,
                    op=ALU.add), deps=bdeps + [msk_tok[mslot], am0])
                msk_war.setdefault(mslot, []).extend([am0, am1])
                nmp = srows[:, par * 2:(par + 1) * 2]
                smp = srows[:, 4 + par * 2: 6 + par * 2]
                rcp = srows[:, 8 + par * 2: 10 + par * 2]
                mx = k.v(lambda e, base=base, nmp=nmp: e.reduce_max(
                    out=nmp, in_=base.rearrange("p (h w) -> p h w", h=2),
                    axis=AX.X, negate=True),
                    deps=[am0, am1] + nm_war.pop(par, []),
                    **({} if use_ev else dict(rb=list(s_banks))))
                prv = pr[:, par * 2 * KW:(par + 1) * 2 * KW] if KW <= 1024 \
                    else pr[:, 0:2 * KW]
                prw = pr_war.pop(par if KW <= 1024 else 0, [])
                exs = []
                for hl in range(gq):
                    ex = k.s(lambda e, base=base, prv=prv, hl=hl, nmp=nmp,
                             smp=smp: e.activation(
                                 out=prv[:, hl * KW:(hl + 1) * KW],
                                 in_=base[:, hl * KW:(hl + 1) * KW],
                                 func=AF.Exp, bias=nmp[:, hl:hl + 1],
                                 accum_out=smp[:, hl:hl + 1]),
                             deps=[mx] + prw,
                             **({} if use_ev else dict(rb=list(s_banks))))
                    exs.append(ex)
                nm_war[par] = list(exs)
                if use_ev:
                    scr_war[par] = list(exs)
                rc = k.v(lambda e, smp=smp, rcp=rcp: e.reciprocal(
                    out=rcp, in_=smp), deps=exs)
                nzs = []
                for hl in range(gq):
                    nz = k.v(lambda e, prv=prv, hl=hl, rcp=rcp:
                             e.tensor_scalar(
                                 out=prv[:, hl * KW:(hl + 1) * KW],
                                 in0=prv[:, hl * KW:(hl + 1) * KW],
                                 scalar1=rcp[:, hl:hl + 1], scalar2=None,
                                 op0=ALU.mult), deps=[rc])
                    nzs.append(nz)
                def emit_tT(key):
                    ptw = ptT_war.pop(key, [])
                    t_ = k.dma("sync",
                               dstT.rearrange("p (j q) -> p j q", j=2 * NKC),
                               prv.rearrange("p (j q) -> p j q", j=2 * NKC),
                               transpose=True, slot="pT", deps=nzs + ptw)
                    pr_war[key] = [t_]
                    return t_
                if KW <= 1024:
                    dstT = ptg[:, par * 2 * KW:(par + 1) * 2 * KW]
                    tT = emit_tT(par)
                    flush_pend()
                    av_offs = [s_off + 2 * KW + hl * 128 for hl in range(gq)]
                else:
                    flush_pend()
                    dstT = ptg[:, 0:2 * KW]
                    tT = emit_tT(0)
                    av_offs = [2560 + par * 768 + hl * 128
                               for hl in range(gq)]
                pend[0] = dict(tT=[tT], dstT=dstT, kstart=kstart, grp=grp,
                               tt=tt, offs=av_offs,
                               ptkey=(par if KW <= 1024 else 0))
            flush_pend()
        for hl, tt in tile_iter:
            hkl = hl // (gq // gkv)
            par = tctr % 2
            tctr += 1
            s_off = par * 2048
            s_banks = (0, 1, 2, 3) if par == 0 else (4, 5, 6, 7)
            qtile = qf[:, hl * TPC + tt * 128: hl * TPC + (tt + 1) * 128]
            qdeps = list(qf_war[hl]) + list(kv_war[hkl])
            sc_toks = []
            for j in range(4):
                c0 = j * 512
                ps_j = k.ps[:, s_off + c0: s_off + c0 + 512]
                mmt = k.mm_group(
                    [(ps_j, qtile,
                      kf[:, hkl * KV + c0: hkl * KV + c0 + 512], qdeps)],
                    banks=list(s_banks) if j == 0 else [])
                sc_toks.append(mmt)
            ps_full = k.ps[:, s_off: s_off + KW]
            scrv = scrr[:, par * 2048:(par + 1) * 2048]
            ev = k.s(lambda e, ps_full=ps_full, scrv=scrv: e.copy(
                out=scrv, in_=ps_full),
                deps=sc_toks + scr_war.pop(par, []), rb=list(s_banks))
            nm = srows[:, par:par + 1]
            sm = srows[:, 2 + par:3 + par]
            mx = k.v(lambda e, scrv=scrv, nm=nm: e.reduce_max(
                out=nm, in_=scrv, axis=AX.X, negate=True),
                deps=[ev] + nm_war.pop(par, []))
            prv = pr[:, par * KW:(par + 1) * KW]
            ex = k.s(lambda e, scrv=scrv, prv=prv, nm=nm, sm=sm:
                     e.activation(out=prv, in_=scrv, func=AF.Exp,
                                  bias=nm, accum_out=sm),
                     deps=[mx] + pr_war.pop(par, []))
            nm_war[par] = [ex]
            scr_war[par] = [ex]
            rc = k.v(lambda e, sm=sm, par=par: e.reciprocal(
                out=srows[:, 4 + par:5 + par], in_=sm), deps=[ex])
            nz = k.v(lambda e, prv=prv, par=par: e.tensor_scalar(
                out=prv, in0=prv, scalar1=srows[:, 4 + par:5 + par],
                scalar2=None, op0=ALU.mult), deps=[rc])
            if tt % 2 == 0:
                nzh0[0] = (nz, prv)
            else:
                flush_pend_f()
                tTs = []
                for half, (nzv, prvv) in enumerate((nzh0[0], (nz, prv))):
                    dstT = ptg.rearrange("p (j q) -> p j q", j=16)[
                        :, :, half * 128:(half + 1) * 128]
                    t_ = k.dma("sync", dstT,
                               prvv.rearrange("p (j q) -> p j q", j=16),
                               transpose=True, slot="pT",
                               deps=[nzv] + ptg_war)
                    tTs.append(t_)
                pr_war[0] = [tTs[0]]
                pr_war[1] = [tTs[1]]
                pend_f[0] = dict(tT=tTs, hl=hl, hkl=hkl, grp=grp, tt=tt)
        if full:
            flush_pend_f()
        tnow = [tensor_now()]
        for hl in range(gq):
            qf_war[hl] = qf_war[hl] + tnow
        for kl in range(gkv):
            kv_war[kl] = kv_war[kl] + tnow
            vt_war[kl] = vt_war[kl] + tnow
        if grp % wo_every == wo_every - 1:
            nh = wo_every * gq
            h0 = (grp // wo_every) * nh
            outp_war = []
            for dch in range(NCH):
                wsl = dch % 2
                wv = wobuf[:, wsl * nh * 128:(wsl + 1) * nh * 128]
                lt = k.dma("gpsimd",
                           wv.rearrange("p (h j) -> p h j", h=nh),
                           W["wo"][dch][:, h0:h0 + nh, :],
                           slot=f"wo{wsl}")
                for tg in range(2):
                    pb = tg % 2
                    ps = k.bank_cols(pb, 512)
                    parts = [(ps, wv[:, hl4 * 128:(hl4 + 1) * 128],
                              outp[:, hl4 * TPC + tg * 512:
                                   hl4 * TPC + tg * 512 + 512],
                              wo_pair_deps + [lt]) for hl4 in range(nh)]
                    mmw = k.mm_group(parts, banks=[pb])
                    k.slot_use(lt, mmw)
                    xs = xres[:, dch * TPC + tg * 512:
                              dch * TPC + tg * 512 + 512]
                    if bias and not bo_done[0]:
                        ut = k.v(lambda e, xs=xs, ps=ps, dch=dch:
                                 e.scalar_tensor_tensor(
                                     out=xs, in0=ps,
                                     scalar=bcols[:, 3 * NCH + dch:
                                                  3 * NCH + dch + 1],
                                     in1=xs, op0=ALU.add, op1=ALU.add),
                                 deps=[mmw] + list(xres_toks) + consts, rb=[pb])
                    else:
                        ut = k.v(lambda e, xs=xs, ps=ps: e.tensor_tensor(
                            out=xs, in0=xs, in1=ps, op=ALU.add),
                            deps=[mmw] + list(xres_toks), rb=[pb])
                    outp_war.append(ut)
                    upd_toks.append(ut)
            bo_done[0] = True
            wo_pair_deps = []
    return upd_toks


def _rope_tables_il(pos, scale):
    hd = 128
    inv = 1.0 / (10000.0 ** (np.arange(0, hd, 2, dtype=np.float64) / hd))
    fr = pos[None, :].astype(np.float64) * inv[:, None]
    c = np.repeat(np.cos(fr), 2, axis=0) * scale
    s = np.repeat(np.sin(fr), 2, axis=0) * scale
    return c.astype(np.float32), s.astype(np.float32)


def _rope_tables_half(pos, scale):
    hd = 128
    frac = np.arange(0, hd, 2, dtype=np.float64) / hd
    ts = 10000.0 ** frac
    fr = pos[None, :].astype(np.float64) / ts[:, None]
    c = np.concatenate([np.cos(fr), np.cos(fr)], axis=0) * scale
    s = np.concatenate([np.sin(fr), np.sin(fr)], axis=0) * scale
    return c.astype(np.float32), s.astype(np.float32)


def _pack_tab(c, s_):
    T = c.shape[1]
    out = np.empty((128, 2 * T), np.float32)
    blocks = [(i * 1024, 1024) for i in range(T // 1024)]
    if T % 1024:
        blocks.append((T - T % 1024, T % 1024))
    for off, wd in blocks:
        out[:, 2 * off: 2 * off + wd] = c[:, off:off + wd]
        out[:, 2 * off + wd: 2 * off + 2 * wd] = s_[:, off:off + wd]
    return out.astype(BF)


def _rot_pt(mode):
    P = np.zeros((128, 128), np.float32)
    if mode == "il":
        for i in range(64):
            P[2 * i, 2 * i + 1] = -1.0
            P[2 * i + 1, 2 * i] = 1.0
    else:
        for i in range(64):
            P[i, i + 64] = -1.0
            P[i + 64, i] = 1.0
    return np.ascontiguousarray(P.T).astype(BF)


def host_prep_attn(which, inputs):
    if which == "l0a":
        wq = prep_fm(inputs["L0_attn_a_Wq"].T).astype(BF)
        wk = prep_fm(inputs["L0_attn_a_Wk"].T).astype(BF)
        wv = prep_fm(inputs["L0_attn_a_Wv"].T).astype(BF)
        wo = prep_fm(inputs["L0_attn_a_Wo"].T).astype(BF)
        KW, WIN = 640, 512
        cfg = dict(nqh=16, nkvh=16, gq=2, gkv=2, kw=KW, rope="il",
                   qknorm=False, bias=False, kvw=1536)
        shared = dict(wq=wq, wk=wk, wv=wv, wo=wo, rot=_rot_pt("il"))
        percore = []
        for h2 in range(2):
            pos_q = np.arange(TPC) + h2 * TPC
            pos_k = np.maximum(np.arange(1536) + h2 * TPC - 512, 0)
            scale = 1.0 / np.sqrt(128.0)
            cq, sq = _rope_tables_il(pos_q, scale)
            ck, sk = _rope_tables_il(pos_k, 1.0)
            tabq = _pack_tab(cq, sq)
            tabk = _pack_tab(ck, sk)
            masks = np.full((8, 128, KW), NEG, np.float32)
            for tt in range(8):
                for i in range(128):
                    p = h2 * TPC + tt * 128 + i
                    for j in range(KW):
                        kg = h2 * TPC + tt * 128 + 512 + j - TPC
                        if i < j <= i + WIN and kg >= 0:
                            masks[tt, i, j] = 0.0
            percore.append(dict(tabq=tabq, tabk=tabk,
                                masks=masks.astype(BF)))
        return cfg, shared, percore
    if which == "l1a":
        wq = prep_fm(inputs["L1_attn_a_wq"].T).astype(BF)
        wk = prep_fm(inputs["L1_attn_a_wk"].T).astype(BF)
        wv = prep_fm(inputs["L1_attn_a_wv"].T).astype(BF)
        wo = prep_fm(inputs["L1_attn_a_wo"].T).astype(BF)
        bcols = np.zeros((128, 64), np.float32)
        bcols[:, 0:16] = inputs["L1_attn_a_bq"].reshape(16, 128).T
        bcols[:, 16:32] = inputs["L1_attn_a_bk"].reshape(16, 128).T
        bcols[:, 32:48] = inputs["L1_attn_a_bv"].reshape(16, 128).T
        bcols[:, 48:64] = inputs["L1_attn_a_bo"].reshape(16, 128).T
        cfg = dict(nqh=16, nkvh=16, gq=2, gkv=2, kw=None, rope=None,
                   qknorm=False, bias=True)
        shared = dict(wq=wq, wk=wk, wv=wv, wo=wo, bcols=bcols)
        return cfg, shared, [dict(), dict()]
    if which == "l1b":
        qk = inputs["L1_attn_b_q_kernel"]
        kvk = inputs["L1_attn_b_kv_kernel"]
        ok = inputs["L1_attn_b_out_kernel"]
        wq = prep_fm(np.ascontiguousarray(
            qk.transpose(1, 0, 2).reshape(D, 16 * 128))).astype(BF)
        wk = prep_fm(np.ascontiguousarray(
            kvk[0].transpose(1, 0, 2).reshape(D, 8 * 128))).astype(BF)
        wv = prep_fm(np.ascontiguousarray(
            kvk[1].transpose(1, 0, 2).reshape(D, 8 * 128))).astype(BF)
        wo = prep_fm(np.ascontiguousarray(
            ok.reshape(16 * 128, D))).astype(BF)
        nsc = np.zeros((128, 2), np.float32)
        nsc[:, 0] = inputs["L1_attn_b_q_norm_scale"]
        nsc[:, 1] = inputs["L1_attn_b_k_norm_scale"]
        KW, WIN = 1152, 1024
        cfg = dict(nqh=16, nkvh=8, gq=2, gkv=1, kw=KW, rope="half",
                   qknorm=True, bias=False, wo_every=1)
        shared = dict(wq=wq, wk=wk, wv=wv, wo=wo, rot=_rot_pt("half"),
                      nscales=nsc)
        percore = []
        for h2 in range(2):
            pos_q = np.arange(TPC) + h2 * TPC
            pos_k = np.maximum(np.arange(2 * TPC) + h2 * TPC - TPC, 0)
            cq, sq = _rope_tables_half(pos_q, 1.0)
            ck, sk = _rope_tables_half(pos_k, 1.0)
            tabq = _pack_tab(cq, sq)
            tabk = _pack_tab(ck, sk)
            masks = np.full((8, 128, KW), NEG, np.float32)
            for tt in range(8):
                for i in range(128):
                    for j in range(KW):
                        kg = h2 * TPC + tt * 128 + j - TPC
                        if i < j <= i + WIN and kg >= 0:
                            masks[tt, i, j] = 0.0
            percore.append(dict(tabq=tabq, tabk=tabk,
                                masks=masks.astype(BF)))
        return cfg, shared, percore
    raise ValueError(which)


def build_attn_nc(which, cfg, shared, percore):
    nc = bass.Bass("TRN2", target_bir_lowering=False, debug=False)
    xr = nc.dram_tensor("xr", [D, TPC], F32, kind="ExternalInput").ap()
    yr = nc.dram_tensor("yr", [D, TPC], F32, kind="ExternalOutput").ap()
    ident = nc.dram_tensor("ident", [128, 128], F32, kind="ExternalInput").ap()
    W = {}
    nfo_q = cfg["nqh"]
    nfo_kv = cfg["nkvh"]
    W["wq"] = nc.dram_tensor("w_wq", [nfo_q, 128, NCH, 128], F32,
                             kind="ExternalInput").ap()
    W["wk"] = nc.dram_tensor("w_wk", [nfo_kv, 128, NCH, 128], F32,
                             kind="ExternalInput").ap()
    W["wv"] = nc.dram_tensor("w_wv", [nfo_kv, 128, NCH, 128], F32,
                             kind="ExternalInput").ap()
    W["wo"] = nc.dram_tensor("w_wo", [NCH, 128, nfo_q, 128], F32,
                             kind="ExternalInput").ap()
    if cfg["rope"]:
        W["tabq"] = nc.dram_tensor("w_tabq", [128, 2 * TPC], BF16,
                                   kind="ExternalInput").ap()
        W["tabk"] = nc.dram_tensor(
            "w_tabk", [128, 2 * cfg.get("kvw", 2 * TPC)], BF16,
            kind="ExternalInput").ap()
        W["rot"] = nc.dram_tensor("w_rot", [128, 128], BF16,
                                  kind="ExternalInput").ap()
    if cfg["kw"] is not None:
        W["masks"] = nc.dram_tensor("w_masks", [8, 128, cfg["kw"]], BF16,
                                    kind="ExternalInput").ap()
    if cfg.get("bias"):
        W["bcols"] = nc.dram_tensor("w_bcols", [128, 64], F32,
                                    kind="ExternalInput").ap()
    if cfg.get("qknorm"):
        W["nscales"] = nc.dram_tensor("w_nscales", [128, 2], F32,
                                      kind="ExternalInput").ap()
    k = K(nc)
    k.setup_consts(ident)
    xres = k.sb_raw("xres", [128, NCH * TPC], F32).ap()
    xn = k.sb_raw("xn", [128, NCH * TPC], BF16).ap()
    scratch = k.sb_raw("lnscr", [128, 8 * 512], F32).ap()
    rows = scratch[0:1, 0:2560]
    xt = emit_load_res_fm(k, xr, xres)
    xnt, scrfree = emit_ln(k, xres, xt, xn, scratch, rows)
    upd = emit_attn(k, "a", cfg, xn, xnt, xres, xt, W,
                    spare_bf=xn, spare_f32=scratch, sf_deps=scrfree)
    st = emit_store_res_fm(k, xres, yr, upd)
    k.em.finalize([st])
    return nc


def run_attn_launch(which, inputs, xres_list, mode="hw", sim_cores=2):
    cfg, shared, percore = host_prep_attn(which, inputs)
    nc = build_attn_nc(which, cfg, shared, percore)
    in_maps = []
    ncores = 8 if mode == "hw" else sim_cores
    for c in range(ncores):
        m = {"xr": xres_list[c], "ident": _IDENT}
        for kk, v in shared.items():
            m["w_" + kk] = v
        for kk, v in percore[c % 2].items():
            m["w_" + kk] = v
        in_maps.append(m)
    res = run_bass_kernel_spmd(nc, in_maps, core_ids=list(range(8)))
    return [res.results[c]["yr"] for c in range(8)], nc


def emit_mamba(k: K, tag, xn, xn_toks, xres, xres_toks, W, lnscr,
               sf_deps=()):
    em = k.em
    DI, CVD, NFO, HALO, NC4, CL = 4096, 6144, 81, 32, 4, 256

    xres_d = k.dram(tag + "_xres_d", [D, TPC], F32)
    ccs_m = k.dram(tag + "_ccs", [D, HALO], BF16)
    ccd_m = k.dram(tag + "_ccd", [2 * D, HALO], BF16)
    gate_d = k.dram(tag + "_gate_d", [DI, TPC], BF16)
    convd = k.dram(tag + "_convd", [CVD, TPC], BF16)
    states_d = k.dram(tag + "_states_d", [NC4, 128, DI], BF16)
    cdx_d = k.dram(tag + "_cdx_d", [NC4, 64, 64], F32)
    h_src = k.dram(tag + "_hsrc", [128, DI], F32)
    h_dst = k.dram(tag + "_hdst", [256, DI], F32)
    ygd = k.dram(tag + "_ygd", [DI, TPC], BF16)

    conv_in = k.sb(tag + "_cvin", [128, 48 * 544], BF16)
    wbuf = k.sb(tag + "_wbuf", [128, 2 * 2 * NCH * 128], BF16)
    halo_sb = k.sb(tag + "_halo", [128, NCH * HALO], BF16)
    cwcols = k.sb(tag + "_cw", [128, 336], F32)
    dtw = k.sb(tag + "_dtw", [64, 64], F32)
    acol = k.sb(tag + "_acol", [128, 4], F32)
    cvout = k.sb(tag + "_cvout", [128, 4 * CL], BF16)
    xtm = k.sb(tag + "_xtm", [128, 4 * 128], BF16)
    btm = k.sb(tag + "_btm", [128, 16 * 128], BF16)
    cbt8 = wbuf[:, 4096:8192]
    bdf = k.sb(tag + "_bdf", [128, 4 * 128], BF16)
    gstg = k.sb(tag + "_gstg", [128, 1024], BF16)
    dftm = k.sb(tag + "_dftm", [128, 2 * 64], F32)
    trimask = k.sb(tag + "_tri", [128, 2 * CL], BF16)
    rrow = xres[0:1, 48 * CL + 2560: 48 * CL + 2560 + 5 * CL]
    cdxt = k.sb(tag + "_cdxt", [64, 64], F32)
    ones64 = k.sb(tag + "_o64", [64, 64], F32)
    ststg = k.sb(tag + "_stst", [128, DI], BF16)

    sqs_defined_early = True
    conv_raw = xres[:, 0:48 * CL]
    adt = xres[:, 48 * CL:48 * CL + 1024]
    acs = xres[:, 48 * CL + 1024: 48 * CL + 2048]
    awrk = xres[:, 48 * CL + 2048: 48 * CL + 2560]
    hbuf = conv_raw[:, 0:DI]
    cdflat = conv_raw[0:1, DI:2 * DI]
    yfm = xn[:, 0:32 * CL]
    hprev = xn[:, 32 * CL:32 * CL + DI]

    addn = []
    for ch in range(NCH):
        t = k.v(lambda e, ch=ch: e.tensor_tensor(
            out=xres[:, ch * TPC:(ch + 1) * TPC],
            in0=xres[:, ch * TPC:(ch + 1) * TPC],
            in1=xn[:, ch * TPC:(ch + 1) * TPC], op=ALU.add),
            deps=list(xres_toks) + list(xn_toks))
        addn.append(t)
    xspill = k.dma("sync", xres_d.rearrange("(ch p) t -> p ch t", p=128),
                   xres[:, :].rearrange("p (ch t) -> p ch t", ch=NCH),
                   slot="xsp", deps=addn)

    c_cw = k.dma("gpsimd", cwcols[:, :], W["cwcols"], slot="ccw")
    sqs0 = [lnscr[:, i * 512:(i + 1) * 512] for i in range(8)]
    sqw0 = [list(sf_deps) for _ in range(8)]
    rmsrow = rrow[0:1, 0:512]
    xn2_toks = []
    for tg in range(2):
        c0 = tg * 512
        rps = k.bank_cols(3, 512, rows=1)
        for grpi in range(2):
            parts = []
            for ch in range(grpi * 8, grpi * 8 + 8):
                sl = ch % 8
                st = k.s(lambda e, ch=ch, sl=sl, c0=c0: e.activation(
                    out=sqs0[sl], in_=xn[:, ch * TPC + c0:ch * TPC + c0 + 512],
                    func=AF.Square), deps=list(xn_toks) + sqw0[sl] + addn)
                parts.append((rps, k.onesc[:, 1:2], sqs0[sl], [st]))
            mmr0 = k.mm_group(parts, banks=[3], extra_deps=k.consts["ones"],
                              cont=(grpi == 1), open_end=(grpi == 0))
            for sl in range(8):
                sqw0[sl] = [mmr0]
        w0 = xn2_toks[-1:]
        m1 = k.v(lambda e: e.tensor_scalar(
            out=rmsrow[0:1, 0:512], in0=rps, scalar1=1.0 / D, scalar2=1e-6,
            op0=ALU.mult, op1=ALU.add), deps=[mmr0, xspill] + w0, rb=[3])
        m2 = k.s(lambda e: e.activation(out=rmsrow[0:1, 0:512],
                                        in_=rmsrow[0:1, 0:512], func=AF.Ln),
                 deps=[m1])
        m3 = k.v(lambda e: e.tensor_scalar(
            out=rmsrow[0:1, 0:512], in0=rmsrow[0:1, 0:512], scalar1=-0.5,
            scalar2=None, op0=ALU.mult), deps=[m2])
        m4 = k.s(lambda e: e.activation(out=rmsrow[0:1, 0:512],
                                        in_=rmsrow[0:1, 0:512], func=AF.Exp),
                 deps=[m3])
        bps0 = k.bank_cols(2, 512)
        mmb0 = k.mm_group([(bps0, k.onesr[:, :], rmsrow[0:1, 0:512], [m4])],
                          banks=[2], extra_deps=k.consts["ones"])
        for ch in range(NCH):
            tt = k.v(lambda e, ch=ch, c0=c0, bps0=bps0: e.scalar_tensor_tensor(
                out=xn[:, ch * TPC + c0:ch * TPC + c0 + 512],
                in0=xn[:, ch * TPC + c0:ch * TPC + c0 + 512],
                scalar=cwcols[:, 320 + ch:321 + ch], in1=bps0,
                op0=ALU.mult, op1=ALU.mult),
                deps=[mmb0, c_cw] + sqw0[ch % 8], rb=[2])
            xn2_toks.append(tt)
    xn_toks = xn2_toks
    rms_rrow_war = [mmb0] + xn2_toks[-1:]

    t_h = k.dma("sync", ccs_m.rearrange("(ch p) t -> p ch t", p=128),
                xn.rearrange("p (ch t) -> p ch t", ch=NCH)[:, :, TPC - HALO:],
                slot="hsrc", deps=list(xn_toks))
    t_cc = emit_allgather(k, tag + "h", ccs_m, ccd_m, [t_h])
    th_l = k.dma("sync", halo_sb[:, :].rearrange("p (ch t) -> p ch t", ch=NCH),
                 ccd_m[0:D].rearrange("(ch p) t -> p ch t", p=128),
                 slot="hld", deps=[t_cc])
    c_dtw = k.dma("gpsimd", dtw[:, :], W["dtw"], slot="cdtw")
    c_ac = k.dma("gpsimd", acol[:, 0:3], W["acols"], slot="cac")
    c_fl = k.dma("gpsimd", acol[:, 3:4], W["flag"], slot="cfl")
    c_tri = k.dma("gpsimd", trimask[:, :], W["tri"], slot="ctri")
    t_o = k.v(lambda e: e.memset(ones64[:, :], 1.0))
    a1 = k.s(lambda e: e.activation(out=acol[:, 2:3], in_=acol[:, 1:2],
                                    func=AF.Exp), deps=[c_ac])
    a2 = k.v(lambda e: e.tensor_scalar(out=acol[:, 2:3], in0=acol[:, 2:3],
                                       scalar1=-1.0, scalar2=None,
                                       op0=ALU.mult), deps=[a1])
    th_f = k.v(lambda e: e.tensor_scalar(
        out=halo_sb[:, :], in0=halo_sb[:, :], scalar1=acol[:, 3:4],
        scalar2=None, op0=ALU.mult), deps=[th_l, c_fl])
    halo_toks = [th_f]

    slabc = [0]
    sq_war = [[] for _ in range(8)]
    sqs = [lnscr[:, i * CL:(i + 1) * CL] for i in range(8)]
    tmpf = lnscr[:, 8 * CL: 9 * CL]
    tmpf2 = lnscr[:, 9 * CL:10 * CL]
    bcv = awrk[:, 256:512]
    dfw = awrk[:, 0:256]

    state = dict(cvout_war=[[] for _ in range(8)], xtm_war={0: [], 1: []},
                 btm_war=[],
                 cb_war=[], lc_war=[], chs_war=[], bdf_war=[], dftm_war=[],
                 yfm_war=[[] for _ in range(32)], stst_war=[], cdxt_war=[],
                 rrow_war=rms_rrow_war, bcv_war=[xspill], tmp_war=[],
                 tmp2_war=[],
                 acs_tok=[None, None], dfz_war=[xspill], adt_done=[],
                 conv_done=[xspill], cdsp=[])

    def ablock(th):
        own0 = th * 512
        z = awrk[0:64, 0:512]
        z2 = acs[0:64, th * 512:(th + 1) * 512]
        ps = k.bank_cols(3, 512, rows=64)
        mmz = k.mm_group([(ps, dtw[:, :], adt[0:64, own0:own0 + 512],
                           state["adt_done"] + [c_dtw])], banks=[3])
        w0 = state["dfz_war"]
        e0 = k.v(lambda e: e.tensor_scalar(
            out=z, in0=ps, scalar1=acol[0:64, 0:1], scalar2=None,
            op0=ALU.add), deps=[mmz, c_ac] + w0, rb=[3])
        e1a = k.v(lambda e: e.tensor_scalar(out=z2, in0=z, scalar1=-1.0,
                                            scalar2=None, op0=ALU.mult),
                  deps=[e0] + w0)
        e1 = k.v(lambda e: e.tensor_tensor(out=z2, in0=z2, in1=z,
                                           op=ALU.min), deps=[e1a])
        e2 = k.s(lambda e: e.activation(out=z2, in_=z2, func=AF.Exp),
                 deps=[e1])
        e3 = k.v(lambda e: e.tensor_scalar(out=z2, in0=z2, scalar1=1.0,
                                           scalar2=None, op0=ALU.add),
                 deps=[e2])
        e4 = k.s(lambda e: e.activation(out=z2, in_=z2, func=AF.Ln),
                 deps=[e3])
        e5 = k.v(lambda e: e.tensor_scalar(out=z, in0=z, scalar1=0.0,
                                           scalar2=None, op0=ALU.max),
                 deps=[e4])
        e6 = k.v(lambda e: e.tensor_tensor(out=z, in0=z, in1=z2, op=ALU.add),
                 deps=[e5])
        e7 = k.v(lambda e: e.tensor_scalar(out=z, in0=z,
                                           scalar1=acol[0:64, 2:3],
                                           scalar2=None, op0=ALU.mult),
                 deps=[e6, a2])
        if th == 0:
            e8 = k.v(lambda e: e.tensor_tensor_scan(
                out=acs[0:64, 0:512], data0=z, data1=z, initial=0.0,
                op0=ALU.add, op1=ALU.bypass),
                deps=[e7] + ([state["acs_tok"][0]] if state["acs_tok"][0]
                             else []))
        else:
            e8 = k.v(lambda e: e.tensor_tensor_scan(
                out=acs[0:64, 512:1024], data0=z, data1=z,
                initial=acs[0:64, 511:512], op0=ALU.add, op1=ALU.bypass),
                deps=[e7, state["acs_tok"][0]])
        state["acs_tok"][th] = e8
        state["dfz_war"] = [e8]
        return e8

    def cdx(c):
        at = state["acs_tok"][c // 2]
        e_i = (c + 1) * CL - 1
        w = state["cdxt_war"]
        col = cdxt[:, 0:1]
        if c == 0:
            t1 = k.v(lambda e: e.tensor_copy(out=col, in_=acs[0:64, 255:256]),
                     deps=[at] + w)
        else:
            t1 = k.v(lambda e, e_i=e_i, c=c: e.tensor_scalar(
                out=col, in0=acs[0:64, e_i:e_i + 1],
                scalar1=acs[0:64, c * CL - 1:c * CL], op0=ALU.subtract,
                scalar2=None), deps=[at] + w)
        t2 = k.s(lambda e: e.activation(out=col, in_=col, func=AF.Exp),
                 deps=[t1])
        t3 = k.v(lambda e: e.tensor_scalar(out=cdxt[:, :], in0=ones64[:, :],
                                           scalar1=col, scalar2=None,
                                           op0=ALU.mult), deps=[t2, t_o] + w)
        sp = k.dma("sync", cdx_d[c], cdxt[0:64, :], slot="cdsp",
                   deps=[t3])
        state["cdxt_war"] = [sp]
        state.setdefault("cdsp", []).append(sp)
        return sp

    def pass1(c):
        th = c // 2
        c0 = HALO + (c % 2) * CL
        at = state["acs_tok"][th]
        sq_ps = k.bank_cols(2, CL, rows=1)
        first = True
        cr_toks = []
        for grpi in range(6):
            parts = []
            for chi in range(grpi * 8, grpi * 8 + 8):
                sl = chi % 8
                cr = conv_raw[:, chi * CL:(chi + 1) * CL]
                ci = conv_in[:, chi * 544:(chi + 1) * 544]
                t0 = k.v(lambda e, ci=ci, cr=cr, c0=c0, chi=chi:
                         e.tensor_scalar(
                             out=cr, in0=ci[:, c0 - 3:c0 - 3 + CL],
                             scalar1=cwcols[:, chi * 4:chi * 4 + 1],
                             scalar2=cwcols[:, 192 + chi:193 + chi],
                             op0=ALU.mult, op1=ALU.add),
                         deps=state["conv_evt"][chi] + [c_cw]
                         + state.setdefault("crw", {}).pop(chi, []))
                for dd in (1, 2, 3):
                    t0 = k.v(lambda e, ci=ci, cr=cr, c0=c0, chi=chi, dd=dd:
                             e.scalar_tensor_tensor(
                                 out=cr, in0=ci[:, c0 - 3 + dd:
                                                c0 - 3 + dd + CL],
                                 scalar=cwcols[:, chi * 4 + dd:
                                               chi * 4 + dd + 1],
                                 in1=cr, op0=ALU.mult, op1=ALU.add),
                             deps=[t0])
                cr_toks.append(t0)
                st = k.s(lambda e, cr=cr, sl=sl: e.activation(
                    out=sqs[sl], in_=cr, func=AF.Square),
                    deps=[t0] + sq_war[sl])
                state["crw"][chi] = [st]
                parts.append((sq_ps, k.onesc[:, 1:2], sqs[sl], [st]))
            mmt = k.mm_group(parts, banks=[2], extra_deps=k.consts["ones"],
                             cont=(grpi > 0), open_end=(grpi < 5))
            for sl in range(8):
                sq_war[sl] = [mmt]
        rr = rrow[0:1, 4 * CL:5 * CL]
        r1 = k.v(lambda e: e.tensor_scalar(
            out=rr, in0=sq_ps, scalar1=1.0 / CVD, scalar2=1e-6,
            op0=ALU.mult, op1=ALU.add), deps=[mmt] + state["rrow_war"],
            rb=[2])
        r2 = k.s(lambda e: e.activation(out=rr, in_=rr, func=AF.Ln), deps=[r1])
        r3 = k.v(lambda e: e.tensor_scalar(out=rr, in0=rr, scalar1=-0.5,
                                           scalar2=None, op0=ALU.mult),
                 deps=[r2])
        r4 = k.s(lambda e: e.activation(out=rr, in_=rr, func=AF.Exp),
                 deps=[r3])
        bps = k.ps[:, 2 * 512: 2 * 512 + CL]
        mmb = k.mm_group([(bps, k.onesr[:, :], rr, [r4])], banks=[2],
                         extra_deps=k.consts["ones"])
        b1 = k.s(lambda e: e.copy(out=bcv, in_=bps),
                 deps=[mmb] + state["bcv_war"], rb=[2])
        state["rrow_war"] = [mmb]
        w = state["dfz_war"]
        d1 = k.v(lambda e, c=c: e.tensor_scalar(
            out=dfw[0:64, :], in0=acs[0:64, c * CL:(c + 1) * CL],
            scalar1=acs[0:64, (c + 1) * CL - 1:(c + 1) * CL],
            scalar2=-1.0, op0=ALU.subtract, op1=ALU.mult),
            deps=[at] + w)
        d2 = k.s(lambda e: e.activation(out=dfw[0:64, :], in_=dfw[0:64, :],
                                        func=AF.Exp), deps=[d1])
        state["dfz_war"] = [d2]
        dtts = []
        for tt in range(2):
            tps = k.ps[:, 3 * 512:3 * 512 + 64]
            mt = k.transpose(tps, dfw[0:64, tt * 128:(tt + 1) * 128],
                             k.id32[0:64, 0:64], banks=[3],
                             deps=[d2] + k.consts["ident"])
            ev = k.v(lambda e, tt=tt, tps=tps: e.tensor_copy(
                out=dftm[:, tt * 64:(tt + 1) * 64], in_=tps),
                deps=[mt] + state["dftm_war"], rb=[3])
            dtts.append(ev)
        state["dftm_war"] = dtts
        state["dfz_war"] = dtts
        cdx(c)
        stst_toks = []
        order = list(range(32, 40)) + list(range(40, 48)) + list(range(32))
        u1_last = None
        for chi in order:
            sl = chi % 4
            ts1 = chi % 2
            twk = "tmp_war" if ts1 == 0 else "tmp2_war"
            tv = (tmpf, tmpf2)[ts1]
            cr = conv_raw[:, chi * CL:(chi + 1) * CL]
            u1 = k.v(lambda e, cr=cr, chi=chi, tv=tv: e.scalar_tensor_tensor(
                out=tv, in0=cr, scalar=cwcols[:, 240 + chi:241 + chi],
                in1=bcv, op0=ALU.mult, op1=ALU.mult),
                deps=cr_toks[-1:] + [b1, c_cw] + state[twk])
            u1_last = u1
            cv = cvout[:, sl * CL:(sl + 1) * CL]
            u2 = k.s(lambda e, cv=cv, tv=tv: e.activation(out=cv, in_=tv,
                                                          func=AF.Silu),
                     deps=[u1] + state["cvout_war"][sl])
            state[twk] = [u2]
            sp = k.dma("sync", convd[chi * 128:(chi + 1) * 128,
                                     c * CL:(c + 1) * CL], cv,
                       slot=f"cvsp{sl % 2}", deps=[u2])
            state.setdefault("cvsp_last", {})[sl % 2] = sp
            war_new = [sp]
            if 32 <= chi < 40:
                g = chi - 32
                tb = k.dma("sync",
                           btm[:, g * 256:(g + 1) * 256].rearrange(
                               "p (j q) -> p j q", j=2),
                           cv.rearrange("p (j q) -> p j q", j=2),
                           transpose=True, slot="btp",
                           deps=[u2] + state["btm_war"])
                state["btm_war"] = []
                war_new.append(tb)
                state["btm_tok"] = state.get("btm_tok", []) + [tb]
            elif chi < 32:
                xs1 = chi % 2
                xb1 = xs1 * 256
                tx = k.dma("sync",
                           xtm[:, xb1:xb1 + 256].rearrange(
                               "p (j q) -> p j q", j=2),
                           cv.rearrange("p (j q) -> p j q", j=2),
                           transpose=True, slot=f"xtp{xs1}",
                           deps=[u2] + state["xtm_war"][xs1])
                war_new.append(tx)
                xw = []
                for hh in (2 * chi, 2 * chi + 1):
                    bo1 = (hh % 2) * 256
                    bdfw = state.setdefault("bdfw", {})
                    bdts = []
                    for tt in range(2):
                        bb = k.v(lambda e, hh=hh, tt=tt, bo1=bo1:
                                 e.tensor_scalar(
                                     out=bdf[:, bo1 + tt * 128:
                                             bo1 + (tt + 1) * 128],
                                     in0=btm[:, ((hh // 8) * 2 + tt) * 128:
                                             ((hh // 8) * 2 + tt + 1) * 128],
                                     scalar1=dftm[:, tt * 64 + hh:
                                                  tt * 64 + hh + 1],
                                     scalar2=None, op0=ALU.mult),
                                 deps=state["btm_tok"] + dtts
                                 + bdfw.pop(hh % 2, []))
                        bdts.append(bb)
                    sps = k.bank_cols(4 + (hh % 2), 64)
                    mmS = k.mm_group(
                        [(sps, bdf[:, bo1 + tt * 128:bo1 + (tt + 1) * 128],
                          xtm[:, xb1 + tt * 128 + (hh % 2) * 64:
                              xb1 + tt * 128 + (hh % 2) * 64 + 64],
                          [tx] + bdts) for tt in range(2)],
                        banks=[4 + (hh % 2)])
                    bdfw[hh % 2] = [mmS]
                    ev = k.v(lambda e, hh=hh, sps=sps: e.tensor_copy(
                        out=ststg[:, hh * 64:(hh + 1) * 64], in_=sps),
                        deps=[mmS] + state["stst_war"], rb=[4 + (hh % 2)])
                    stst_toks.append(ev)
                    xw.append(mmS)
                state["xtm_war"][xs1] = xw + [tx]
            state["cvout_war"][sl] = war_new
        ssp = k.dma("sync", states_d[c], ststg[:, :], slot="stsp",
                    deps=stst_toks)
        state["stst_war"] = [ssp]
        state["bcv_war"] = [u1_last]
        state["btm_war"] = [Tok("c_vector", em.counts["c_vector"])]
        state["btm_tok"] = []
        state["conv_done"] = [Tok("c_tensor", em.counts["c_tensor"]),
                              Tok("c_vector", em.counts["c_vector"]),
                              Tok("c_scalar", em.counts["c_scalar"])]

    for sl in range(8):
        sq_war[sl] = list(sqw0[sl])
    proj_evac_war = {("a",): [xspill]}
    gate_sp_d = {0: [], 1: []}
    for th in range(2):
        own0 = th * 512
        pair_lt = {}
        for fo in range(NFO):
            is_gate = 48 <= fo < 80
            is_adt = fo == 80
            fop = fo & ~1
            if fo == fop:
                sl = slabc[0] % 2
                slabc[0] += 1
                n_fo = 2 if fop + 1 < NFO else 1
                wpair = wbuf[:, sl * 4096: sl * 4096 + n_fo * 2048]
                lt = k.dma("gpsimd",
                           wpair.rearrange("p (f kc j) -> p f kc j",
                                           f=n_fo, kc=NCH),
                           W["inproj"][fop:fop + n_fo].rearrange(
                               "f p kc j -> p f kc j"),
                           slot=f"w{sl}")
                pair_lt[fop] = (lt, wpair)
            lt, wpair = pair_lt[fop]
            wv = wpair[:, (fo - fop) * 2048:(fo - fop + 1) * 2048]
            last_use = None
            if not (is_gate or is_adt):
                ps = k.bank_cols(2, HALO)
                if th == 0:
                    parts = [(ps, wv[:, kc * 128:(kc + 1) * 128],
                              halo_sb[:, kc * HALO:(kc + 1) * HALO],
                              halo_toks + [lt]) for kc in range(NCH)]
                else:
                    parts = [(ps, wv[:, kc * 128:(kc + 1) * 128],
                              xn[:, kc * TPC + 480: kc * TPC + 512],
                              list(xn_toks) + [lt]) for kc in range(NCH)]
                mmh = k.mm_group(parts, banks=[2])
                dst = conv_in[:, fo * 544: fo * 544 + HALO]
                e1 = k.s(lambda e, dst=dst, ps=ps: e.copy(out=dst, in_=ps),
                         deps=[mmh] + proj_evac_war.pop(("h", fo), []),
                         rb=[2])
                proj_evac_war[("h", fo)] = []
                last_use = e1
            pb = fo % 2
            ps = k.bank_cols(pb, 512)
            parts = [(ps, wv[:, kc * 128:(kc + 1) * 128],
                      xn[:, kc * TPC + own0: kc * TPC + own0 + 512],
                      list(xn_toks) + [lt]) for kc in range(NCH)]
            mmo = k.mm_group(parts, banks=[pb])
            if is_adt:
                ev = k.v(lambda e, ps=ps, own0=own0: e.tensor_copy(
                    out=adt[:, own0:own0 + 512], in_=ps),
                    deps=[mmo] + proj_evac_war.pop(("a",), []), rb=[pb])
                proj_evac_war[("a",)] = []
                state["adt_done"].append(ev)
            elif is_gate:
                gsl = (fo - 48) % 2
                gs = gstg[:, gsl * 512:(gsl + 1) * 512]
                ev = k.v(lambda e, ps=ps, gs=gs: e.tensor_copy(
                    out=gs, in_=ps),
                    deps=[mmo] + proj_evac_war.pop(("g", gsl), []), rb=[pb])
                sp = k.dma("sync",
                           gate_d[(fo - 48) * 128:(fo - 47) * 128,
                                  own0:own0 + 512], gs,
                           slot=f"gsp{gsl}", deps=[ev])
                proj_evac_war[("g", gsl)] = [sp]
                gate_sp_d[gsl] = [sp]
            else:
                dst = conv_in[:, fo * 544 + HALO: fo * 544 + 544]
                ev = k.s(lambda e, dst=dst, ps=ps: e.copy(out=dst, in_=ps),
                         deps=[mmo] + proj_evac_war.pop(("o", fo), []),
                         rb=[pb])
                proj_evac_war[("o", fo)] = []
            k.slot_use(lt, ev if last_use is None else last_use)
            k.slot_use(lt, ev)
            if not (is_gate or is_adt):
                state["conv_done"].append(ev)
                state.setdefault("conv_evt", {})[fo] = [ev] + (
                    [last_use] if last_use is not None else [])
                if last_use is not None:
                    state["conv_done"].append(last_use)
        ablock(th)
        for c in (2 * th, 2 * th + 1):
            pass1(c)
            state["conv_done"] = state["conv_done"][-3:]
        proj_evac_war = {("o", fo): list(state["conv_done"])
                         for fo in range(48)}
        proj_evac_war.update({("h", fo): list(state["conv_done"])
                              for fo in range(48)})
        proj_evac_war[("a",)] = []
        proj_evac_war[("g", 0)] = list(gate_sp_d[0])
        proj_evac_war[("g", 1)] = list(gate_sp_d[1])
        state["conv_done"] = []
    last_gate_sp = gate_sp_d[0] + gate_sp_d[1]

    hbA = hbuf
    hbB = hprev
    hz = k.v(lambda e: e.memset(hbA[:, :], 0.0),
             deps=state["tmp_war"] + sq_war[0] + [Tok("c_scalar",
                                                      em.counts["c_scalar"]),
                                                  Tok("c_tensor",
                                                      em.counts["c_tensor"])])

    def rec_chunk(c, hr, hw, rd_toks, wr_war, bf_war=None):
        ld1 = k.dma("sync", ststg[:, :], states_d[c], slot="stld",
                    deps=state["stst_war"])
        ld2 = k.dma("sync", cdflat,
                    cdx_d[c:c + 1].rearrange("c h d -> c (h d)"),
                    slot="cdld",
                    deps=state.get("cdflat_war", []) + state["cdsp"])
        nt = []
        nbf = []
        for piece in range(8):
            pb = piece % 2
            bp = k.bank_cols(pb, 512)
            mmc = k.mm_group(
                [(bp, k.onesr[:, :], cdflat[0:1, piece * 512:
                                            (piece + 1) * 512], [ld2])],
                banks=[pb], extra_deps=k.consts["ones"])
            v1 = k.v(lambda e, piece=piece, bp=bp, hr=hr, hw=hw:
                     e.tensor_tensor(
                         out=hw[:, piece * 512:(piece + 1) * 512],
                         in0=hr[:, piece * 512:(piece + 1) * 512],
                         in1=bp, op=ALU.mult),
                     deps=[mmc] + rd_toks + wr_war, rb=[pb])
            v2 = k.v(lambda e, piece=piece, hw=hw: e.tensor_tensor(
                out=hw[:, piece * 512:(piece + 1) * 512],
                in0=hw[:, piece * 512:(piece + 1) * 512],
                in1=ststg[:, piece * 512:(piece + 1) * 512],
                op=ALU.add), deps=[v1, ld1])
            nt.append(v2)
            if bf_war is not None:
                v3 = k.v(lambda e, piece=piece, hw=hw: e.tensor_copy(
                    out=hb_bf[:, piece * 512:(piece + 1) * 512],
                    in_=hw[:, piece * 512:(piece + 1) * 512]),
                    deps=[v2] + bf_war)
                nbf.append(v3)
        state["stst_war"] = nt
        state["cdflat_war"] = [Tok("c_tensor", em.counts["c_tensor"])]
        return nt, nbf

    state["cdflat_war"] = list(state["conv_done"]) + sq_war[0] \
        + state["tmp_war"] + [Tok("c_tensor", em.counts["c_tensor"])]
    rd = [hz]
    for c in range(NC4):
        rd, _ = rec_chunk(c, hbA, hbA, rd, [])
    hsp = k.dma("sync", h_src[:, :], hbA[:, :], slot="hsp", deps=rd)
    hcc = emit_allgather(k, tag + "s", h_src, h_dst, [hsp])
    hld = k.dma("sync", hbA[:, :], h_dst[0:128, :], slot="hseed",
                deps=[hcc, hsp])
    hse = k.v(lambda e: e.tensor_scalar(out=hbA[:, :], in0=hbA[:, :],
                                        scalar1=acol[:, 3:4], scalar2=None,
                                        op0=ALU.mult), deps=[hld, c_fl])
    state["stst_war"] = [hse]

    GC = 32 * CL
    xn_free = [Tok("c_tensor", em.counts["c_tensor"]), t_h, xspill] \
        + list(state.get("cvsp_last", {}).values())

    def gate_r(s):
        return conv_in[:, (s % 2) * GC:(s % 2) * GC + GC]

    def bfmb_r(s):
        return conv_in[:, 2 * GC + (s % 2) * 8 * CL:
                       2 * GC + (s % 2 + 1) * 8 * CL]

    def cfmb_r(s):
        return conv_in[:, 2 * GC + 16 * CL + (s % 2) * 8 * CL:
                       2 * GC + 16 * CL + (s % 2 + 1) * 8 * CL]

    def lcbt_r(s):
        return btm[:, (s % 4) * 2 * CL:(s % 4 + 1) * 2 * CL]

    def chs_r(s):
        return xn[:, 32 * CL + DI + (s % 4) * CL:
                  32 * CL + DI + (s % 4 + 1) * CL]

    rowsl = [
        dict(A=lnscr[:, 2560:2816], Bb=lnscr[:, 2816:3073],
             rex=lnscr[0:1, 3073:3329]),
        dict(A=lnscr[:, 3329:3585], Bb=lnscr[:, 3585:3842],
             rex=xres[0:1, 15616:15872]),
        dict(A=xres[:, 14848:15104], Bb=xres[:, 15104:15361],
             rex=xres[0:1, 16128:16384]),
    ]
    p2i = [Tok("c_vector", em.counts["c_vector"]),
           Tok("c_scalar", em.counts["c_scalar"])]
    rows_war = {0: sqw0[5] + sqw0[6] + [xspill],
                1: sqw0[6] + sqw0[7] + [xspill],
                2: list(rms_rrow_war) + [xspill]}
    lcbt_war = {s: [Tok("c_vector", em.counts["c_vector"]),
                    Tok("c_tensor", em.counts["c_tensor"])]
                for s in range(4)}
    chs_war = {s: list(xn_free) for s in range(4)}
    cvsp = list(state.get("cvsp_last", {}).values())
    bfm_war = {0: p2i + cvsp, 1: p2i + cvsp}
    cfm_war = {0: p2i + cvsp, 1: p2i + cvsp}
    gld_war = {0: list(p2i), 1: list(p2i)}
    xtm_war2 = state["xtm_war"]
    cv_war = {sl: list(state["cvout_war"][sl]) for sl in range(4)}
    tmp_r = (tmpf, tmpf2)
    tmp_war_r = {0: list(state["tmp_war"]), 1: list(state["tmp2_war"])}
    for chy in range(32):
        state["yfm_war"][chy] = list(xn_free)
    ygd_sp = []
    hc_ctr = [0]
    riw = [xspill] + sqw0[5] + sqw0[6] + sqw0[7] + list(rms_rrow_war)
    rowc = []
    for s, rs in enumerate(rowsl):
        m = k.v(lambda e, rs=rs: e.memset(rs["A"][0:1, :], 1.0), deps=riw)
        rowc.append([m])
    mB = k.v(lambda e: e.memset(rowsl[0]["Bb"][0:1, :], -1.0), deps=riw)
    dprev = mB
    for s, rs in enumerate(rowsl):
        d = k.dma("sync", rs["Bb"][1:2, 0:257],
                  rowsl[0]["Bb"][0:1, 0:257], slot="rci", deps=[dprev])
        rowc[s].append(d)
        dprev = d
    rows_war[0] = rows_war[0] + [dprev]

    def yield_pass2(c, hb_rd):
        cs = c % 2
        lb = k.dma("sync", bfmb_r(cs).rearrange("p (g t) -> p g t", g=8),
                   convd.rearrange("(g p) t -> p g t", p=128)[
                       :, 32:40, c * CL:(c + 1) * CL],
                   slot=f"lb{cs}", deps=bfm_war.pop(cs, []))
        lc = k.dma("sync", cfmb_r(cs).rearrange("p (g t) -> p g t", g=8),
                   convd.rearrange("(g p) t -> p g t", p=128)[
                       :, 40:48, c * CL:(c + 1) * CL],
                   slot=f"lc{cs}", deps=cfm_war.pop(cs, []))
        lg = k.dma("sync", gate_r(cs).rearrange("p (g t) -> p g t", g=32),
                   gate_d.rearrange("(g p) t -> p g t", p=128)[
                       :, :, c * CL:(c + 1) * CL],
                   slot=f"lg{cs}", deps=gld_war.pop(cs, []) + last_gate_sp)
        cb_toks = []
        cb_mm_last = None
        for i in range(16):
            g, sti = i // 2, i % 2
            cbb = (6, 7)[i % 2]
            cps = k.bank_cols(cbb, CL)
            mmc2 = k.mm_group(
                [(cps, bfmb_r(cs)[:, g * CL + sti * 128:
                                  g * CL + sti * 128 + 128],
                  cfmb_r(cs)[:, g * CL:(g + 1) * CL], [lb, lc])],
                banks=[cbb])
            cb_mm_last = mmc2
            ct = k.v(lambda e, g=g, sti=sti, cps=cps: e.tensor_tensor(
                out=cbt8[:, (g * 2 + sti) * CL:(g * 2 + sti + 1) * CL],
                in0=cps, in1=trimask[:, sti * CL:(sti + 1) * CL],
                op=ALU.mult), deps=[mmc2, c_tri] + state["cb_war"],
                rb=[cbb])
            cb_toks.append(ct)
        state["cb_war"] = []
        bfm_war[cs] = [cb_mm_last]
        yev_toks = []
        mmy_list = []
        ch_last = None
        for chx in range(32):
            xs = chx % 2
            xb = xs * 256
            lx = k.dma("sync", cvout[:, 2 * xs * CL:(2 * xs + 1) * CL],
                       convd[chx * 128:(chx + 1) * 128,
                             c * CL:(c + 1) * CL],
                       slot=f"lx{xs}", deps=cv_war.pop(2 * xs, []))
            tx = k.dma("sync",
                       xtm[:, xb:xb + 256].rearrange("p (j q) -> p j q",
                                                     j=2),
                       cvout[:, 2 * xs * CL:(2 * xs + 1) * CL].rearrange(
                           "p (j q) -> p j q", j=2),
                       transpose=True, slot=f"xtp{xs}",
                       deps=[lx] + xtm_war2.pop(xs, []))
            cv_war[2 * xs] = [tx]
            xw = []
            for hl2 in range(2):
                hh = 2 * chx + hl2
                g = hh // 8
                p = hh % 2
                hs = hc_ctr[0] % 3
                h4 = hc_ctr[0] % 4
                hc_ctr[0] += 1
                rs = rowsl[hs]
                rA, rB, rex = rs["A"], rs["Bb"], rs["rex"]
                rw = rows_war.pop(hs, [])
                b_x = (6, 7)[p]
                bce = k.bank_cols(b_x, CL)
                at = state["acs_tok"][c // 2]
                d1 = k.dma("sync", rA[1:2, 0:CL],
                           acs[hh:hh + 1, c * CL:(c + 1) * CL],
                           slot=f"rx{hs}", deps=[at] + rw)
                if c == 0:
                    d2 = k.dma("sync", rB[0:1, 1:257],
                               acs[hh:hh + 1, 0:CL], slot=f"ry{hs}",
                               deps=[at] + rw)
                    q3 = k.s(lambda e, rB=rB, rex=rex: e.activation(
                        out=rex, in_=rB[0:1, 1:1 + CL],
                        func=AF.Exp), deps=[d2] + rw)
                    q3a = q3
                else:
                    d2 = k.dma("sync", rB[0:1, 0:257],
                               acs[hh:hh + 1, c * CL - 1:(c + 1) * CL],
                               slot=f"ry{hs}", deps=[at] + rw)
                    q3a = k.v(lambda e, rB=rB, rex=rex:
                              e.tensor_scalar(
                                  out=rex, in0=rB[0:1, 1:1 + CL],
                                  scalar1=rB[0:1, 0:1], scalar2=None,
                                  op0=ALU.subtract), deps=[d2] + rw)
                    q3 = k.s(lambda e, rex=rex: e.activation(
                        out=rex, in_=rex, func=AF.Exp), deps=[q3a])
                mme = k.mm_group([(bce, k.onesr[:, :], rex, [q3])],
                                 banks=[b_x], extra_deps=k.consts["ones"])
                chv = chs_r(h4)
                ch1 = k.v(lambda e, g=g, bce=bce, chv=chv, cs=cs:
                          e.tensor_tensor(
                              out=chv, in0=cfmb_r(cs)[:, g * CL:
                                                      (g + 1) * CL],
                              in1=bce, op=ALU.mult),
                          deps=[mme, lc] + chs_war.pop(h4, []), rb=[b_x])
                ch_last = ch1
                lv = lcbt_r(h4)
                lw = lcbt_war.pop(h4, [])
                lts = []
                for sti in range(2):
                    bidx = (0, 2)[p] + sti
                    dps = k.bank_cols(bidx, CL)
                    mmf = k.mm_group(
                        [(dps, rA[0:2, sti * 128:sti * 128 + 128],
                          rB[0:2, 1:1 + CL],
                          [d1, d2] + rowc[hs])],
                        banks=[bidx])
                    z1 = k.v(lambda e, dps=dps: e.tensor_scalar(
                        out=dps, in0=dps, scalar1=0.0, scalar2=None,
                        op0=ALU.min), deps=[mmf], rb=[bidx], wb=[bidx])
                    z2 = k.s(lambda e, sti=sti, dps=dps, lv=lv:
                             e.activation(
                                 out=lv[:, sti * CL:(sti + 1) * CL],
                                 in_=dps, func=AF.Exp),
                             deps=[z1] + lw, rb=[bidx])
                    z3 = k.v(lambda e, sti=sti, g=g, lv=lv: e.tensor_tensor(
                        out=lv[:, sti * CL:(sti + 1) * CL],
                        in0=lv[:, sti * CL:(sti + 1) * CL],
                        in1=cbt8[:, (g * 2 + sti) * CL:
                                 (g * 2 + sti + 1) * CL],
                        op=ALU.mult), deps=[z2] + cb_toks)
                    lts.append(z3)
                    mmf_last = mmf
                rows_war[hs] = [mmf_last, q3a]
                b_y = 4 + p
                yps = (k.ps[0:64, b_y * 512:b_y * 512 + CL] if p == 0
                       else k.ps[64:128, b_y * 512:b_y * 512 + CL])
                parts = [(yps, xtm[:, xb + tt * 128 + hl2 * 64:
                                   xb + tt * 128 + hl2 * 64 + 64],
                          lv[:, tt * CL:(tt + 1) * CL], [tx] + lts)
                         for tt in range(2)]
                parts.append((yps, hb_bf[:, hh * 64:(hh + 1) * 64],
                              chv, hb_rd + [ch1]))
                mmy = k.mm_group(parts, banks=[b_y])
                mmy_list.append(mmy)
                chs_war[h4] = [mmy]
                lcbt_war[h4] = [mmy]
                ydst = yfm[hl2 * 64:hl2 * 64 + 64,
                           (hh // 2) * CL:(hh // 2 + 1) * CL]
                yv = k.s(lambda e, ydst=ydst, yps=yps: e.copy(
                    out=ydst, in_=yps),
                    deps=[mmy] + state["yfm_war"][hh // 2], rb=[b_y])
                yev_toks.append(yv)
                state["yfm_war"][hh // 2] = []
                xw.append(mmy)
            xtm_war2[xs] = xw + [tx]
        cfm_war[cs] = [cb_mm_last, ch_last]
        sq_ps2 = k.bank_cols(2, CL, rows=1)
        for grpi in range(4):
            parts = []
            for chy in range(grpi * 8, grpi * 8 + 8):
                sl = chy % 8
                st = k.s(lambda e, chy=chy, sl=sl: e.activation(
                    out=sqs[sl], in_=yfm[:, chy * CL:(chy + 1) * CL],
                    func=AF.Square), deps=yev_toks[-1:] + sq_war[sl])
                parts.append((sq_ps2, k.onesc[:, 1:2], sqs[sl], [st]))
            mmt2 = k.mm_group(parts, banks=[2], extra_deps=k.consts["ones"],
                              cont=(grpi > 0), open_end=(grpi < 3))
            for sl in range(8):
                sq_war[sl] = [mmt2]
        rr2 = rrow[0:1, 4 * CL:5 * CL]
        n1 = k.v(lambda e: e.tensor_scalar(
            out=rr2, in0=sq_ps2, scalar1=1.0 / DI, scalar2=1e-6,
            op0=ALU.mult, op1=ALU.add), deps=[mmt2] + state["rrow_war"],
            rb=[2])
        n2 = k.s(lambda e: e.activation(out=rr2, in_=rr2, func=AF.Ln),
                 deps=[n1])
        n3 = k.v(lambda e: e.tensor_scalar(out=rr2, in0=rr2, scalar1=-0.5,
                                           scalar2=None, op0=ALU.mult),
                 deps=[n2])
        n4 = k.s(lambda e: e.activation(out=rr2, in_=rr2, func=AF.Exp),
                 deps=[n3])
        bps2 = k.ps[:, 2 * 512:2 * 512 + CL]
        mmb2 = k.mm_group([(bps2, k.onesr[:, :], rr2, [n4])], banks=[2],
                          extra_deps=k.consts["ones"])
        b2 = k.s(lambda e: e.copy(out=bcv, in_=bps2),
                 deps=[mmb2] + state["bcv_war"], rb=[2])
        state["rrow_war"] = [mmb2]
        yg_toks = []
        bcv_r = []
        g2_last = None
        for chy in range(32):
            yv2 = yfm[:, chy * CL:(chy + 1) * CL]
            ts = chy % 2
            tv = tmp_r[ts]
            g1 = k.v(lambda e, yv2=yv2, chy=chy, tv=tv:
                     e.scalar_tensor_tensor(
                         out=tv, in0=yv2,
                         scalar=cwcols[:, 288 + chy:289 + chy],
                         in1=bcv, op0=ALU.mult, op1=ALU.mult),
                     deps=[b2, c_cw, yev_toks[2 * chy + 1]]
                     + tmp_war_r.pop(ts, []))
            gsl = 1 + 2 * (chy % 2)
            g2 = k.s(lambda e, chy=chy, gsl=gsl, cs=cs: e.activation(
                out=cvout[:, gsl * CL:(gsl + 1) * CL],
                in_=gate_r(cs)[:, chy * CL:(chy + 1) * CL],
                func=AF.Silu), deps=[lg] + cv_war.pop(gsl, []))
            g2_last = g2
            g3 = k.v(lambda e, yv2=yv2, gsl=gsl, tv=tv: e.tensor_tensor(
                out=yv2, in0=tv, in1=cvout[:, gsl * CL:(gsl + 1) * CL],
                op=ALU.mult), deps=[g1, g2])
            tmp_war_r[ts] = [g3]
            cv_war[gsl] = [g3]
            state["yfm_war"][chy] = [g3]
            bcv_r.append(g3)
            yg_toks.append(g3)
        state["bcv_war"] = bcv_r[-1:]
        gld_war[cs] = [g2_last]
        sp = k.dma("sync",
                   ygd.rearrange("(g p) t -> p g t", p=128)[
                       :, :, c * CL:(c + 1) * CL],
                   yfm.rearrange("p (g t) -> p g t", g=32),
                   slot="ygsp", deps=yg_toks)
        ygd_sp.append(sp)
        for chy in range(32):
            state["yfm_war"][chy].append(sp)
        return mmy_list

    hb_bf = wbuf[:, 0:4096]
    hbf0 = k.v(lambda e: e.tensor_copy(out=hb_bf, in_=hbA[:, :]),
               deps=[hse] + list(xn_free))
    rd_f = [hse]
    rd_bf = [hbf0]
    prev_mmy = []
    for c in range(NC4):
        hr = hbA if c % 2 == 0 else hbB
        hw = hbB if c % 2 == 0 else hbA
        mmy_toks = yield_pass2(c, rd_bf)
        if c < NC4 - 1:
            war_w = prev_mmy if c > 0 else list(xn_free)
            rd_f, rd_bf = rec_chunk(c, hr, hw, rd_f, war_w,
                                    bf_war=mmy_toks)
        prev_mmy = mmy_toks

    xrl = k.dma("sync", xres[:, :].rearrange("p (ch t) -> p ch t", ch=NCH),
                xres_d.rearrange("(ch p) t -> p ch t", p=128),
                slot="xrl",
                deps=[Tok("c_tensor", em.counts["c_tensor"]),
                      Tok("c_vector", em.counts["c_vector"]),
                      Tok("c_scalar", em.counts["c_scalar"])])
    upd = []
    yglb = conv_in[:, 0:32 * 512]
    yld_war = [Tok("c_scalar", em.counts["c_scalar"]),
               Tok("c_vector", em.counts["c_vector"])]
    op_war = {0: [Tok("c_tensor", em.counts["c_tensor"])],
              1: [Tok("c_tensor", em.counts["c_tensor"])]}
    for tg in range(2):
        yl = k.dma("sync", yglb.rearrange("p (g t) -> p g t", g=32),
                   ygd.rearrange("(g p) t -> p g t", p=128)[
                       :, :, tg * 512:(tg + 1) * 512],
                   slot="ygl", deps=ygd_sp + yld_war)
        yld_war = []
        for dch in range(NCH):
            parts = []
            lts = []
            for hhalf in range(2):
                sl = (dch * 2 + hhalf) % 2
                wv = wbuf[:, sl * NCH * 128:(sl + 1) * NCH * 128]
                lt = k.dma("gpsimd", wv,
                           W["outproj"][dch][:, hhalf * 16:(hhalf + 1) * 16]
                           .rearrange("p ic j -> p (ic j)"),
                           slot=f"w{sl}", deps=op_war.pop(sl, []))
                lts.append(lt)
                parts += [(k.bank_cols(dch % 2, 512),
                           wv[:, ic * 128:(ic + 1) * 128],
                           yglb[:, (hhalf * 16 + ic) * 512:
                                (hhalf * 16 + ic + 1) * 512],
                           [yl, lt]) for ic in range(16)]
            pb = dch % 2
            ps = k.bank_cols(pb, 512)
            mmo = k.mm_group(parts, banks=[pb])
            for lt in lts:
                k.slot_use(lt, mmo)
            yld_war.append(mmo)
            xs = xres[:, dch * TPC + tg * 512: dch * TPC + tg * 512 + 512]
            ut = k.v(lambda e, xs=xs, ps=ps: e.tensor_tensor(
                out=xs, in0=xs, in1=ps, op=ALU.add),
                deps=[mmo, xrl], rb=[pb])
            upd.append(ut)
    return upd


def host_prep_mamba(inputs):
    ipw = inputs["L0_attn_b_in_proj_w"]
    X_, Adt, B_, C_, G_ = (ipw[0:4096], ipw[4096:4160], ipw[4160:5184],
                           ipw[5184:6208], ipw[6208:10304])
    reord = np.concatenate(
        [X_, B_, C_, G_, Adt, np.zeros((64, D), np.float32)], axis=0)
    inproj = prep_fm(np.ascontiguousarray(reord.T)).astype(BF)
    outproj = prep_fm(np.ascontiguousarray(
        inputs["L0_attn_b_out_proj_w"].T)).astype(BF)
    cw = inputs["L0_attn_b_conv_w"][:, 0, :]
    cwcols = np.zeros((128, 336), np.float32)
    for ch in range(48):
        for dd in range(4):
            cwcols[:, ch * 4 + dd] = cw[ch * 128:(ch + 1) * 128, dd]
        cwcols[:, 192 + ch] = inputs["L0_attn_b_conv_b"][ch * 128:(ch + 1) * 128]
        cwcols[:, 240 + ch] = inputs["L0_attn_b_conv_norm_w"][
            ch * 128:(ch + 1) * 128]
    for ch in range(32):
        cwcols[:, 288 + ch] = inputs["L0_attn_b_out_norm_w"][
            ch * 128:(ch + 1) * 128]
    for ch in range(16):
        cwcols[:, 320 + ch] = inputs["L0_attn_b_norm_w"][
            ch * 128:(ch + 1) * 128]
    dtw = np.ascontiguousarray(inputs["L0_attn_b_dt_w"].T)
    acols = np.zeros((128, 3), np.float32)
    acols[0:64, 0] = inputs["L0_attn_b_dt_b"]
    acols[0:64, 1] = inputs["L0_attn_b_A_log"]
    tri = np.zeros((128, 512), np.float32)
    for sti in range(2):
        for i in range(128):
            tri[i, sti * 256:(sti + 1) * 256] = (
                (sti * 128 + i) <= np.arange(256)).astype(np.float32)
    shared = dict(inproj=inproj, outproj=outproj, cwcols=cwcols, dtw=dtw,
                  acols=acols, tri=tri)
    percore = [dict(flag=np.zeros((128, 1), np.float32)),
               dict(flag=np.ones((128, 1), np.float32))]
    return shared, percore


def build_mamba_nc():
    nc = bass.Bass("TRN2", target_bir_lowering=False, debug=False)
    xr = nc.dram_tensor("xr", [D, TPC], F32, kind="ExternalInput").ap()
    yr = nc.dram_tensor("yr", [D, TPC], F32, kind="ExternalOutput").ap()
    ident = nc.dram_tensor("ident", [128, 128], F32, kind="ExternalInput").ap()
    W = {
        "inproj": nc.dram_tensor("w_inproj", [81, 128, NCH, 128], F32,
                                 kind="ExternalInput").ap(),
        "outproj": nc.dram_tensor("w_outproj", [NCH, 128, 32, 128], F32,
                                  kind="ExternalInput").ap(),
        "cwcols": nc.dram_tensor("w_cwcols", [128, 336], F32,
                                 kind="ExternalInput").ap(),
        "dtw": nc.dram_tensor("w_dtw", [64, 64], F32,
                              kind="ExternalInput").ap(),
        "acols": nc.dram_tensor("w_acols", [128, 3], F32,
                                kind="ExternalInput").ap(),
        "tri": nc.dram_tensor("w_tri", [128, 512], F32,
                              kind="ExternalInput").ap(),
        "flag": nc.dram_tensor("w_flag", [128, 1], F32,
                               kind="ExternalInput").ap(),
    }
    k = K(nc)
    k.setup_consts(ident)
    xres = k.sb_raw("xres", [128, NCH * TPC], F32).ap()
    xn = k.sb_raw("xn", [128, NCH * TPC], BF16).ap()
    scratch = k.sb_raw("lnscr", [128, 8 * 512], F32).ap()
    rows = scratch[0:1, 0:2560]
    xt = emit_load_res_fm(k, xr, xres)
    xnt, scrfree = emit_ln(k, xres, xt, xn, scratch, rows)
    upd = emit_mamba(k, "m", xn, xnt, xres, xt, W, scratch, sf_deps=scrfree)
    st = emit_store_res_fm(k, xres, yr, upd)
    k.em.finalize([st])
    return nc


def run_mamba_launch(inputs, xres_list, mode="hw", sim_cores=2):
    shared, percore = host_prep_mamba(inputs)
    nc = build_mamba_nc()
    ncores = 8 if mode == "hw" else sim_cores
    in_maps = []
    for c in range(ncores):
        m = {"xr": xres_list[c], "ident": _IDENT}
        for kk, v in shared.items():
            m["w_" + kk] = v
        for kk, v in percore[c % 2].items():
            m["w_" + kk] = v
        in_maps.append(m)
    res = run_bass_kernel_spmd(nc, in_maps, core_ids=list(range(8)))
    return [res.results[c]["yr"] for c in range(8)], nc


def build_fused_nc(cfg0, cfg1, cfg2):
    nc = bass.Bass("TRN2", target_bir_lowering=False, debug=False)
    xr = nc.dram_tensor("xr", [D, TPC], F32, kind="ExternalInput").ap()
    yr = nc.dram_tensor("yr", [D, TPC], F32, kind="ExternalOutput").ap()
    ident = nc.dram_tensor("ident", [128, 128], F32, kind="ExternalInput").ap()

    def attn_w(pfx, cfg):
        W = {}
        W["wq"] = nc.dram_tensor(pfx + "wq", [cfg["nqh"], 128, NCH, 128],
                                 BF16, kind="ExternalInput").ap()
        W["wk"] = nc.dram_tensor(pfx + "wk", [cfg["nkvh"], 128, NCH, 128],
                                 BF16, kind="ExternalInput").ap()
        W["wv"] = nc.dram_tensor(pfx + "wv", [cfg["nkvh"], 128, NCH, 128],
                                 BF16, kind="ExternalInput").ap()
        W["wo"] = nc.dram_tensor(pfx + "wo", [NCH, 128, cfg["nqh"], 128],
                                 BF16, kind="ExternalInput").ap()
        if cfg["rope"]:
            W["tabq"] = nc.dram_tensor(pfx + "tabq", [128, 2 * TPC], BF16,
                                       kind="ExternalInput").ap()
            W["tabk"] = nc.dram_tensor(
                pfx + "tabk", [128, 2 * cfg.get("kvw", 2 * TPC)], BF16,
                kind="ExternalInput").ap()
            W["rot"] = nc.dram_tensor(pfx + "rot", [128, 128], BF16,
                                      kind="ExternalInput").ap()
        if cfg["kw"] is not None:
            W["masks"] = nc.dram_tensor(pfx + "masks", [8, 128, cfg["kw"]],
                                        BF16, kind="ExternalInput").ap()
        if cfg.get("bias"):
            W["bcols"] = nc.dram_tensor(pfx + "bcols", [128, 64], F32,
                                        kind="ExternalInput").ap()
        if cfg.get("qknorm"):
            W["nscales"] = nc.dram_tensor(pfx + "nscales", [128, 2], F32,
                                          kind="ExternalInput").ap()
        return W

    def ffn_w(pfx, nI, w3):
        W = {"router": nc.dram_tensor(pfx + "router", [NCH, 128, 8], F32,
                                      kind="ExternalInput").ap(),
             "w2": nc.dram_tensor(pfx + "w2", [3, NCH, 3, 128, nI, 128],
                                  BF16, kind="ExternalInput").ap()}
        if w3:
            W["w1"] = nc.dram_tensor(pfx + "w1", [9, nI, 2, 128, NCH, 128],
                                     BF16, kind="ExternalInput").ap()
            W["w3"] = W["w1"]
        else:
            W["w1"] = nc.dram_tensor(pfx + "w1", [9, nI, 128, NCH, 128],
                                     BF16, kind="ExternalInput").ap()
        return W

    W0a = attn_w("a0_", cfg0)
    Wf0 = ffn_w("f0_", 8, True)
    Wm = {
        "inproj": nc.dram_tensor("m_inproj", [81, 128, NCH, 128], BF16,
                                 kind="ExternalInput").ap(),
        "outproj": nc.dram_tensor("m_outproj", [NCH, 128, 32, 128], BF16,
                                  kind="ExternalInput").ap(),
        "cwcols": nc.dram_tensor("m_cwcols", [128, 336], F32,
                                 kind="ExternalInput").ap(),
        "dtw": nc.dram_tensor("m_dtw", [64, 64], F32,
                              kind="ExternalInput").ap(),
        "acols": nc.dram_tensor("m_acols", [128, 3], F32,
                                kind="ExternalInput").ap(),
        "tri": nc.dram_tensor("m_tri", [128, 512], F32,
                              kind="ExternalInput").ap(),
        "flag": nc.dram_tensor("m_flag", [128, 1], F32,
                               kind="ExternalInput").ap(),
    }
    W1a = attn_w("a1_", cfg1)
    Wf1 = ffn_w("f1_", 11, False)
    W2a = attn_w("a2_", cfg2)

    k = K(nc)
    k.setup_consts(ident)
    xres = k.sb_raw("xres", [128, NCH * TPC], F32).ap()
    xn = k.sb_raw("xn", [128, NCH * TPC], BF16).ap()
    scratch = k.sb_raw("lnscr", [128, 8 * 512], F32).ap()
    rows = scratch[0:1, 0:2560]
    k.mk_arenas(46720, 768)
    xt = emit_load_res_fm(k, xr, xres)
    upd = xt

    def ln():
        return emit_ln(k, xres, upd, xn, scratch, rows)

    xnt, sf = ln()
    upd = emit_attn(k, "a0", cfg0, xn, xnt, xres, upd, W0a,
                    spare_bf=xn, spare_f32=scratch, sf_deps=sf)
    k.phase_barrier()
    xnt, sf = ln()
    upd = emit_ffn(k, "f0a", xn, xnt, xres, upd, Wf0, 1024, False,
                   scratch, sf_deps=sf)
    k.phase_barrier()
    xnt, sf = ln()
    upd = emit_mamba(k, "m", xn, xnt, xres, upd, Wm, scratch, sf_deps=sf)
    k.phase_barrier()
    xnt, sf = ln()
    upd = emit_ffn(k, "f0b", xn, xnt, xres, upd, Wf0, 1024, False,
                   scratch, sf_deps=sf)
    k.phase_barrier()
    xnt, sf = ln()
    upd = emit_attn(k, "a1", cfg1, xn, xnt, xres, upd, W1a,
                    spare_bf=xn, spare_f32=scratch, sf_deps=sf)
    k.phase_barrier()
    xnt, sf = ln()
    upd = emit_ffn(k, "f1a", xn, xnt, xres, upd, Wf1, 1408, True,
                   scratch, sf_deps=sf)
    k.phase_barrier()
    xnt, sf = ln()
    upd = emit_attn(k, "a2", cfg2, xn, xnt, xres, upd, W2a,
                    spare_bf=xn, spare_f32=scratch, sf_deps=sf)
    k.phase_barrier()
    xnt, sf = ln()
    upd = emit_ffn(k, "f1b", xn, xnt, xres, upd, Wf1, 1408, True,
                   scratch, sf_deps=sf)
    k.phase_barrier()
    st = emit_store_res_fm(k, xres, yr, upd)
    k.em.finalize([st])
    return nc


def kernel(**inputs):
    xres = _to_fm_shards(np.ascontiguousarray(inputs["L0_attn_a_x"]))
    cfg0, sh0, pc0 = host_prep_attn("l0a", inputs)
    cfg1, sh1, pc1 = host_prep_attn("l1a", inputs)
    cfg2, sh2, pc2 = host_prep_attn("l1b", inputs)
    Wf0 = host_prep_ffn(0, inputs)
    Wf1 = host_prep_ffn(1, inputs)
    shm, pcm = host_prep_mamba(inputs)
    nc = build_fused_nc(cfg0, cfg1, cfg2)
    in_maps = []
    for c in range(8):
        m = {"xr": xres[c], "ident": _IDENT}
        for kk, v in sh0.items():
            m["a0_" + kk] = v
        for kk, v in pc0[c % 2].items():
            m["a0_" + kk] = v
        for kk, v in sh1.items():
            m["a1_" + kk] = v
        for kk, v in sh2.items():
            m["a2_" + kk] = v
        for kk, v in pc2[c % 2].items():
            m["a2_" + kk] = v
        m["f0_router"] = Wf0["router"]
        m["f0_w1"] = Wf0["w1"]
        m["f0_w2"] = Wf0["w2"]
        m["f1_router"] = Wf1["router"]
        m["f1_w1"] = Wf1["w1"]
        m["f1_w2"] = Wf1["w2"]
        for kk, v in shm.items():
            m["m_" + kk] = v
        for kk, v in pcm[c % 2].items():
            m["m_" + kk] = v
        in_maps.append(m)
    res = run_bass_kernel_spmd(nc, in_maps, core_ids=list(range(8)))
    return _from_fm_shards([res.results[c]["yr"] for c in range(8)])


# version 26
# speedup vs baseline: 1.1293x; 1.1293x over previous
import contextlib
import numpy as np
import ml_dtypes

import concourse.bass as bass
import concourse.mybir as mybir
from concourse.bass_utils import run_bass_kernel_spmd

F32 = mybir.dt.float32
BF16 = mybir.dt.bfloat16
AF = mybir.ActivationFunctionType
ALU = mybir.AluOpType
AX = mybir.AxisListType

B, S, D = 4, 2048, 2048
TPC = 1024
NCH = D // 128
NEG = -1.0e30

BF = ml_dtypes.bfloat16


class Tok:
    __slots__ = ("sem", "val")

    def __init__(self, sem, val):
        self.sem, self.val = sem, val


class Emitter:

    ENGINES = ("tensor", "vector", "scalar", "gpsimd", "sync")

    def __init__(self, nc):
        self.nc = nc
        self.ctx = contextlib.ExitStack()
        self.items = {e: [] for e in self.ENGINES}
        self.sems = {}
        self.counts = {}
        self.hw = {e: {} for e in self.ENGINES}
        for e in ("tensor", "vector", "scalar", "gpsimd"):
            self._mksem("c_" + e)
        self.bank_w = [None] * 8
        self.bank_r = [[] for _ in range(8)]

    def _mksem(self, name):
        h = self.ctx.enter_context(self.nc.semaphore(name))
        self.sems[name] = h
        self.counts[name] = 0
        return name

    def dma_slot(self, name):
        if name not in self.sems:
            self._mksem(name)
        return name

    def emit(self, eng, fn, deps=(), inc="auto", inc_by=1):
        waits = []
        hw = self.hw[eng]
        for t in deps:
            if t is None:
                continue
            if hw.get(t.sem, 0) < t.val:
                waits.append((t.sem, t.val))
                hw[t.sem] = t.val
        semname = "c_" + eng if inc == "auto" else inc
        if semname is not None:
            self.counts[semname] += inc_by
            tok = Tok(semname, self.counts[semname])
            inc_pair = (semname, inc_by)
        else:
            tok, inc_pair = None, None
        self.items[eng].append((waits, fn, inc_pair))
        return tok

    def wait(self, eng, deps):
        hw = self.hw[eng]
        for t in deps:
            if t is None:
                continue
            if hw.get(t.sem, 0) < t.val:
                self.items[eng].append(([(t.sem, t.val)], None, None))
                hw[t.sem] = t.val

    def bank_write_deps(self, banks):
        deps = []
        for b in banks:
            deps.extend(self.bank_r[b])
            if self.bank_w[b] is not None:
                deps.append(self.bank_w[b])
        return deps

    def bank_set_writer(self, banks, tok):
        for b in banks:
            self.bank_w[b] = tok
            self.bank_r[b] = []

    def bank_read(self, banks, tok):
        for b in banks:
            self.bank_r[b].append(tok)

    def finalize(self, final_deps):
        self.wait("sync", list(final_deps))
        nc = self.nc
        sems = self.sems
        with nc.Block() as blk:
            for engname in self.ENGINES:
                items = self.items[engname]
                if not items:
                    continue
                section = getattr(blk, engname)

                def body(e, items=items):
                    for waits, fn, inc_pair in items:
                        for semname, val in waits:
                            e.wait_ge(sems[semname], val)
                        if fn is not None:
                            ins = fn(e)
                            if inc_pair is not None:
                                ins.then_inc(sems[inc_pair[0]], inc_pair[1])

                section(body)
        self.ctx.close()


class K:
    def __init__(self, nc):
        self.nc = nc
        self.em = Emitter(nc)
        self.ctx = self.em.ctx
        self._names = set()
        self.ps = self.sb_raw("psum32", [128, 4096], F32, psum=True)
        self._dma_ctr = 0
        self._slot_last_use = {}
        self.consts = {}

    def sb_raw(self, name, shape, dtype, psum=False):
        assert name not in self._names, name
        self._names.add(name)
        f = self.nc.psum_tensor if psum else self.nc.sbuf_tensor
        return self.ctx.enter_context(f(name, shape, dtype))

    def mk_arenas(self, bf_cols, f32_cols):
        self.abf = self.sb_raw("arena_bf", [128, bf_cols], BF16).ap()
        self.af32 = self.sb_raw("arena_f32", [128, f32_cols], F32).ap()
        self.fused = True
        self._aoff = {"bf": 0, "f32": 0}
        self._amax = {"bf": 0, "f32": 0}

    def phase_reset(self):
        for kk in self._aoff:
            self._amax[kk] = max(self._amax[kk], self._aoff[kk])
            self._aoff[kk] = 0

    def phase_barrier(self):
        em = self.em
        toks = [Tok(name, em.counts[name]) for name in em.sems
                if em.counts[name] > 0]
        for eng in Emitter.ENGINES:
            em.wait(eng, toks)
        em.bank_w = [None] * 8
        em.bank_r = [[] for _ in range(8)]
        self.phase_reset()

    def sb(self, name, shape, dtype):
        if not getattr(self, "fused", False):
            return self.sb_raw(name, shape, dtype).ap()
        akey = "bf" if dtype == BF16 else "f32"
        arena = self.abf if akey == "bf" else self.af32
        off = self._aoff[akey]
        n = shape[1]
        na = (n + 15) & ~15
        self._aoff[akey] = off + na
        assert off + na <= arena.shape[-1], (name, akey, off + na)
        return arena[0:shape[0], off:off + n]

    def dram(self, name, shape, dtype, kind="Internal"):
        assert name not in self._names, name
        self._names.add(name)
        return self.nc.dram_tensor(name, shape, dtype, kind=kind).ap()

    def bank(self, i, n=1, rows=128):
        return self.ps[0:rows, i * 512:(i + n) * 512]

    def bank_cols(self, i, cols, rows=128):
        return self.ps[0:rows, i * 512:i * 512 + cols]

    def dma(self, queue, out, in_, slot=None, deps=(), **kw):
        if slot is None:
            self._dma_ctr += 1
            slot = f"d{queue[0]}{self._dma_ctr}"
        slot = self.em.dma_slot(slot)
        deps = list(deps) + self._slot_last_use.pop(slot, [])
        prev = self.em.counts[slot]
        if prev:
            deps.append(Tok(slot, prev))
        return self.em.emit(queue, lambda e: e.dma_start(out=out, in_=in_, **kw),
                            deps=deps, inc=slot, inc_by=16)

    def slot_use(self, slot_tok, consumer_tok):
        self._slot_last_use.setdefault(slot_tok.sem, []).append(consumer_tok)

    def mm_group(self, parts, banks, extra_deps=(), cont=False,
                 open_end=False):
        n = len(parts)
        tok = None
        bdeps = (self.em.bank_write_deps(banks) if not cont else []) \
            + list(extra_deps)
        for i, (out, lhsT, rhs, deps) in enumerate(parts):
            d = list(deps) + (bdeps if i == 0 else [])
            start = (i == 0 and not cont)
            stop = (i == n - 1) and not open_end
            last = i == n - 1
            tok = self.em.emit(
                "tensor",
                (lambda e, o=out, l=lhsT, r=rhs, st=start, sp=stop:
                 e.matmul(o, l, r, start=st, stop=sp)),
                deps=d,
                inc="auto" if last else None)
        self.em.bank_set_writer(banks, tok)
        return tok

    def transpose(self, out, in_, identity, banks, deps=()):
        deps = list(deps) + self.em.bank_write_deps(banks)
        tok = self.em.emit(
            "tensor", lambda e: e.transpose(out, in_, identity), deps=deps)
        self.em.bank_set_writer(banks, tok)
        return tok

    def v(self, fn, deps=(), rb=(), wb=()):
        return self._comp("vector", fn, deps, rb, wb)

    def s(self, fn, deps=(), rb=(), wb=()):
        return self._comp("scalar", fn, deps, rb, wb)

    def g(self, fn, deps=(), rb=(), wb=()):
        return self._comp("gpsimd", fn, deps, rb, wb)

    def _comp(self, eng, fn, deps, rb, wb):
        deps = list(deps)
        for b in rb:
            if self.em.bank_w[b] is not None:
                deps.append(self.em.bank_w[b])
        if wb:
            deps += self.em.bank_write_deps(wb)
        tok = self.em.emit(eng, fn, deps=deps)
        self.em.bank_read(rb, tok)
        if wb:
            self.em.bank_set_writer(wb, tok)
        return tok

    def setup_consts(self, ident32_ap):
        c = {}
        self.onesc = self.sb_raw("ones_col", [128, 3], F32).ap()
        self.onesr = self.sb_raw("ones_row", [1, 128], F32).ap()
        self.onesrb = self.sb_raw("ones_rowb", [1, 128], BF16).ap()
        self.id32 = self.sb_raw("ident32", [128, 128], F32).ap()
        t1 = self.v(lambda e: e.memset(self.onesc[:, 0:1], 1.0 / D))
        t2 = self.v(lambda e: e.memset(self.onesc[:, 1:2], 1.0))
        t2b = self.v(lambda e: e.memset(self.onesc[:, 2:3], 1.0 / 128.0))
        t3 = self.v(lambda e: e.memset(self.onesr[:, :], 1.0))
        t3b = self.v(lambda e: e.memset(self.onesrb[:, :], 1.0))
        t4 = self.dma("gpsimd", self.id32[:, :], ident32_ap, slot="ident")
        c["ones"] = [t1, t2, t2b, t3, t3b]
        c["ident"] = [t4]
        self.consts = c


def prep_fm(Wc):
    KD, FO = Wc.shape
    assert KD % 128 == 0 and FO % 128 == 0
    return np.ascontiguousarray(
        Wc.reshape(KD // 128, 128, FO // 128, 128).transpose(2, 1, 0, 3))


def prep_rhs(Wc):
    KD, FO = Wc.shape
    assert KD % 128 == 0
    return np.ascontiguousarray(Wc.reshape(KD // 128, 128, FO))


def emit_ln(k: K, xres, xres_toks, xn, scratch, rows):
    toks = []
    sq_tok = [None] * 16
    rows_war = []
    for tg in range(2):
        c0 = tg * 512
        mu_ps = k.bank_cols(0, 512, rows=1)
        sq_ps = k.bank_cols(1, 512, rows=1)
        k.mm_group(
            [(mu_ps, k.onesc[:, 0:1], xres[:, ch * TPC + c0: ch * TPC + c0 + 512],
              list(xres_toks)) for ch in range(NCH)],
            banks=[0], extra_deps=k.consts["ones"])
        for half in range(2):
            parts = []
            for ch in range(half * 8, half * 8 + 8):
                sl = ch % 8
                st = k.s(lambda e, ch=ch, sl=sl, c0=c0: e.activation(
                    out=scratch[:, sl * 512:(sl + 1) * 512],
                    in_=xres[:, ch * TPC + c0: ch * TPC + c0 + 512],
                    func=AF.Square),
                    deps=list(xres_toks) + rows_war
                    + ([sq_tok[sl]] if sq_tok[sl] else []))
                parts.append((sq_ps, k.onesc[:, 0:1],
                              scratch[:, sl * 512:(sl + 1) * 512], [st]))
            mmt = k.mm_group(parts, banks=[1], extra_deps=k.consts["ones"],
                             cont=(half == 1), open_end=(half == 0))
            for sl in range(8):
                sq_tok[sl] = mmt
        r_mu, r_msq, r_t, r_A, r_B = (rows[0:1, i * 512:(i + 1) * 512]
                                      for i in range(5))
        t1 = k.v(lambda e: e.tensor_copy(out=r_mu, in_=mu_ps), rb=[0],
                 deps=rows_war + [sq_tok[0]])
        t2 = k.v(lambda e: e.tensor_copy(out=r_msq, in_=sq_ps), rb=[1])
        t3 = k.v(lambda e: e.tensor_tensor(out=r_t, in0=r_mu, in1=r_mu,
                                           op=ALU.mult), deps=[t1])
        t4 = k.v(lambda e: e.scalar_tensor_tensor(
            out=r_t, in0=r_msq, scalar=1.0e-5, in1=r_t,
            op0=ALU.add, op1=ALU.subtract), deps=[t2, t3])
        t5 = k.s(lambda e: e.activation(out=r_t, in_=r_t, func=AF.Ln),
                 deps=[t4])
        t5b = k.v(lambda e: e.tensor_scalar(out=r_t, in0=r_t, scalar1=-0.5,
                                            scalar2=None, op0=ALU.mult),
                  deps=[t5])
        t6 = k.s(lambda e: e.activation(out=r_A, in_=r_t, func=AF.Exp),
                 deps=[t5b])
        t7 = k.v(lambda e: e.scalar_tensor_tensor(
            out=r_B, in0=r_mu, scalar=-1.0, in1=r_A,
            op0=ALU.mult, op1=ALU.mult), deps=[t3, t6])
        bcA = k.bank_cols(2, 512)
        bcB = k.bank_cols(3, 512)
        mmA = k.mm_group([(bcA, k.onesr[:, :], r_A, [t6])], banks=[2],
                         extra_deps=k.consts["ones"])
        mmB = k.mm_group([(bcB, k.onesr[:, :], r_B, [t7])], banks=[3],
                         extra_deps=k.consts["ones"])
        rows_war = [mmA, mmB, t6, t7]
        for ch in range(NCH):
            sl = ch % 8
            tmp = scratch[:, sl * 512:(sl + 1) * 512]
            ta = k.v(lambda e, ch=ch, tmp=tmp, c0=c0: e.tensor_tensor(
                out=tmp, in0=xres[:, ch * TPC + c0: ch * TPC + c0 + 512],
                in1=bcA, op=ALU.mult),
                deps=list(xres_toks) + [sq_tok[sl]] + rows_war, rb=[2])
            tb = k.v(lambda e, ch=ch, tmp=tmp, c0=c0: e.tensor_tensor(
                out=xn[:, ch * TPC + c0: ch * TPC + c0 + 512], in0=tmp,
                in1=bcB, op=ALU.add), deps=[ta], rb=[3])
            sq_tok[sl] = tb
            toks.append(tb)
    free = [Tok(sn, k.em.counts[sn])
            for sn in ("c_tensor", "c_vector", "c_scalar")]
    return toks, free


def emit_ffn(k: K, tag, xn, xn_toks, xres, xres_toks, W, I, renorm,
             scratch=None, sf_deps=()):
    nI = I // 128
    has_w3 = "w3" in W
    em = k.em

    rw = k.sb(tag + "_rw", [128, NCH * 8], BF16)
    scores = k.sb(tag + "_scores", [128, 8], F32)
    srows = k.sb(tag + "_srows", [128, 16], F32)
    comb = k.sb(tag + "_comb", [128, 16], F32)
    combf = k.sb(tag + "_combf", [16, TPC], BF16)
    combrow = k.sb(tag + "_combrow", [1, 2 * 512], BF16)
    comb_d = k.dram(tag + "_comb_d", [16, TPC], BF16)
    probs = k.sb(tag + "_probs", [128, 8], F32)
    wsl_cols = (2 if has_w3 else 1) * NCH * 128
    w1buf = k.sb(tag + "_w1buf", [128, 2 * wsl_cols], BF16)
    if has_w3:
        w2h = [(0, nI)]
        h1s = k.sb(tag + "_h1s", [128, 4 * 512], BF16)
    else:
        icA = (nI + 1) // 2
        w2h = [(0, icA), (icA, nI - icA)]
        h1s = None
    w2cols = 3 * w2h[0][1] * 128
    w2buf = k.sb(tag + "_w2buf", [128, 2 * w2cols], BF16)
    g_all = k.sb(tag + "_gall", [128, 3 * nI * 2 * 512], BF16)
    cbc = scratch[:, 0:1024]

    def gsl_of(el, fi, tg):
        o = ((el * nI + fi) * 2 + tg) * 512
        return g_all[:, o:o + 512]

    rwt = k.dma("gpsimd", rw[:, :].rearrange("p (kc e) -> p kc e", kc=NCH),
                W["router"].rearrange("kc p e -> p kc e"), slot="rw")

    combf_toks = []
    iter_war = []
    for tt in range(8):
        ps_r = k.bank_cols(4, 8)
        mmt = k.mm_group(
            [(ps_r, xn[:, kc * TPC + tt * 128:kc * TPC + (tt + 1) * 128],
              rw[:, kc * 8:(kc + 1) * 8], list(xn_toks) + [rwt])
             for kc in range(NCH)], banks=[4])
        m1n = srows[:, 0:1]
        sums = srows[:, 1:2]
        rec = srows[:, 2:3]
        m1 = srows[:, 3:4]
        m2 = srows[:, 4:5]
        den = srows[:, 5:6]
        t0 = k.v(lambda e: e.reduce_max(out=m1n, in_=ps_r, axis=AX.X,
                                        negate=True), rb=[4], deps=iter_war)
        t1 = k.s(lambda e: e.activation(out=scores, in_=ps_r, func=AF.Exp,
                                        bias=m1n, accum_out=sums),
                 deps=[t0] + iter_war, rb=[4])
        t2 = k.v(lambda e: e.reciprocal(out=rec, in_=sums), deps=[t1])
        t3 = k.v(lambda e: e.tensor_scalar(out=probs, in0=scores, scalar1=rec,
                                           scalar2=None, op0=ALU.mult),
                 deps=[t1, t2])
        t4 = k.v(lambda e: e.reduce_max(out=m1, in_=probs, axis=AX.X), deps=[t3])
        t6 = k.v(lambda e: e.scalar_tensor_tensor(
            out=comb[:, 8:16], in0=probs, scalar=m1, in1=probs,
            op0=ALU.is_ge, op1=ALU.bypass), deps=[t4])
        t7 = k.v(lambda e: e.scalar_tensor_tensor(
            out=comb[:, 8:16], in0=comb[:, 8:16], scalar=-1.0e9, in1=probs,
            op0=ALU.mult, op1=ALU.add), deps=[t6, t3])
        t8 = k.v(lambda e: e.reduce_max(out=m2, in_=comb[:, 8:16], axis=AX.X),
                 deps=[t7])
        t9 = k.v(lambda e: e.tensor_scalar(out=comb[:, 0:8], in0=probs,
                                           scalar1=m2, scalar2=None,
                                           op0=ALU.is_ge), deps=[t8])
        t10 = k.v(lambda e: e.tensor_tensor(out=comb[:, 0:8], in0=comb[:, 0:8],
                                            in1=probs, op=ALU.mult), deps=[t9])
        last = t10
        if renorm:
            t11 = k.v(lambda e: e.tensor_tensor(out=den, in0=m1, in1=m2,
                                                op=ALU.add), deps=[t4, t8])
            t12 = k.v(lambda e: e.reciprocal(out=den, in_=den), deps=[t11])
            last = k.v(lambda e: e.tensor_scalar(out=comb[:, 0:8],
                                                 in0=comb[:, 0:8], scalar1=den,
                                                 scalar2=None, op0=ALU.mult),
                       deps=[t10, t12])
        tz = k.v(lambda e: e.memset(comb[:, 8:9], 1.0), deps=[last])
        tz2 = k.v(lambda e: e.memset(comb[:, 9:16], 0.0), deps=[tz])
        ctp = k.ps[0:16, 5 * 512:5 * 512 + 128]
        tt_t = k.transpose(ctp, comb[:, 0:16], k.id32[:, :], banks=[5],
                           deps=[last, tz, tz2] + k.consts["ident"])
        cv = k.v(lambda e, tt=tt: e.tensor_copy(
            out=combf[:, tt * 128:(tt + 1) * 128], in_=ctp), deps=[tt_t], rb=[5])
        combf_toks.append(cv)
        iter_war = [tt_t, cv, t1]
    cd = k.dma("sync", comb_d[:, :], combf[:, :], slot="cd",
               deps=combf_toks)

    groups = [(0, 1, 2), (3, 4, 5), (6, 7, 8)]
    upd_toks = []
    g_war = []
    h_war = {}
    cbc_war = list(sf_deps)
    ya_tok = {}
    w1ctr = [0]
    for gi, grp in enumerate(groups):
        gdeps = []
        for el, e_idx in enumerate(grp):
            cbts = []
            for tg in range(2):
                crv = combrow[0:1, tg * 512:(tg + 1) * 512]
                crt = k.dma("sync", crv,
                            comb_d[e_idx:e_idx + 1, tg * 512:(tg + 1) * 512],
                            slot=f"cr{tg}", deps=[cd])
                bc = k.bank_cols(6, 512)
                bmt = k.mm_group([(bc, k.onesrb[:, :], crv, [crt])],
                                 banks=[6], extra_deps=k.consts["ones"])
                k.slot_use(crt, bmt)
                cbt = k.s(lambda e, tg=tg, bc=bc: e.copy(
                    out=cbc[:, tg * 512:(tg + 1) * 512], in_=bc),
                    rb=[6], deps=cbc_war)
                cbc_war = []
                cbts.append(cbt)
            for fi in range(nI):
                sl = w1ctr[0] % 2
                w1q = "gpsimd" if sl == 0 else "sync"
                w1ctr[0] += 1
                wslab = w1buf[:, sl * wsl_cols:(sl + 1) * wsl_cols]
                w1v = wslab[:, 0:NCH * 128]
                if has_w3:
                    lt = k.dma(w1q,
                               wslab.rearrange("p (w kc j) -> p w kc j",
                                               w=2, kc=NCH),
                               W["w1"][e_idx, fi].rearrange(
                                   "w p kc j -> p w kc j"),
                               slot=f"w1{sl}")
                else:
                    lt = k.dma(w1q, w1v,
                               W["w1"][e_idx, fi].rearrange(
                                   "p kc j -> p (kc j)"),
                               slot=f"w1{sl}")
                for tg in range(2):
                    c0 = tg * 512
                    b1 = tg
                    ps_h1 = k.bank_cols(b1, 512)
                    mm1 = k.mm_group(
                        [(ps_h1, w1v[:, kc * 128:(kc + 1) * 128],
                          xn[:, kc * TPC + c0:kc * TPC + c0 + 512],
                          list(xn_toks) + [lt]) for kc in range(NCH)],
                        banks=[b1])
                    k.slot_use(lt, mm1)
                    gsl = gsl_of(el, fi, tg)
                    if has_w3:
                        hsl = h1s[:, (sl * 2 + tg) * 512:
                                  (sl * 2 + tg + 1) * 512]
                        sil = k.s(lambda e, hsl=hsl, ps_h1=ps_h1:
                                  e.activation(out=hsl, in_=ps_h1,
                                               func=AF.Silu),
                                  deps=[mm1] + h_war.pop((sl, tg), []),
                                  rb=[b1])
                        w3v = wslab[:, NCH * 128:2 * NCH * 128]
                        b3 = 2 + tg
                        ps_h3 = k.bank_cols(b3, 512)
                        mm3 = k.mm_group(
                            [(ps_h3, w3v[:, kc * 128:(kc + 1) * 128],
                              xn[:, kc * TPC + c0:kc * TPC + c0 + 512],
                              list(xn_toks) + [lt]) for kc in range(NCH)],
                            banks=[b3])
                        k.slot_use(lt, mm3)
                        hc = k.v(lambda e, gsl=gsl, ps_h3=ps_h3, tg=tg:
                                 e.tensor_tensor(
                                     out=gsl, in0=ps_h3,
                                     in1=cbc[:, tg * 512:(tg + 1) * 512],
                                     op=ALU.mult),
                                 deps=[mm3, cbts[tg]] + g_war, rb=[b3])
                        gt = k.v(lambda e, gsl=gsl, hsl=hsl: e.tensor_tensor(
                            out=gsl, in0=gsl, in1=hsl, op=ALU.mult),
                            deps=[sil, hc])
                        cbc_war.append(hc)
                        h_war[(sl, tg)] = [gt]
                    else:
                        sil = k.s(lambda e, gsl=gsl, ps_h1=ps_h1:
                                  e.activation(out=gsl, in_=ps_h1,
                                               func=AF.Silu),
                                  deps=[mm1] + g_war + h_war.pop((0, tg), []),
                                  rb=[b1])
                        gt = k.v(lambda e, gsl=gsl, tg=tg: e.tensor_tensor(
                            out=gsl, in0=gsl,
                            in1=cbc[:, tg * 512:(tg + 1) * 512],
                            op=ALU.mult), deps=[sil, cbts[tg]])
                        cbc_war.append(gt)
                        h_war[(0, tg)] = [gt]
                    gdeps.append(gt)
            g_war = []
        for dch in range(NCH):
            lts = []
            wvs = []
            for hi, (ic0, icn) in enumerate(w2h):
                sl = (dch * len(w2h) + hi) % 2
                w2q = "sync" if sl == 0 else "gpsimd"
                wv = w2buf[:, sl * w2cols: sl * w2cols + 3 * icn * 128]
                lt = k.dma(w2q,
                           wv.rearrange("p (el ic j) -> p el ic j",
                                        el=3, ic=icn),
                           W["w2"][gi, dch][:, :, ic0:ic0 + icn, :]
                           .rearrange("el p ic j -> p el ic j"),
                           slot=f"w2{sl}")
                lts.append(lt)
                wvs.append((wv, ic0, icn))
            for tg in range(2):
                b_y = 4 + tg
                ps_y = k.bank_cols(b_y, 512)
                parts = []
                for hi, (wv, ic0, icn) in enumerate(wvs):
                    for el in range(3):
                        for ici in range(icn):
                            parts.append(
                                (ps_y,
                                 wv[:, (el * icn + ici) * 128:
                                    (el * icn + ici + 1) * 128],
                                 gsl_of(el, ic0 + ici, tg),
                                 gdeps + [lts[hi]]))
                mmy = k.mm_group(parts, banks=[b_y])
                for lt in lts:
                    k.slot_use(lt, mmy)
                g_war.append(mmy)
                xs = xres[:, dch * TPC + tg * 512:dch * TPC + tg * 512 + 512]
                yd = [ya_tok[(dch, tg)]] if (dch, tg) in ya_tok else []
                ut = k.v(lambda e, xs=xs, ps_y=ps_y: e.tensor_tensor(
                    out=xs, in0=xs, in1=ps_y, op=ALU.add),
                    deps=[mmy] + yd + list(xres_toks), rb=[b_y])
                ya_tok[(dch, tg)] = ut
                if gi == 2:
                    upd_toks.append(ut)
    return upd_toks


def emit_load_res_fm(k: K, xr_ap, xres):
    t = k.dma("sync", xres[:, :].rearrange("p (ch t) -> p ch t", ch=NCH),
              xr_ap.rearrange("(ch p) t -> p ch t", p=128), slot="xload")
    return [t]


def emit_store_res_fm(k: K, xres, yr_ap, deps):
    t = k.dma("sync", yr_ap.rearrange("(ch p) t -> p ch t", p=128),
              xres[:, :].rearrange("p (ch t) -> p ch t", ch=NCH),
              slot="xstore", deps=deps)
    return t


def host_prep_ffn(L, inputs):
    if L == 0:
        gw = inputs["L0_ffn_gate_w"]
        w1 = inputs["L0_ffn_w1"]
        w2 = inputs["L0_ffn_w2"]
        w3 = inputs["L0_ffn_w3"]
        ws1, ws2, ws3 = (inputs["L0_ffn_ws1"], inputs["L0_ffn_ws2"],
                         inputs["L0_ffn_ws3"])
        I = 1024
        W1 = np.stack([prep_fm(w1[e].T) for e in range(8)] + [prep_fm(ws1.T)])
        W3 = np.stack([prep_fm(w3[e].T) for e in range(8)] + [prep_fm(ws3.T)])
        W13 = np.ascontiguousarray(np.stack([W1, W3], axis=2)).astype(BF)
        W2 = np.stack([prep_fm(w2[e].T) for e in range(8)] + [prep_fm(ws2.T)])
        W2g = np.ascontiguousarray(
            W2.reshape(3, 3, NCH, 128, I // 128, 128)
            .transpose(0, 2, 1, 3, 4, 5)).astype(BF)
        router = prep_rhs(gw.T)
        return dict(router=router, w1=W13, w2=W2g, I=I, renorm=False)
    else:
        w1 = inputs["L1_ffn_w1"]
        w2 = inputs["L1_ffn_w2"]
        ws1, ws2 = inputs["L1_ffn_w_shared1"], inputs["L1_ffn_w_shared2"]
        router = prep_rhs(inputs["L1_ffn_w_router"])
        I = 1408
        W1 = np.stack([prep_fm(w1[e]) for e in range(8)]
                      + [prep_fm(ws1)]).astype(BF)
        W2 = np.stack([prep_fm(w2[e]) for e in range(8)] + [prep_fm(ws2)])
        W2g = np.ascontiguousarray(
            W2.reshape(3, 3, NCH, 128, I // 128, 128)
            .transpose(0, 2, 1, 3, 4, 5)).astype(BF)
        return dict(router=router, w1=W1, w2=W2g, I=I, renorm=True)


def build_ffn_nc(L, I, renorm, has_w3):
    nc = bass.Bass("TRN2", target_bir_lowering=False, debug=False)
    nI = I // 128
    xr = nc.dram_tensor("xr", [D, TPC], F32, kind="ExternalInput").ap()
    yr = nc.dram_tensor("yr", [D, TPC], F32, kind="ExternalOutput").ap()
    ident = nc.dram_tensor("ident", [128, 128], F32, kind="ExternalInput").ap()
    router = nc.dram_tensor("w_router", [NCH, 128, 8], F32,
                            kind="ExternalInput").ap()
    w2 = nc.dram_tensor("w_w2", [3, NCH, 3, 128, nI, 128], F32,
                        kind="ExternalInput").ap()
    if has_w3:
        w1 = nc.dram_tensor("w_w1", [9, nI, 2, 128, NCH, 128], F32,
                            kind="ExternalInput").ap()
        W = dict(router=router, w1=w1, w2=w2, w3=w1)
    else:
        w1 = nc.dram_tensor("w_w1", [9, nI, 128, NCH, 128], F32,
                            kind="ExternalInput").ap()
        W = dict(router=router, w1=w1, w2=w2)
    k = K(nc)
    k.setup_consts(ident)
    xres = k.sb_raw("xres", [128, NCH * TPC], F32).ap()
    xn = k.sb_raw("xn", [128, NCH * TPC], BF16).ap()
    scratch = k.sb_raw("lnscr", [128, 8 * 512], F32).ap()
    rows = scratch[0:1, 0:2560]
    xt = emit_load_res_fm(k, xr, xres)
    xnt, _scrfree = emit_ln(k, xres, xt, xn, scratch, rows)
    upd = emit_ffn(k, "f", xn, xnt, xres, xt, W, I, renorm, scratch,
                   sf_deps=_scrfree)
    st = emit_store_res_fm(k, xres, yr, upd)
    k.em.finalize([st])
    return nc


_IDENT = np.eye(128, dtype=np.float32)


def run_ffn_launch(L, inputs, xres_list):
    Wp = host_prep_ffn(L, inputs)
    nc = build_ffn_nc(L, Wp["I"], Wp["renorm"], L == 0)
    in_maps = []
    for c in range(8):
        m = {"xr": xres_list[c], "ident": _IDENT, "w_router": Wp["router"],
             "w_w1": Wp["w1"], "w_w2": Wp["w2"]}
        in_maps.append(m)
    res = run_bass_kernel_spmd(nc, in_maps, core_ids=list(range(8)))
    return [res.results[c]["yr"] for c in range(8)]


def _to_fm_shards(x):
    out = []
    for c in range(8):
        b, h = c // 2, c % 2
        out.append(np.ascontiguousarray(x[b, h * TPC:(h + 1) * TPC, :].T))
    return out


def _from_fm_shards(shards):
    x = np.empty((B, S, D), np.float32)
    for c in range(8):
        b, h = c // 2, c % 2
        x[b, h * TPC:(h + 1) * TPC, :] = shards[c].T
    return x


def kernel_unfused(**inputs):
    xres = _to_fm_shards(np.ascontiguousarray(inputs["L0_attn_a_x"]))
    xres, _ = run_attn_launch("l0a", inputs, xres)
    xres = run_ffn_launch(0, inputs, xres)
    xres, _ = run_mamba_launch(inputs, xres)
    xres = run_ffn_launch(0, inputs, xres)
    xres, _ = run_attn_launch("l1a", inputs, xres)
    xres = run_ffn_launch(1, inputs, xres)
    xres, _ = run_attn_launch("l1b", inputs, xres)
    xres = run_ffn_launch(1, inputs, xres)
    return _from_fm_shards(xres)


RG_PAIRS = [[0, 1], [2, 3], [4, 5], [6, 7]]


def emit_allgather(k: K, tag, src, dst, deps):
    sem = k.em.dma_slot("cc")
    return k.em.emit(
        "gpsimd",
        lambda e: e.collective_compute(
            "AllGather", ALU.bypass, replica_groups=RG_PAIRS,
            ins=[src.opt()], outs=[dst.opt()]),
        deps=deps, inc=sem, inc_by=1)


def emit_attn(k: K, tag, cfg, xn, xn_toks, xres, xres_toks, W,
              spare_bf=None, spare_f32=None, sf_deps=()):
    nqh, nkvh = cfg["nqh"], cfg["nkvh"]
    gq, gkv = cfg["gq"], cfg["gkv"]
    ngrp = nqh // gq
    KV = cfg.get("kvw", 2 * TPC)
    full = cfg["kw"] is None
    KW = 2048 if full else cfg["kw"]
    NKC = KW // 128
    rope = cfg["rope"]
    qknorm = cfg.get("qknorm", False)
    bias = cfg.get("bias", False)
    L0SCALE = 0.08838834764831845

    xnw = k.sb(tag + "_xnw", [128, NCH * KV], BF16)
    kf = k.sb(tag + "_kf", [128, gkv * KV], BF16)
    wbuf = k.sb(tag + "_wbuf", [128, 2 * NCH * 128], BF16)
    wobuf = k.sb(tag + "_wobuf", [128, 2 * 2 * gq * 128], BF16)
    srows = k.sb(tag + "_srows", [128, 16], F32)
    ob = [0]

    def carve_bf(n):
        v = spare_bf[:, ob[0]:ob[0] + n]
        ob[0] += n
        return v

    of = [0]

    def carve_f(n):
        v = spare_f32[:, of[0]:of[0] + n]
        of[0] += n
        return v

    if full:
        pr = carve_bf(2 * KW)
        ptg = carve_bf(16 * 256)
        scrr = spare_f32[:, 0:2 * 2048]
    elif KW <= 1024:
        pr = carve_bf(2 * 2 * KW)
        ptg = carve_bf(2 * 2 * KW)
        scrr = None
    else:
        pr = carve_bf(2 * KW)
        ptg = carve_bf(2 * KW)
        scrr = k.sb(tag + "_scrr", [128, 2 * 2 * KW], BF16)
    vfm = carve_bf(KV)
    wo_every = cfg.get("wo_every", 2)
    outp = carve_bf(wo_every * gq * TPC)
    qf = carve_bf(gq * TPC)
    spare_total = NCH * TPC
    mcols = (2 * KW) if (not full) else 0
    if spare_total - ob[0] - mcols >= gkv * KV:
        vt = carve_bf(gkv * KV)
    else:
        vt = k.sb(tag + "_vt", [128, gkv * KV], BF16)
    if rope:
        tabq = k.sb(tag + "_tabq", [128, 2048], BF16)
        tabk = tabq
        ptm = k.sb(tag + "_ptm", [128, 128], BF16)
        rsc = carve_f(2048)
    if not full:
        msk = carve_bf(2 * KW)
    if qknorm:
        nsc = carve_f(1024)
        ncols = k.sb(tag + "_ncols", [128, 6], F32)
        nrows = nsc[0:1, 0:1024]
        bcsb = carve_f(1024)
    if bias:
        bcols = k.sb(tag + "_bcols", [128, 4 * NCH], F32)
    assert ob[0] <= spare_bf.shape[-1], ob[0]
    assert spare_f32 is None or of[0] <= spare_f32.shape[-1], of[0]

    ccs = k.dram(tag + "_ccs", [D, TPC], BF16)
    ccd = k.dram(tag + "_ccd", [2 * D, TPC], BF16)
    t_src = k.dma("sync", ccs.rearrange("(ch p) t -> p ch t", p=128),
                  xn.rearrange("p (ch t) -> p ch t", ch=NCH),
                  slot="src", deps=list(xn_toks))
    t_cc = emit_allgather(k, tag, ccs, ccd, [t_src])
    xnw_toks = []
    xnw_own = []
    PREV = KV - TPC
    for ch in range(NCH):
        tp = k.dma("sync", xnw[:, ch * KV: ch * KV + PREV],
                   ccd[ch * 128:(ch + 1) * 128, TPC - PREV:TPC],
                   slot=f"xw{ch % 4}", deps=[t_cc])
        to = k.v(lambda e, ch=ch: e.tensor_copy(
            out=xnw[:, ch * KV + PREV: (ch + 1) * KV],
            in_=xn[:, ch * TPC:(ch + 1) * TPC]), deps=list(xn_toks))
        xnw_toks += [tp, to]
        xnw_own.append(to)

    consts = []
    ptt = None
    sfw = list(sf_deps)
    if rope:
        ptt = k.dma("sync", ptm[:, :], W["rot"], slot="pt")
        consts.append(ptt)
    if bias:
        consts.append(k.dma("gpsimd", bcols[:, :], W["bcols"], slot="bb"))
    nconst_toks = []
    if qknorm:
        tq = k.dma("gpsimd", ncols[:, 0:2], W["nscales"], slot="ns")
        t1 = k.v(lambda e: e.tensor_scalar(
            out=ncols[:, 0:1], in0=ncols[:, 0:1], scalar1=L0SCALE,
            scalar2=None, op0=ALU.mult), deps=[tq])
        nconst_toks = [t1]

    rope_war = {}
    qn_war = []

    def rope_head(buf, c0, length, wtab, ring, ring_name, deps):
        toks = list(deps)
        out_toks = []
        blocks = [(i * 1024, 1024) for i in range(length // 1024)]
        if length % 1024:
            blocks.append((length - length % 1024, length % 1024))
        for off, wd in blocks:
            war = rope_war.pop(ring_name, [])
            rw2 = rope_war.pop("rsc", [])
            rslot = ring[:, 0:2 * wd]
            tl = k.dma("sync", rslot, wtab[:, 2 * off: 2 * off + 2 * wd],
                       slot="tb" + ring_name, deps=sfw)
            mms = []
            for sub in range(wd // 512):
                mm_ = k.mm_group(
                    [(k.bank_cols(2 + sub, 512), ptm[:, :],
                      buf[:, c0 + off + sub * 512:
                          c0 + off + sub * 512 + 512],
                      toks + [ptt])], banks=[2 + sub])
                mms.append(mm_)
            r1 = rsc[:, 0:wd]
            r2 = rsc[:, 1024:1024 + wd]
            v1 = k.v(lambda e, rslot=rslot, wd=wd, r1=r1: e.tensor_tensor(
                out=r1, in0=k.ps[:, 2 * 512: 2 * 512 + wd],
                in1=rslot[:, wd:2 * wd], op=ALU.mult),
                deps=mms + [tl] + war + rw2,
                rb=list(range(2, 2 + wd // 512)))
            g1 = k.g(lambda e, off=off, wd=wd, rslot=rslot, r2=r2:
                     e.tensor_tensor(
                         out=r2, in0=buf[:, c0 + off: c0 + off + wd],
                         in1=rslot[:, 0:wd], op=ALU.mult),
                     deps=toks + [tl] + war + rw2)
            v2 = k.v(lambda e, off=off, wd=wd, r1=r1, r2=r2: e.tensor_tensor(
                out=buf[:, c0 + off: c0 + off + wd],
                in0=r1, in1=r2, op=ALU.add), deps=[v1, g1])
            rope_war[ring_name] = [v1, g1]
            rope_war["rsc"] = [v2]
            k.slot_use(tl, v1)
            k.slot_use(tl, g1)
            out_toks.append(v2)
        return out_toks

    def qk_norm_head(buf, c0, length, is_q, deps):
        nonlocal qn_war
        scol = ncols[:, 0:1] if is_q else ncols[:, 1:2]
        out_toks = []
        for b in range(length // 1024):
            bb = c0 + b * 1024
            war = qn_war
            st = k.s(lambda e, bb=bb: e.activation(
                out=nsc[:, 0:1024], in_=buf[:, bb:bb + 1024],
                func=AF.Square), deps=list(deps) + nconst_toks + war)
            rr = nrows[0:1, 0:1024]
            mms = []
            for hh2 in range(2):
                rp = k.ps[0:1, (2 + hh2) * 512:(2 + hh2) * 512 + 512]
                mm = k.mm_group(
                    [(rp, k.onesc[:, 2:3], nsc[:, hh2 * 512:(hh2 + 1) * 512],
                      [st])], banks=[2 + hh2], extra_deps=k.consts["ones"])
                mms.append((mm, rp, hh2))
            c1a = k.v(lambda e: e.tensor_scalar(
                out=rr[0:1, 0:512], in0=mms[0][1], scalar1=1e-6, scalar2=None,
                op0=ALU.add), deps=[mms[0][0]] + war, rb=[2])
            c1b = k.v(lambda e: e.tensor_scalar(
                out=rr[0:1, 512:1024], in0=mms[1][1], scalar1=1e-6,
                scalar2=None, op0=ALU.add), deps=[mms[1][0]] + war, rb=[3])
            c2 = k.s(lambda e: e.activation(out=rr, in_=rr, func=AF.Ln),
                     deps=[c1a, c1b])
            c3 = k.v(lambda e: e.tensor_scalar(
                out=rr, in0=rr, scalar1=-0.5, scalar2=None, op0=ALU.mult),
                deps=[c2])
            c4 = k.s(lambda e: e.activation(out=rr, in_=rr, func=AF.Exp),
                     deps=[c3])
            bt = []
            for hh2 in range(2):
                bp = k.ps[:, (2 + hh2) * 512:(2 + hh2) * 512 + 512]
                mmb = k.mm_group(
                    [(bp, k.onesr[:, :], rr[0:1, hh2 * 512:(hh2 + 1) * 512],
                      [c4])], banks=[2 + hh2], extra_deps=k.consts["ones"])
                cp = k.s(lambda e, hh2=hh2, bp=bp: e.copy(
                    out=bcsb[:, hh2 * 512:(hh2 + 1) * 512], in_=bp),
                    deps=[mmb] + war, rb=[2 + hh2])
                bt.append(cp)
            c6 = k.v(lambda e, bb=bb: e.scalar_tensor_tensor(
                out=buf[:, bb:bb + 1024], in0=buf[:, bb:bb + 1024],
                scalar=scol, in1=bcsb[:, 0:1024], op0=ALU.mult,
                op1=ALU.mult), deps=bt + list(deps))
            qn_war = [c6, st]
            out_toks.append(c6)
        return out_toks

    slab_ctr = [0]

    def proj_head(wkey, fo, rhs_c0, rhs_len, dst, dst_c0, evac, war,
                  rdeps=None):
        rdeps = xnw_toks if rdeps is None else rdeps
        sl = slab_ctr[0] % 2
        slab_ctr[0] += 1
        wv = wbuf[:, sl * NCH * 128:(sl + 1) * NCH * 128]
        lt = k.dma("gpsimd", wv,
                   W[wkey][fo].rearrange("p kc j -> p (kc j)"),
                   slot=f"w{sl}")
        toks = []
        for tg in range(rhs_len // 512):
            pb = tg % 2
            ps = k.bank_cols(pb, 512)
            mm = k.mm_group(
                [(ps, wv[:, kc * 128:(kc + 1) * 128],
                  xnw[:, kc * KV + rhs_c0 + tg * 512:
                      kc * KV + rhs_c0 + tg * 512 + 512],
                  rdeps + [lt]) for kc in range(NCH)], banks=[pb])
            toks.append(evac(ps, dst, dst_c0 + tg * 512, 512, mm, pb,
                             tg % 2, war))
        k.slot_use(lt, toks[-1])
        return toks

    def evac_plain(ps, dst, c0, width, mm_tok, pbank, sl, war):
        return k.s(lambda e: e.copy(out=dst[:, c0:c0 + width], in_=ps),
                   deps=[mm_tok] + war, rb=[pbank])

    def mk_evac_bias(colidx, scale):
        def ev(ps, dst, c0, width, mm_tok, pbank, sl, war):
            col = bcols[:, colidx:colidx + 1]
            if scale is None:
                return k.v(lambda e: e.tensor_scalar(
                    out=dst[:, c0:c0 + width], in0=ps, scalar1=col,
                    scalar2=None, op0=ALU.add),
                    deps=[mm_tok] + consts + war, rb=[pbank])
            return k.v(lambda e: e.tensor_scalar(
                out=dst[:, c0:c0 + width], in0=ps, scalar1=col, scalar2=scale,
                op0=ALU.add, op1=ALU.mult),
                deps=[mm_tok] + consts + war, rb=[pbank])
        return ev

    def ev_qn_q(ps, dst, c0, width, mm_tok, pbank, sl, war):
        return qknorm_evac(ps, dst[:, c0:c0 + width], width, mm_tok, pbank,
                           True, sl, war)

    def ev_qn_k(ps, dst, c0, width, mm_tok, pbank, sl, war):
        return qknorm_evac(ps, dst[:, c0:c0 + width], width, mm_tok, pbank,
                           False, sl, war)

    def tensor_now():
        return Tok("c_tensor", k.em.counts["c_tensor"])

    donor = list(xn_toks) + xnw_toks + list(sf_deps)
    upd_toks = []
    wo_pair_deps = []
    outp_war = list(donor)
    msk_tok = {}
    msk_tt = {}
    msk_war = {0: list(donor), 1: list(donor)}
    pr_war = {0: list(donor), 1: list(donor)}
    scr_war = {0: list(donor) + list(sf_deps), 1: list(donor) + list(sf_deps)}
    ptT_war = {0: list(donor), 1: list(donor)}
    ptg_war = list(donor)
    nm_war = {}
    qf_war = [list(donor) for _ in range(gq)]
    kv_war = [[] for _ in range(gkv)]
    vt_war = [list(donor) for _ in range(gkv)]
    vfm_war = list(donor)
    bo_done = [False]
    pend = []

    def flush_pend():
        if not pend:
            return
        p = pend.pop(0)
        for hl in range(gq):
            hkl = hl // (gq // gkv)
            ps_av = k.ps[:, p["offs"][hl]:p["offs"][hl] + 128]
            avb = [p["offs"][hl] // 512]
            parts = [(ps_av,
                      vt[:, hkl * KV + (p["kstart"] // 128 + j) * 128:
                         hkl * KV + (p["kstart"] // 128 + j + 1) * 128],
                      p["dstT"][:, (hl * NKC + j) * 128:
                                (hl * NKC + j + 1) * 128],
                      p["tT"] + list(vt_war[hkl]))
                     for j in range(NKC)]
            mav = k.mm_group(parts, banks=avb)
            oc = ((p["grp"] % wo_every) * gq + hl) * TPC + p["tt"] * 128
            ot = k.s(lambda e, oc=oc, ps_av=ps_av: e.copy(
                out=outp[:, oc:oc + 128], in_=ps_av),
                deps=[mav] + outp_war, rb=avb)
            wo_pair_deps.append(ot)
        ptT_war[p["ptkey"]] = \
            [Tok("c_tensor", k.em.counts["c_tensor"])]

    pend_f = [None]
    nzh0 = [None]

    def flush_pend_f():
        nonlocal ptg_war
        p = pend_f[0]
        if p is None:
            return
        pend_f[0] = None
        ps_av = k.ps[:, 0:256]
        parts = [(ps_av,
                  vt[:, p["hkl"] * KV + j * 128:
                     p["hkl"] * KV + (j + 1) * 128],
                  ptg[:, j * 256:(j + 1) * 256],
                  p["tT"] + list(vt_war[p["hkl"]]))
                 for j in range(16)]
        mav = k.mm_group(parts, banks=[0])
        ptg_war = [mav]
        oc = ((p["grp"] % wo_every) * gq + p["hl"]) * TPC \
            + (p["tt"] - 1) * 128
        ot = k.s(lambda e, oc=oc, ps_av=ps_av: e.copy(
            out=outp[:, oc:oc + 256], in_=ps_av),
            deps=[mav] + outp_war, rb=[0])
        wo_pair_deps.append(ot)

    for grp in range(ngrp):
        pair = grp // 2
        pl = grp % 2
        for hl in range(gq):
            h = grp * gq + hl
            dst_c0 = hl * TPC
            if bias:
                toks = proj_head("wq", h, KV - TPC, TPC, qf, dst_c0,
                                 mk_evac_bias(0 * NCH + h, L0SCALE),
                                 qf_war[hl], rdeps=xnw_own)
            else:
                toks = proj_head("wq", h, KV - TPC, TPC, qf, dst_c0,
                                 evac_plain, qf_war[hl], rdeps=xnw_own)
            if qknorm:
                toks = qk_norm_head(qf, dst_c0, TPC, True, toks)
            if rope:
                toks = rope_head(qf, dst_c0, TPC, W["tabq"], tabq, "t", toks)
            qf_war[hl] = toks
        for kl in range(gkv):
            hk = grp * gkv + kl
            if bias:
                ktoks = proj_head("wk", hk, 0, KV, kf, kl * KV,
                                  mk_evac_bias(1 * NCH + hk, None), kv_war[kl])
            else:
                ktoks = proj_head("wk", hk, 0, KV, kf, kl * KV, evac_plain,
                                  kv_war[kl])
            if qknorm:
                ktoks = qk_norm_head(kf, kl * KV, KV, False, ktoks)
            if rope:
                ktoks = rope_head(kf, kl * KV, KV, W["tabk"], tabk, "t",
                                  ktoks)
            kv_war[kl] = ktoks
            if bias:
                vtoks = proj_head("wv", hk, 0, KV, vfm, 0,
                                  mk_evac_bias(2 * NCH + hk, None),
                                  vfm_war + vt_war[kl])
            else:
                vtoks = proj_head("wv", hk, 0, KV, vfm, 0, evac_plain,
                                  vfm_war + vt_war[kl])
            tv = k.dma("sync",
                       vt[:, kl * KV:(kl + 1) * KV].rearrange(
                           "p (j q) -> p j q", j=KV // 128),
                       vfm[:, 0:KV].rearrange("p (j q) -> p j q",
                                              j=KV // 128),
                       transpose=True, slot="vtp", deps=vtoks)
            vfm_war = [tv]
            vt_war[kl] = [tv]
        if full:
            tile_iter = [(hl, tt) for hl in range(gq) for tt in range(8)]
        else:
            tile_iter = []
        tctr = 0
        if not full:
            for tt in range(8):
                par = tt % 2
                tctr += 1
                if KW <= 1024:
                    s_off = par * 2048
                    s_banks = (0, 1, 2) if par == 0 else (4, 5, 6)
                    kstart = tt * 128
                    use_ev = False
                else:
                    s_off = 0
                    s_banks = (0, 1, 2, 3, 4)
                    kstart = tt * 128
                    use_ev = True

                def ensure_mask(tti):
                    sl_ = tti % 2
                    if msk_tt.get(sl_) != tti:
                        mt_ = k.dma("sync", msk[:, sl_ * KW:(sl_ + 1) * KW],
                                    W["masks"][tti], slot=f"mk{sl_}",
                                    deps=msk_war.pop(sl_, []))
                        msk_tok[sl_] = mt_
                        msk_tt[sl_] = tti
                mslot = tt % 2
                mv = msk[:, mslot * KW:(mslot + 1) * KW]
                ensure_mask(tt)
                if tt + 1 < 8:
                    ensure_mask(tt + 1)
                sc_toks = []
                for hl in range(gq):
                    hkl = hl // (gq // gkv)
                    qtile = qf[:, hl * TPC + tt * 128:
                               hl * TPC + (tt + 1) * 128]
                    qdeps = list(qf_war[hl]) + list(kv_war[hkl])
                    p0 = s_off + hl * KW
                    c0 = 0
                    while c0 < KW:
                        w = min(512 - (p0 + c0) % 512, KW - c0)
                        ps_j = k.ps[:, p0 + c0: p0 + c0 + w]
                        mmt = k.mm_group(
                            [(ps_j, qtile,
                              kf[:, hkl * KV + kstart + c0:
                                 hkl * KV + kstart + c0 + w], qdeps)],
                            banks=list(s_banks) if (c0 == 0 and hl == 0)
                            else [])
                        sc_toks.append(mmt)
                        c0 += w
                if use_ev:
                    base = scrr[:, par * 2 * KW:(par + 1) * 2 * KW]
                    ev = k.s(lambda e, base=base: e.copy(
                        out=base, in_=k.ps[:, s_off:s_off + 2 * KW]),
                        deps=sc_toks + scr_war.pop(par, []),
                        rb=list(s_banks))
                    bdeps = [ev]
                else:
                    base = k.ps[:, s_off:s_off + 2 * KW]
                    bdeps = sc_toks
                am0 = k.v(lambda e, base=base, mv=mv: e.tensor_tensor(
                    out=base[:, 0:KW], in0=base[:, 0:KW], in1=mv,
                    op=ALU.add),
                    deps=bdeps + [msk_tok[mslot]],
                    **({} if use_ev else dict(rb=list(s_banks),
                                              wb=list(s_banks))))
                am1 = k.v(lambda e, base=base, mv=mv: e.tensor_tensor(
                    out=base[:, KW:2 * KW], in0=base[:, KW:2 * KW], in1=mv,
                    op=ALU.add), deps=bdeps + [msk_tok[mslot], am0])
                msk_war.setdefault(mslot, []).extend([am0, am1])
                nmp = srows[:, par * 2:(par + 1) * 2]
                smp = srows[:, 4 + par * 2: 6 + par * 2]
                rcp = srows[:, 8 + par * 2: 10 + par * 2]
                mx = k.v(lambda e, base=base, nmp=nmp: e.reduce_max(
                    out=nmp, in_=base.rearrange("p (h w) -> p h w", h=2),
                    axis=AX.X, negate=True),
                    deps=[am0, am1] + nm_war.pop(par, []),
                    **({} if use_ev else dict(rb=list(s_banks))))
                prv = pr[:, par * 2 * KW:(par + 1) * 2 * KW] if KW <= 1024 \
                    else pr[:, 0:2 * KW]
                prw = pr_war.pop(par if KW <= 1024 else 0, [])
                exs = []
                for hl in range(gq):
                    ex = k.s(lambda e, base=base, prv=prv, hl=hl, nmp=nmp,
                             smp=smp: e.activation(
                                 out=prv[:, hl * KW:(hl + 1) * KW],
                                 in_=base[:, hl * KW:(hl + 1) * KW],
                                 func=AF.Exp, bias=nmp[:, hl:hl + 1],
                                 accum_out=smp[:, hl:hl + 1]),
                             deps=[mx] + prw,
                             **({} if use_ev else dict(rb=list(s_banks))))
                    exs.append(ex)
                nm_war[par] = list(exs)
                if use_ev:
                    scr_war[par] = list(exs)
                rc = k.v(lambda e, smp=smp, rcp=rcp: e.reciprocal(
                    out=rcp, in_=smp), deps=exs)
                nzs = []
                for hl in range(gq):
                    nz = k.v(lambda e, prv=prv, hl=hl, rcp=rcp:
                             e.tensor_scalar(
                                 out=prv[:, hl * KW:(hl + 1) * KW],
                                 in0=prv[:, hl * KW:(hl + 1) * KW],
                                 scalar1=rcp[:, hl:hl + 1], scalar2=None,
                                 op0=ALU.mult), deps=[rc])
                    nzs.append(nz)
                def emit_tT(key):
                    ptw = ptT_war.pop(key, [])
                    t_ = k.dma("sync",
                               dstT.rearrange("p (j q) -> p j q", j=2 * NKC),
                               prv.rearrange("p (j q) -> p j q", j=2 * NKC),
                               transpose=True, slot="pT", deps=nzs + ptw)
                    pr_war[key] = [t_]
                    return t_
                if KW <= 1024:
                    if len(pend) >= 2:
                        flush_pend()
                    dstT = ptg[:, par * 2 * KW:(par + 1) * 2 * KW]
                    tT = emit_tT(par)
                    av_offs = [s_off + 1536 + hl * 128 for hl in range(gq)]
                else:
                    flush_pend()
                    dstT = ptg[:, 0:2 * KW]
                    tT = emit_tT(0)
                    av_offs = [2560 + par * 768 + hl * 128
                               for hl in range(gq)]
                pend.append(dict(tT=[tT], dstT=dstT, kstart=kstart, grp=grp,
                                 tt=tt, offs=av_offs,
                                 ptkey=(par if KW <= 1024 else 0)))
            while pend:
                flush_pend()
        for hl, tt in tile_iter:
            hkl = hl // (gq // gkv)
            par = tctr % 2
            tctr += 1
            s_off = par * 2048
            s_banks = (0, 1, 2, 3) if par == 0 else (4, 5, 6, 7)
            qtile = qf[:, hl * TPC + tt * 128: hl * TPC + (tt + 1) * 128]
            qdeps = list(qf_war[hl]) + list(kv_war[hkl])
            sc_toks = []
            for j in range(4):
                c0 = j * 512
                ps_j = k.ps[:, s_off + c0: s_off + c0 + 512]
                mmt = k.mm_group(
                    [(ps_j, qtile,
                      kf[:, hkl * KV + c0: hkl * KV + c0 + 512], qdeps)],
                    banks=list(s_banks) if j == 0 else [])
                sc_toks.append(mmt)
            ps_full = k.ps[:, s_off: s_off + KW]
            scrv = scrr[:, par * 2048:(par + 1) * 2048]
            ev = k.s(lambda e, ps_full=ps_full, scrv=scrv: e.copy(
                out=scrv, in_=ps_full),
                deps=sc_toks + scr_war.pop(par, []), rb=list(s_banks))
            nm = srows[:, par:par + 1]
            sm = srows[:, 2 + par:3 + par]
            mx = k.v(lambda e, scrv=scrv, nm=nm: e.reduce_max(
                out=nm, in_=scrv, axis=AX.X, negate=True),
                deps=[ev] + nm_war.pop(par, []))
            prv = pr[:, par * KW:(par + 1) * KW]
            ex = k.s(lambda e, scrv=scrv, prv=prv, nm=nm, sm=sm:
                     e.activation(out=prv, in_=scrv, func=AF.Exp,
                                  bias=nm, accum_out=sm),
                     deps=[mx] + pr_war.pop(par, []))
            nm_war[par] = [ex]
            scr_war[par] = [ex]
            rc = k.v(lambda e, sm=sm, par=par: e.reciprocal(
                out=srows[:, 4 + par:5 + par], in_=sm), deps=[ex])
            nz = k.v(lambda e, prv=prv, par=par: e.tensor_scalar(
                out=prv, in0=prv, scalar1=srows[:, 4 + par:5 + par],
                scalar2=None, op0=ALU.mult), deps=[rc])
            if tt % 2 == 0:
                nzh0[0] = (nz, prv)
            else:
                flush_pend_f()
                tTs = []
                for half, (nzv, prvv) in enumerate((nzh0[0], (nz, prv))):
                    dstT = ptg.rearrange("p (j q) -> p j q", j=16)[
                        :, :, half * 128:(half + 1) * 128]
                    t_ = k.dma("sync", dstT,
                               prvv.rearrange("p (j q) -> p j q", j=16),
                               transpose=True, slot="pT",
                               deps=[nzv] + ptg_war)
                    tTs.append(t_)
                pr_war[0] = [tTs[0]]
                pr_war[1] = [tTs[1]]
                pend_f[0] = dict(tT=tTs, hl=hl, hkl=hkl, grp=grp, tt=tt)
        if full:
            flush_pend_f()
        tnow = [tensor_now()]
        for hl in range(gq):
            qf_war[hl] = qf_war[hl] + tnow
        for kl in range(gkv):
            kv_war[kl] = kv_war[kl] + tnow
            vt_war[kl] = vt_war[kl] + tnow
        if grp % wo_every == wo_every - 1:
            nh = wo_every * gq
            h0 = (grp // wo_every) * nh
            outp_war = []
            for dch in range(NCH):
                wsl = dch % 2
                wv = wobuf[:, wsl * nh * 128:(wsl + 1) * nh * 128]
                lt = k.dma("gpsimd",
                           wv.rearrange("p (h j) -> p h j", h=nh),
                           W["wo"][dch][:, h0:h0 + nh, :],
                           slot=f"wo{wsl}")
                for tg in range(2):
                    pb = tg % 2
                    ps = k.bank_cols(pb, 512)
                    parts = [(ps, wv[:, hl4 * 128:(hl4 + 1) * 128],
                              outp[:, hl4 * TPC + tg * 512:
                                   hl4 * TPC + tg * 512 + 512],
                              wo_pair_deps + [lt]) for hl4 in range(nh)]
                    mmw = k.mm_group(parts, banks=[pb])
                    k.slot_use(lt, mmw)
                    xs = xres[:, dch * TPC + tg * 512:
                              dch * TPC + tg * 512 + 512]
                    if bias and not bo_done[0]:
                        ut = k.v(lambda e, xs=xs, ps=ps, dch=dch:
                                 e.scalar_tensor_tensor(
                                     out=xs, in0=ps,
                                     scalar=bcols[:, 3 * NCH + dch:
                                                  3 * NCH + dch + 1],
                                     in1=xs, op0=ALU.add, op1=ALU.add),
                                 deps=[mmw] + list(xres_toks) + consts, rb=[pb])
                    else:
                        ut = k.v(lambda e, xs=xs, ps=ps: e.tensor_tensor(
                            out=xs, in0=xs, in1=ps, op=ALU.add),
                            deps=[mmw] + list(xres_toks), rb=[pb])
                    outp_war.append(ut)
                    upd_toks.append(ut)
            bo_done[0] = True
            wo_pair_deps = []
    return upd_toks


def _rope_tables_il(pos, scale):
    hd = 128
    inv = 1.0 / (10000.0 ** (np.arange(0, hd, 2, dtype=np.float64) / hd))
    fr = pos[None, :].astype(np.float64) * inv[:, None]
    c = np.repeat(np.cos(fr), 2, axis=0) * scale
    s = np.repeat(np.sin(fr), 2, axis=0) * scale
    return c.astype(np.float32), s.astype(np.float32)


def _rope_tables_half(pos, scale):
    hd = 128
    frac = np.arange(0, hd, 2, dtype=np.float64) / hd
    ts = 10000.0 ** frac
    fr = pos[None, :].astype(np.float64) / ts[:, None]
    c = np.concatenate([np.cos(fr), np.cos(fr)], axis=0) * scale
    s = np.concatenate([np.sin(fr), np.sin(fr)], axis=0) * scale
    return c.astype(np.float32), s.astype(np.float32)


def _pack_tab(c, s_):
    T = c.shape[1]
    out = np.empty((128, 2 * T), np.float32)
    blocks = [(i * 1024, 1024) for i in range(T // 1024)]
    if T % 1024:
        blocks.append((T - T % 1024, T % 1024))
    for off, wd in blocks:
        out[:, 2 * off: 2 * off + wd] = c[:, off:off + wd]
        out[:, 2 * off + wd: 2 * off + 2 * wd] = s_[:, off:off + wd]
    return out.astype(BF)


def _rot_pt(mode):
    P = np.zeros((128, 128), np.float32)
    if mode == "il":
        for i in range(64):
            P[2 * i, 2 * i + 1] = -1.0
            P[2 * i + 1, 2 * i] = 1.0
    else:
        for i in range(64):
            P[i, i + 64] = -1.0
            P[i + 64, i] = 1.0
    return np.ascontiguousarray(P.T).astype(BF)


def host_prep_attn(which, inputs):
    if which == "l0a":
        wq = prep_fm(inputs["L0_attn_a_Wq"].T).astype(BF)
        wk = prep_fm(inputs["L0_attn_a_Wk"].T).astype(BF)
        wv = prep_fm(inputs["L0_attn_a_Wv"].T).astype(BF)
        wo = prep_fm(inputs["L0_attn_a_Wo"].T).astype(BF)
        KW, WIN = 640, 512
        cfg = dict(nqh=16, nkvh=16, gq=2, gkv=2, kw=KW, rope="il",
                   qknorm=False, bias=False, kvw=1536)
        shared = dict(wq=wq, wk=wk, wv=wv, wo=wo, rot=_rot_pt("il"))
        percore = []
        for h2 in range(2):
            pos_q = np.arange(TPC) + h2 * TPC
            pos_k = np.maximum(np.arange(1536) + h2 * TPC - 512, 0)
            scale = 1.0 / np.sqrt(128.0)
            cq, sq = _rope_tables_il(pos_q, scale)
            ck, sk = _rope_tables_il(pos_k, 1.0)
            tabq = _pack_tab(cq, sq)
            tabk = _pack_tab(ck, sk)
            masks = np.full((8, 128, KW), NEG, np.float32)
            for tt in range(8):
                for i in range(128):
                    p = h2 * TPC + tt * 128 + i
                    for j in range(KW):
                        kg = h2 * TPC + tt * 128 + 512 + j - TPC
                        if i < j <= i + WIN and kg >= 0:
                            masks[tt, i, j] = 0.0
            percore.append(dict(tabq=tabq, tabk=tabk,
                                masks=masks.astype(BF)))
        return cfg, shared, percore
    if which == "l1a":
        wq = prep_fm(inputs["L1_attn_a_wq"].T).astype(BF)
        wk = prep_fm(inputs["L1_attn_a_wk"].T).astype(BF)
        wv = prep_fm(inputs["L1_attn_a_wv"].T).astype(BF)
        wo = prep_fm(inputs["L1_attn_a_wo"].T).astype(BF)
        bcols = np.zeros((128, 64), np.float32)
        bcols[:, 0:16] = inputs["L1_attn_a_bq"].reshape(16, 128).T
        bcols[:, 16:32] = inputs["L1_attn_a_bk"].reshape(16, 128).T
        bcols[:, 32:48] = inputs["L1_attn_a_bv"].reshape(16, 128).T
        bcols[:, 48:64] = inputs["L1_attn_a_bo"].reshape(16, 128).T
        cfg = dict(nqh=16, nkvh=16, gq=2, gkv=2, kw=None, rope=None,
                   qknorm=False, bias=True)
        shared = dict(wq=wq, wk=wk, wv=wv, wo=wo, bcols=bcols)
        return cfg, shared, [dict(), dict()]
    if which == "l1b":
        qk = inputs["L1_attn_b_q_kernel"]
        kvk = inputs["L1_attn_b_kv_kernel"]
        ok = inputs["L1_attn_b_out_kernel"]
        wq = prep_fm(np.ascontiguousarray(
            qk.transpose(1, 0, 2).reshape(D, 16 * 128))).astype(BF)
        wk = prep_fm(np.ascontiguousarray(
            kvk[0].transpose(1, 0, 2).reshape(D, 8 * 128))).astype(BF)
        wv = prep_fm(np.ascontiguousarray(
            kvk[1].transpose(1, 0, 2).reshape(D, 8 * 128))).astype(BF)
        wo = prep_fm(np.ascontiguousarray(
            ok.reshape(16 * 128, D))).astype(BF)
        nsc = np.zeros((128, 2), np.float32)
        nsc[:, 0] = inputs["L1_attn_b_q_norm_scale"]
        nsc[:, 1] = inputs["L1_attn_b_k_norm_scale"]
        KW, WIN = 1152, 1024
        cfg = dict(nqh=16, nkvh=8, gq=2, gkv=1, kw=KW, rope="half",
                   qknorm=True, bias=False, wo_every=1)
        shared = dict(wq=wq, wk=wk, wv=wv, wo=wo, rot=_rot_pt("half"),
                      nscales=nsc)
        percore = []
        for h2 in range(2):
            pos_q = np.arange(TPC) + h2 * TPC
            pos_k = np.maximum(np.arange(2 * TPC) + h2 * TPC - TPC, 0)
            cq, sq = _rope_tables_half(pos_q, 1.0)
            ck, sk = _rope_tables_half(pos_k, 1.0)
            tabq = _pack_tab(cq, sq)
            tabk = _pack_tab(ck, sk)
            masks = np.full((8, 128, KW), NEG, np.float32)
            for tt in range(8):
                for i in range(128):
                    for j in range(KW):
                        kg = h2 * TPC + tt * 128 + j - TPC
                        if i < j <= i + WIN and kg >= 0:
                            masks[tt, i, j] = 0.0
            percore.append(dict(tabq=tabq, tabk=tabk,
                                masks=masks.astype(BF)))
        return cfg, shared, percore
    raise ValueError(which)


def build_attn_nc(which, cfg, shared, percore):
    nc = bass.Bass("TRN2", target_bir_lowering=False, debug=False)
    xr = nc.dram_tensor("xr", [D, TPC], F32, kind="ExternalInput").ap()
    yr = nc.dram_tensor("yr", [D, TPC], F32, kind="ExternalOutput").ap()
    ident = nc.dram_tensor("ident", [128, 128], F32, kind="ExternalInput").ap()
    W = {}
    nfo_q = cfg["nqh"]
    nfo_kv = cfg["nkvh"]
    W["wq"] = nc.dram_tensor("w_wq", [nfo_q, 128, NCH, 128], F32,
                             kind="ExternalInput").ap()
    W["wk"] = nc.dram_tensor("w_wk", [nfo_kv, 128, NCH, 128], F32,
                             kind="ExternalInput").ap()
    W["wv"] = nc.dram_tensor("w_wv", [nfo_kv, 128, NCH, 128], F32,
                             kind="ExternalInput").ap()
    W["wo"] = nc.dram_tensor("w_wo", [NCH, 128, nfo_q, 128], F32,
                             kind="ExternalInput").ap()
    if cfg["rope"]:
        W["tabq"] = nc.dram_tensor("w_tabq", [128, 2 * TPC], BF16,
                                   kind="ExternalInput").ap()
        W["tabk"] = nc.dram_tensor(
            "w_tabk", [128, 2 * cfg.get("kvw", 2 * TPC)], BF16,
            kind="ExternalInput").ap()
        W["rot"] = nc.dram_tensor("w_rot", [128, 128], BF16,
                                  kind="ExternalInput").ap()
    if cfg["kw"] is not None:
        W["masks"] = nc.dram_tensor("w_masks", [8, 128, cfg["kw"]], BF16,
                                    kind="ExternalInput").ap()
    if cfg.get("bias"):
        W["bcols"] = nc.dram_tensor("w_bcols", [128, 64], F32,
                                    kind="ExternalInput").ap()
    if cfg.get("qknorm"):
        W["nscales"] = nc.dram_tensor("w_nscales", [128, 2], F32,
                                      kind="ExternalInput").ap()
    k = K(nc)
    k.setup_consts(ident)
    xres = k.sb_raw("xres", [128, NCH * TPC], F32).ap()
    xn = k.sb_raw("xn", [128, NCH * TPC], BF16).ap()
    scratch = k.sb_raw("lnscr", [128, 8 * 512], F32).ap()
    rows = scratch[0:1, 0:2560]
    xt = emit_load_res_fm(k, xr, xres)
    xnt, scrfree = emit_ln(k, xres, xt, xn, scratch, rows)
    upd = emit_attn(k, "a", cfg, xn, xnt, xres, xt, W,
                    spare_bf=xn, spare_f32=scratch, sf_deps=scrfree)
    st = emit_store_res_fm(k, xres, yr, upd)
    k.em.finalize([st])
    return nc


def run_attn_launch(which, inputs, xres_list, mode="hw", sim_cores=2):
    cfg, shared, percore = host_prep_attn(which, inputs)
    nc = build_attn_nc(which, cfg, shared, percore)
    in_maps = []
    ncores = 8 if mode == "hw" else sim_cores
    for c in range(ncores):
        m = {"xr": xres_list[c], "ident": _IDENT}
        for kk, v in shared.items():
            m["w_" + kk] = v
        for kk, v in percore[c % 2].items():
            m["w_" + kk] = v
        in_maps.append(m)
    res = run_bass_kernel_spmd(nc, in_maps, core_ids=list(range(8)))
    return [res.results[c]["yr"] for c in range(8)], nc


def emit_mamba(k: K, tag, xn, xn_toks, xres, xres_toks, W, lnscr,
               sf_deps=()):
    em = k.em
    DI, CVD, NFO, HALO, NC4, CL = 4096, 6144, 81, 32, 4, 256

    xres_d = k.dram(tag + "_xres_d", [D, TPC], F32)
    ccs_m = k.dram(tag + "_ccs", [D, HALO], BF16)
    ccd_m = k.dram(tag + "_ccd", [2 * D, HALO], BF16)
    gate_d = k.dram(tag + "_gate_d", [DI, TPC], BF16)
    convd = k.dram(tag + "_convd", [CVD, TPC], BF16)
    states_d = k.dram(tag + "_states_d", [NC4, 128, DI], BF16)
    cdx_d = k.dram(tag + "_cdx_d", [NC4, 64, 64], F32)
    h_src = k.dram(tag + "_hsrc", [128, DI], F32)
    h_dst = k.dram(tag + "_hdst", [256, DI], F32)
    ygd = k.dram(tag + "_ygd", [DI, TPC], BF16)

    conv_in = k.sb(tag + "_cvin", [128, 48 * 544], BF16)
    wbuf = k.sb(tag + "_wbuf", [128, 2 * 2 * NCH * 128], BF16)
    halo_sb = k.sb(tag + "_halo", [128, NCH * HALO], BF16)
    cwcols = k.sb(tag + "_cw", [128, 336], F32)
    dtw = k.sb(tag + "_dtw", [64, 64], F32)
    acol = k.sb(tag + "_acol", [128, 4], F32)
    cvout = k.sb(tag + "_cvout", [128, 4 * CL], BF16)
    xtm = k.sb(tag + "_xtm", [128, 4 * 128], BF16)
    btm = k.sb(tag + "_btm", [128, 16 * 128], BF16)
    cbt8 = wbuf[:, 4096:8192]
    bdf = k.sb(tag + "_bdf", [128, 4 * 128], BF16)
    gstg = k.sb(tag + "_gstg", [128, 1024], BF16)
    dftm = k.sb(tag + "_dftm", [128, 2 * 64], F32)
    trimask = k.sb(tag + "_tri", [128, 2 * CL], BF16)
    rrow = xres[0:1, 48 * CL + 2560: 48 * CL + 2560 + 5 * CL]
    cdxt = k.sb(tag + "_cdxt", [64, 64], F32)
    ones64 = k.sb(tag + "_o64", [64, 64], F32)
    ststg = k.sb(tag + "_stst", [128, DI], BF16)

    sqs_defined_early = True
    conv_raw = xres[:, 0:48 * CL]
    adt = xres[:, 48 * CL:48 * CL + 1024]
    acs = xres[:, 48 * CL + 1024: 48 * CL + 2048]
    awrk = xres[:, 48 * CL + 2048: 48 * CL + 2560]
    hbuf = conv_raw[:, 0:DI]
    cdflat = conv_raw[0:1, DI:2 * DI]
    yfm = xn[:, 0:32 * CL]
    hprev = xn[:, 32 * CL:32 * CL + DI]

    addn = []
    for ch in range(NCH):
        t = k.v(lambda e, ch=ch: e.tensor_tensor(
            out=xres[:, ch * TPC:(ch + 1) * TPC],
            in0=xres[:, ch * TPC:(ch + 1) * TPC],
            in1=xn[:, ch * TPC:(ch + 1) * TPC], op=ALU.add),
            deps=list(xres_toks) + list(xn_toks))
        addn.append(t)
    xspill = k.dma("sync", xres_d.rearrange("(ch p) t -> p ch t", p=128),
                   xres[:, :].rearrange("p (ch t) -> p ch t", ch=NCH),
                   slot="xsp", deps=addn)

    c_cw = k.dma("gpsimd", cwcols[:, :], W["cwcols"], slot="ccw")
    sqs0 = [lnscr[:, i * 512:(i + 1) * 512] for i in range(8)]
    sqw0 = [list(sf_deps) for _ in range(8)]
    rmsrow = rrow[0:1, 0:512]
    xn2_toks = []
    for tg in range(2):
        c0 = tg * 512
        rps = k.bank_cols(3, 512, rows=1)
        for grpi in range(2):
            parts = []
            for ch in range(grpi * 8, grpi * 8 + 8):
                sl = ch % 8
                st = k.s(lambda e, ch=ch, sl=sl, c0=c0: e.activation(
                    out=sqs0[sl], in_=xn[:, ch * TPC + c0:ch * TPC + c0 + 512],
                    func=AF.Square), deps=list(xn_toks) + sqw0[sl] + addn)
                parts.append((rps, k.onesc[:, 1:2], sqs0[sl], [st]))
            mmr0 = k.mm_group(parts, banks=[3], extra_deps=k.consts["ones"],
                              cont=(grpi == 1), open_end=(grpi == 0))
            for sl in range(8):
                sqw0[sl] = [mmr0]
        w0 = xn2_toks[-1:]
        m1 = k.v(lambda e: e.tensor_scalar(
            out=rmsrow[0:1, 0:512], in0=rps, scalar1=1.0 / D, scalar2=1e-6,
            op0=ALU.mult, op1=ALU.add), deps=[mmr0, xspill] + w0, rb=[3])
        m2 = k.s(lambda e: e.activation(out=rmsrow[0:1, 0:512],
                                        in_=rmsrow[0:1, 0:512], func=AF.Ln),
                 deps=[m1])
        m3 = k.v(lambda e: e.tensor_scalar(
            out=rmsrow[0:1, 0:512], in0=rmsrow[0:1, 0:512], scalar1=-0.5,
            scalar2=None, op0=ALU.mult), deps=[m2])
        m4 = k.s(lambda e: e.activation(out=rmsrow[0:1, 0:512],
                                        in_=rmsrow[0:1, 0:512], func=AF.Exp),
                 deps=[m3])
        bps0 = k.bank_cols(2, 512)
        mmb0 = k.mm_group([(bps0, k.onesr[:, :], rmsrow[0:1, 0:512], [m4])],
                          banks=[2], extra_deps=k.consts["ones"])
        for ch in range(NCH):
            tt = k.v(lambda e, ch=ch, c0=c0, bps0=bps0: e.scalar_tensor_tensor(
                out=xn[:, ch * TPC + c0:ch * TPC + c0 + 512],
                in0=xn[:, ch * TPC + c0:ch * TPC + c0 + 512],
                scalar=cwcols[:, 320 + ch:321 + ch], in1=bps0,
                op0=ALU.mult, op1=ALU.mult),
                deps=[mmb0, c_cw] + sqw0[ch % 8], rb=[2])
            xn2_toks.append(tt)
    xn_toks = xn2_toks
    rms_rrow_war = [mmb0] + xn2_toks[-1:]

    t_h = k.dma("sync", ccs_m.rearrange("(ch p) t -> p ch t", p=128),
                xn.rearrange("p (ch t) -> p ch t", ch=NCH)[:, :, TPC - HALO:],
                slot="hsrc", deps=list(xn_toks))
    t_cc = emit_allgather(k, tag + "h", ccs_m, ccd_m, [t_h])
    th_l = k.dma("sync", halo_sb[:, :].rearrange("p (ch t) -> p ch t", ch=NCH),
                 ccd_m[0:D].rearrange("(ch p) t -> p ch t", p=128),
                 slot="hld", deps=[t_cc])
    c_dtw = k.dma("gpsimd", dtw[:, :], W["dtw"], slot="cdtw")
    c_ac = k.dma("gpsimd", acol[:, 0:3], W["acols"], slot="cac")
    c_fl = k.dma("gpsimd", acol[:, 3:4], W["flag"], slot="cfl")
    c_tri = k.dma("gpsimd", trimask[:, :], W["tri"], slot="ctri")
    t_o = k.v(lambda e: e.memset(ones64[:, :], 1.0))
    a1 = k.s(lambda e: e.activation(out=acol[:, 2:3], in_=acol[:, 1:2],
                                    func=AF.Exp), deps=[c_ac])
    a2 = k.v(lambda e: e.tensor_scalar(out=acol[:, 2:3], in0=acol[:, 2:3],
                                       scalar1=-1.0, scalar2=None,
                                       op0=ALU.mult), deps=[a1])
    th_f = k.v(lambda e: e.tensor_scalar(
        out=halo_sb[:, :], in0=halo_sb[:, :], scalar1=acol[:, 3:4],
        scalar2=None, op0=ALU.mult), deps=[th_l, c_fl])
    halo_toks = [th_f]

    slabc = [0]
    sq_war = [[] for _ in range(8)]
    sqs = [lnscr[:, i * CL:(i + 1) * CL] for i in range(8)]
    tmpf = lnscr[:, 8 * CL: 9 * CL]
    tmpf2 = lnscr[:, 9 * CL:10 * CL]
    bcv = awrk[:, 256:512]
    dfw = awrk[:, 0:256]

    state = dict(cvout_war=[[] for _ in range(8)], xtm_war={0: [], 1: []},
                 btm_war=[],
                 cb_war=[], lc_war=[], chs_war=[], bdf_war=[], dftm_war=[],
                 yfm_war=[[] for _ in range(32)], stst_war=[], cdxt_war=[],
                 rrow_war=rms_rrow_war, bcv_war=[xspill], tmp_war=[],
                 tmp2_war=[],
                 acs_tok=[None, None], dfz_war=[xspill], adt_done=[],
                 conv_done=[xspill], cdsp=[])

    def ablock(th):
        own0 = th * 512
        z = awrk[0:64, 0:512]
        z2 = acs[0:64, th * 512:(th + 1) * 512]
        ps = k.bank_cols(3, 512, rows=64)
        mmz = k.mm_group([(ps, dtw[:, :], adt[0:64, own0:own0 + 512],
                           state["adt_done"] + [c_dtw])], banks=[3])
        w0 = state["dfz_war"]
        e0 = k.v(lambda e: e.tensor_scalar(
            out=z, in0=ps, scalar1=acol[0:64, 0:1], scalar2=None,
            op0=ALU.add), deps=[mmz, c_ac] + w0, rb=[3])
        e1a = k.v(lambda e: e.tensor_scalar(out=z2, in0=z, scalar1=-1.0,
                                            scalar2=None, op0=ALU.mult),
                  deps=[e0] + w0)
        e1 = k.v(lambda e: e.tensor_tensor(out=z2, in0=z2, in1=z,
                                           op=ALU.min), deps=[e1a])
        e2 = k.s(lambda e: e.activation(out=z2, in_=z2, func=AF.Exp),
                 deps=[e1])
        e3 = k.v(lambda e: e.tensor_scalar(out=z2, in0=z2, scalar1=1.0,
                                           scalar2=None, op0=ALU.add),
                 deps=[e2])
        e4 = k.s(lambda e: e.activation(out=z2, in_=z2, func=AF.Ln),
                 deps=[e3])
        e5 = k.v(lambda e: e.tensor_scalar(out=z, in0=z, scalar1=0.0,
                                           scalar2=None, op0=ALU.max),
                 deps=[e4])
        e6 = k.v(lambda e: e.tensor_tensor(out=z, in0=z, in1=z2, op=ALU.add),
                 deps=[e5])
        e7 = k.v(lambda e: e.tensor_scalar(out=z, in0=z,
                                           scalar1=acol[0:64, 2:3],
                                           scalar2=None, op0=ALU.mult),
                 deps=[e6, a2])
        if th == 0:
            e8 = k.v(lambda e: e.tensor_tensor_scan(
                out=acs[0:64, 0:512], data0=z, data1=z, initial=0.0,
                op0=ALU.add, op1=ALU.bypass),
                deps=[e7] + ([state["acs_tok"][0]] if state["acs_tok"][0]
                             else []))
        else:
            e8 = k.v(lambda e: e.tensor_tensor_scan(
                out=acs[0:64, 512:1024], data0=z, data1=z,
                initial=acs[0:64, 511:512], op0=ALU.add, op1=ALU.bypass),
                deps=[e7, state["acs_tok"][0]])
        state["acs_tok"][th] = e8
        state["dfz_war"] = [e8]
        return e8

    def cdx(c):
        at = state["acs_tok"][c // 2]
        e_i = (c + 1) * CL - 1
        w = state["cdxt_war"]
        col = cdxt[:, 0:1]
        if c == 0:
            t1 = k.v(lambda e: e.tensor_copy(out=col, in_=acs[0:64, 255:256]),
                     deps=[at] + w)
        else:
            t1 = k.v(lambda e, e_i=e_i, c=c: e.tensor_scalar(
                out=col, in0=acs[0:64, e_i:e_i + 1],
                scalar1=acs[0:64, c * CL - 1:c * CL], op0=ALU.subtract,
                scalar2=None), deps=[at] + w)
        t2 = k.s(lambda e: e.activation(out=col, in_=col, func=AF.Exp),
                 deps=[t1])
        t3 = k.v(lambda e: e.tensor_scalar(out=cdxt[:, :], in0=ones64[:, :],
                                           scalar1=col, scalar2=None,
                                           op0=ALU.mult), deps=[t2, t_o] + w)
        sp = k.dma("sync", cdx_d[c], cdxt[0:64, :], slot="cdsp",
                   deps=[t3])
        state["cdxt_war"] = [sp]
        state.setdefault("cdsp", []).append(sp)
        return sp

    def pass1(c):
        th = c // 2
        c0 = HALO + (c % 2) * CL
        at = state["acs_tok"][th]
        sq_ps = k.bank_cols(2, CL, rows=1)
        first = True
        cr_toks = []
        for grpi in range(6):
            parts = []
            for chi in range(grpi * 8, grpi * 8 + 8):
                sl = chi % 8
                cr = conv_raw[:, chi * CL:(chi + 1) * CL]
                ci = conv_in[:, chi * 544:(chi + 1) * 544]
                t0 = k.v(lambda e, ci=ci, cr=cr, c0=c0, chi=chi:
                         e.tensor_scalar(
                             out=cr, in0=ci[:, c0 - 3:c0 - 3 + CL],
                             scalar1=cwcols[:, chi * 4:chi * 4 + 1],
                             scalar2=cwcols[:, 192 + chi:193 + chi],
                             op0=ALU.mult, op1=ALU.add),
                         deps=state["conv_evt"][chi] + [c_cw]
                         + state.setdefault("crw", {}).pop(chi, []))
                for dd in (1, 2, 3):
                    t0 = k.v(lambda e, ci=ci, cr=cr, c0=c0, chi=chi, dd=dd:
                             e.scalar_tensor_tensor(
                                 out=cr, in0=ci[:, c0 - 3 + dd:
                                                c0 - 3 + dd + CL],
                                 scalar=cwcols[:, chi * 4 + dd:
                                               chi * 4 + dd + 1],
                                 in1=cr, op0=ALU.mult, op1=ALU.add),
                             deps=[t0])
                cr_toks.append(t0)
                st = k.s(lambda e, cr=cr, sl=sl: e.activation(
                    out=sqs[sl], in_=cr, func=AF.Square),
                    deps=[t0] + sq_war[sl])
                state["crw"][chi] = [st]
                parts.append((sq_ps, k.onesc[:, 1:2], sqs[sl], [st]))
            mmt = k.mm_group(parts, banks=[2], extra_deps=k.consts["ones"],
                             cont=(grpi > 0), open_end=(grpi < 5))
            for sl in range(8):
                sq_war[sl] = [mmt]
        rr = rrow[0:1, 4 * CL:5 * CL]
        r1 = k.v(lambda e: e.tensor_scalar(
            out=rr, in0=sq_ps, scalar1=1.0 / CVD, scalar2=1e-6,
            op0=ALU.mult, op1=ALU.add), deps=[mmt] + state["rrow_war"],
            rb=[2])
        r2 = k.s(lambda e: e.activation(out=rr, in_=rr, func=AF.Ln), deps=[r1])
        r3 = k.v(lambda e: e.tensor_scalar(out=rr, in0=rr, scalar1=-0.5,
                                           scalar2=None, op0=ALU.mult),
                 deps=[r2])
        r4 = k.s(lambda e: e.activation(out=rr, in_=rr, func=AF.Exp),
                 deps=[r3])
        bps = k.ps[:, 2 * 512: 2 * 512 + CL]
        mmb = k.mm_group([(bps, k.onesr[:, :], rr, [r4])], banks=[2],
                         extra_deps=k.consts["ones"])
        b1 = k.s(lambda e: e.copy(out=bcv, in_=bps),
                 deps=[mmb] + state["bcv_war"], rb=[2])
        state["rrow_war"] = [mmb]
        w = state["dfz_war"]
        d1 = k.v(lambda e, c=c: e.tensor_scalar(
            out=dfw[0:64, :], in0=acs[0:64, c * CL:(c + 1) * CL],
            scalar1=acs[0:64, (c + 1) * CL - 1:(c + 1) * CL],
            scalar2=-1.0, op0=ALU.subtract, op1=ALU.mult),
            deps=[at] + w)
        d2 = k.s(lambda e: e.activation(out=dfw[0:64, :], in_=dfw[0:64, :],
                                        func=AF.Exp), deps=[d1])
        state["dfz_war"] = [d2]
        dtts = []
        for tt in range(2):
            tps = k.ps[:, 3 * 512:3 * 512 + 64]
            mt = k.transpose(tps, dfw[0:64, tt * 128:(tt + 1) * 128],
                             k.id32[0:64, 0:64], banks=[3],
                             deps=[d2] + k.consts["ident"])
            ev = k.v(lambda e, tt=tt, tps=tps: e.tensor_copy(
                out=dftm[:, tt * 64:(tt + 1) * 64], in_=tps),
                deps=[mt] + state["dftm_war"], rb=[3])
            dtts.append(ev)
        state["dftm_war"] = dtts
        state["dfz_war"] = dtts
        cdx(c)
        stst_toks = []
        order = list(range(32, 40)) + list(range(40, 48)) + list(range(32))
        u1_last = None
        for chi in order:
            sl = chi % 4
            ts1 = chi % 2
            twk = "tmp_war" if ts1 == 0 else "tmp2_war"
            tv = (tmpf, tmpf2)[ts1]
            cr = conv_raw[:, chi * CL:(chi + 1) * CL]
            u1 = k.v(lambda e, cr=cr, chi=chi, tv=tv: e.scalar_tensor_tensor(
                out=tv, in0=cr, scalar=cwcols[:, 240 + chi:241 + chi],
                in1=bcv, op0=ALU.mult, op1=ALU.mult),
                deps=cr_toks[-1:] + [b1, c_cw] + state[twk])
            u1_last = u1
            cv = cvout[:, sl * CL:(sl + 1) * CL]
            u2 = k.s(lambda e, cv=cv, tv=tv: e.activation(out=cv, in_=tv,
                                                          func=AF.Silu),
                     deps=[u1] + state["cvout_war"][sl])
            state[twk] = [u2]
            sp = k.dma("sync", convd[chi * 128:(chi + 1) * 128,
                                     c * CL:(c + 1) * CL], cv,
                       slot=f"cvsp{sl % 2}", deps=[u2])
            state.setdefault("cvsp_last", {})[sl % 2] = sp
            war_new = [sp]
            if 32 <= chi < 40:
                g = chi - 32
                tb = k.dma("sync",
                           btm[:, g * 256:(g + 1) * 256].rearrange(
                               "p (j q) -> p j q", j=2),
                           cv.rearrange("p (j q) -> p j q", j=2),
                           transpose=True, slot="btp",
                           deps=[u2] + state["btm_war"])
                state["btm_war"] = []
                war_new.append(tb)
                state["btm_tok"] = state.get("btm_tok", []) + [tb]
            elif chi < 32:
                xs1 = chi % 2
                xb1 = xs1 * 256
                tx = k.dma("sync",
                           xtm[:, xb1:xb1 + 256].rearrange(
                               "p (j q) -> p j q", j=2),
                           cv.rearrange("p (j q) -> p j q", j=2),
                           transpose=True, slot=f"xtp{xs1}",
                           deps=[u2] + state["xtm_war"][xs1])
                war_new.append(tx)
                xw = []
                for hh in (2 * chi, 2 * chi + 1):
                    bo1 = (hh % 2) * 256
                    bdfw = state.setdefault("bdfw", {})
                    bdts = []
                    for tt in range(2):
                        bb = k.v(lambda e, hh=hh, tt=tt, bo1=bo1:
                                 e.tensor_scalar(
                                     out=bdf[:, bo1 + tt * 128:
                                             bo1 + (tt + 1) * 128],
                                     in0=btm[:, ((hh // 8) * 2 + tt) * 128:
                                             ((hh // 8) * 2 + tt + 1) * 128],
                                     scalar1=dftm[:, tt * 64 + hh:
                                                  tt * 64 + hh + 1],
                                     scalar2=None, op0=ALU.mult),
                                 deps=state["btm_tok"] + dtts
                                 + bdfw.pop(hh % 2, []))
                        bdts.append(bb)
                    sps = k.bank_cols(4 + (hh % 2), 64)
                    mmS = k.mm_group(
                        [(sps, bdf[:, bo1 + tt * 128:bo1 + (tt + 1) * 128],
                          xtm[:, xb1 + tt * 128 + (hh % 2) * 64:
                              xb1 + tt * 128 + (hh % 2) * 64 + 64],
                          [tx] + bdts) for tt in range(2)],
                        banks=[4 + (hh % 2)])
                    bdfw[hh % 2] = [mmS]
                    ev = k.v(lambda e, hh=hh, sps=sps: e.tensor_copy(
                        out=ststg[:, hh * 64:(hh + 1) * 64], in_=sps),
                        deps=[mmS] + state["stst_war"], rb=[4 + (hh % 2)])
                    stst_toks.append(ev)
                    xw.append(mmS)
                state["xtm_war"][xs1] = xw + [tx]
            state["cvout_war"][sl] = war_new
        ssp = k.dma("sync", states_d[c], ststg[:, :], slot="stsp",
                    deps=stst_toks)
        state["stst_war"] = [ssp]
        state["bcv_war"] = [u1_last]
        state["btm_war"] = [Tok("c_vector", em.counts["c_vector"])]
        state["btm_tok"] = []
        state["conv_done"] = [Tok("c_tensor", em.counts["c_tensor"]),
                              Tok("c_vector", em.counts["c_vector"]),
                              Tok("c_scalar", em.counts["c_scalar"])]

    for sl in range(8):
        sq_war[sl] = list(sqw0[sl])
    proj_evac_war = {("a",): [xspill]}
    gate_sp_d = {0: [], 1: []}
    for th in range(2):
        own0 = th * 512
        pair_lt = {}
        for fo in range(NFO):
            is_gate = 48 <= fo < 80
            is_adt = fo == 80
            fop = fo & ~1
            if fo == fop:
                sl = slabc[0] % 2
                slabc[0] += 1
                n_fo = 2 if fop + 1 < NFO else 1
                wpair = wbuf[:, sl * 4096: sl * 4096 + n_fo * 2048]
                lt = k.dma("gpsimd" if sl == 0 else "sync",
                           wpair.rearrange("p (f kc j) -> p f kc j",
                                           f=n_fo, kc=NCH),
                           W["inproj"][fop:fop + n_fo].rearrange(
                               "f p kc j -> p f kc j"),
                           slot=f"w{sl}")
                pair_lt[fop] = (lt, wpair)
            lt, wpair = pair_lt[fop]
            wv = wpair[:, (fo - fop) * 2048:(fo - fop + 1) * 2048]
            last_use = None
            if not (is_gate or is_adt):
                ps = k.bank_cols(2, HALO)
                if th == 0:
                    parts = [(ps, wv[:, kc * 128:(kc + 1) * 128],
                              halo_sb[:, kc * HALO:(kc + 1) * HALO],
                              halo_toks + [lt]) for kc in range(NCH)]
                else:
                    parts = [(ps, wv[:, kc * 128:(kc + 1) * 128],
                              xn[:, kc * TPC + 480: kc * TPC + 512],
                              list(xn_toks) + [lt]) for kc in range(NCH)]
                mmh = k.mm_group(parts, banks=[2])
                dst = conv_in[:, fo * 544: fo * 544 + HALO]
                e1 = k.s(lambda e, dst=dst, ps=ps: e.copy(out=dst, in_=ps),
                         deps=[mmh] + proj_evac_war.pop(("h", fo), []),
                         rb=[2])
                proj_evac_war[("h", fo)] = []
                last_use = e1
            pb = fo % 2
            ps = k.bank_cols(pb, 512)
            parts = [(ps, wv[:, kc * 128:(kc + 1) * 128],
                      xn[:, kc * TPC + own0: kc * TPC + own0 + 512],
                      list(xn_toks) + [lt]) for kc in range(NCH)]
            mmo = k.mm_group(parts, banks=[pb])
            if is_adt:
                ev = k.v(lambda e, ps=ps, own0=own0: e.tensor_copy(
                    out=adt[:, own0:own0 + 512], in_=ps),
                    deps=[mmo] + proj_evac_war.pop(("a",), []), rb=[pb])
                proj_evac_war[("a",)] = []
                state["adt_done"].append(ev)
            elif is_gate:
                gsl = (fo - 48) % 2
                gs = gstg[:, gsl * 512:(gsl + 1) * 512]
                ev = k.v(lambda e, ps=ps, gs=gs: e.tensor_copy(
                    out=gs, in_=ps),
                    deps=[mmo] + proj_evac_war.pop(("g", gsl), []), rb=[pb])
                sp = k.dma("sync",
                           gate_d[(fo - 48) * 128:(fo - 47) * 128,
                                  own0:own0 + 512], gs,
                           slot=f"gsp{gsl}", deps=[ev])
                proj_evac_war[("g", gsl)] = [sp]
                gate_sp_d[gsl] = [sp]
            else:
                dst = conv_in[:, fo * 544 + HALO: fo * 544 + 544]
                ev = k.s(lambda e, dst=dst, ps=ps: e.copy(out=dst, in_=ps),
                         deps=[mmo] + proj_evac_war.pop(("o", fo), []),
                         rb=[pb])
                proj_evac_war[("o", fo)] = []
            k.slot_use(lt, ev if last_use is None else last_use)
            k.slot_use(lt, ev)
            if not (is_gate or is_adt):
                state["conv_done"].append(ev)
                state.setdefault("conv_evt", {})[fo] = [ev] + (
                    [last_use] if last_use is not None else [])
                if last_use is not None:
                    state["conv_done"].append(last_use)
        ablock(th)
        for c in (2 * th, 2 * th + 1):
            pass1(c)
            state["conv_done"] = state["conv_done"][-3:]
        proj_evac_war = {("o", fo): list(state["conv_done"])
                         for fo in range(48)}
        proj_evac_war.update({("h", fo): list(state["conv_done"])
                              for fo in range(48)})
        proj_evac_war[("a",)] = []
        proj_evac_war[("g", 0)] = list(gate_sp_d[0])
        proj_evac_war[("g", 1)] = list(gate_sp_d[1])
        state["conv_done"] = []
    last_gate_sp = gate_sp_d[0] + gate_sp_d[1]

    hbA = hbuf
    hbB = hprev
    hz = k.v(lambda e: e.memset(hbA[:, :], 0.0),
             deps=state["tmp_war"] + sq_war[0] + [Tok("c_scalar",
                                                      em.counts["c_scalar"]),
                                                  Tok("c_tensor",
                                                      em.counts["c_tensor"])])

    def rec_chunk(c, hr, hw, rd_toks, wr_war, bf_war=None):
        ld1 = k.dma("sync", ststg[:, :], states_d[c], slot="stld",
                    deps=state["stst_war"])
        ld2 = k.dma("sync", cdflat,
                    cdx_d[c:c + 1].rearrange("c h d -> c (h d)"),
                    slot="cdld",
                    deps=state.get("cdflat_war", []) + state["cdsp"])
        nt = []
        nbf = []
        for piece in range(8):
            pb = piece % 2
            bp = k.bank_cols(pb, 512)
            mmc = k.mm_group(
                [(bp, k.onesr[:, :], cdflat[0:1, piece * 512:
                                            (piece + 1) * 512], [ld2])],
                banks=[pb], extra_deps=k.consts["ones"])
            v1 = k.v(lambda e, piece=piece, bp=bp, hr=hr, hw=hw:
                     e.tensor_tensor(
                         out=hw[:, piece * 512:(piece + 1) * 512],
                         in0=hr[:, piece * 512:(piece + 1) * 512],
                         in1=bp, op=ALU.mult),
                     deps=[mmc] + rd_toks + wr_war, rb=[pb])
            v2 = k.v(lambda e, piece=piece, hw=hw: e.tensor_tensor(
                out=hw[:, piece * 512:(piece + 1) * 512],
                in0=hw[:, piece * 512:(piece + 1) * 512],
                in1=ststg[:, piece * 512:(piece + 1) * 512],
                op=ALU.add), deps=[v1, ld1])
            nt.append(v2)
            if bf_war is not None:
                v3 = k.v(lambda e, piece=piece, hw=hw: e.tensor_copy(
                    out=hb_bf[:, piece * 512:(piece + 1) * 512],
                    in_=hw[:, piece * 512:(piece + 1) * 512]),
                    deps=[v2] + bf_war)
                nbf.append(v3)
        state["stst_war"] = nt
        state["cdflat_war"] = [Tok("c_tensor", em.counts["c_tensor"])]
        return nt, nbf

    state["cdflat_war"] = list(state["conv_done"]) + sq_war[0] \
        + state["tmp_war"] + [Tok("c_tensor", em.counts["c_tensor"])]
    rd = [hz]
    for c in range(NC4):
        rd, _ = rec_chunk(c, hbA, hbA, rd, [])
    hsp = k.dma("sync", h_src[:, :], hbA[:, :], slot="hsp", deps=rd)
    hcc = emit_allgather(k, tag + "s", h_src, h_dst, [hsp])
    hld = k.dma("sync", hbA[:, :], h_dst[0:128, :], slot="hseed",
                deps=[hcc, hsp])
    hse = k.v(lambda e: e.tensor_scalar(out=hbA[:, :], in0=hbA[:, :],
                                        scalar1=acol[:, 3:4], scalar2=None,
                                        op0=ALU.mult), deps=[hld, c_fl])
    state["stst_war"] = [hse]

    GC = 32 * CL
    xn_free = [Tok("c_tensor", em.counts["c_tensor"]), t_h, xspill] \
        + list(state.get("cvsp_last", {}).values())

    def gate_r(s):
        return conv_in[:, (s % 2) * GC:(s % 2) * GC + GC]

    def bfmb_r(s):
        return conv_in[:, 2 * GC + (s % 2) * 8 * CL:
                       2 * GC + (s % 2 + 1) * 8 * CL]

    def cfmb_r(s):
        return conv_in[:, 2 * GC + 16 * CL + (s % 2) * 8 * CL:
                       2 * GC + 16 * CL + (s % 2 + 1) * 8 * CL]

    def lcbt_r(s):
        return btm[:, (s % 4) * 2 * CL:(s % 4 + 1) * 2 * CL]

    def chs_r(s):
        return xn[:, 32 * CL + DI + (s % 4) * CL:
                  32 * CL + DI + (s % 4 + 1) * CL]

    rowsl = [
        dict(A=lnscr[:, 2560:2816], Bb=lnscr[:, 2816:3073],
             rex=lnscr[0:1, 3073:3329]),
        dict(A=lnscr[:, 3329:3585], Bb=lnscr[:, 3585:3842],
             rex=xres[0:1, 15616:15872]),
        dict(A=xres[:, 14848:15104], Bb=xres[:, 15104:15361],
             rex=xres[0:1, 16128:16384]),
    ]
    p2i = [Tok("c_vector", em.counts["c_vector"]),
           Tok("c_scalar", em.counts["c_scalar"])]
    rows_war = {0: sqw0[5] + sqw0[6] + [xspill],
                1: sqw0[6] + sqw0[7] + [xspill],
                2: list(rms_rrow_war) + [xspill]}
    lcbt_war = {s: [Tok("c_vector", em.counts["c_vector"]),
                    Tok("c_tensor", em.counts["c_tensor"])]
                for s in range(4)}
    chs_war = {s: list(xn_free) for s in range(4)}
    cvsp = list(state.get("cvsp_last", {}).values())
    bfm_war = {0: p2i + cvsp, 1: p2i + cvsp}
    cfm_war = {0: p2i + cvsp, 1: p2i + cvsp}
    gld_war = {0: list(p2i), 1: list(p2i)}
    xtm_war2 = state["xtm_war"]
    cv_war = {sl: list(state["cvout_war"][sl]) for sl in range(4)}
    tmp_r = (tmpf, tmpf2)
    tmp_war_r = {0: list(state["tmp_war"]), 1: list(state["tmp2_war"])}
    for chy in range(32):
        state["yfm_war"][chy] = list(xn_free)
    ygd_sp = []
    hc_ctr = [0]
    riw = [xspill] + sqw0[5] + sqw0[6] + sqw0[7] + list(rms_rrow_war)
    rowc = []
    for s, rs in enumerate(rowsl):
        m = k.v(lambda e, rs=rs: e.memset(rs["A"][0:1, :], 1.0), deps=riw)
        rowc.append([m])
    mB = k.v(lambda e: e.memset(rowsl[0]["Bb"][0:1, :], -1.0), deps=riw)
    dprev = mB
    for s, rs in enumerate(rowsl):
        d = k.dma("sync", rs["Bb"][1:2, 0:257],
                  rowsl[0]["Bb"][0:1, 0:257], slot="rci", deps=[dprev])
        rowc[s].append(d)
        dprev = d
    rows_war[0] = rows_war[0] + [dprev]

    def yield_pass2(c, hb_rd):
        cs = c % 2
        lb = k.dma("sync", bfmb_r(cs).rearrange("p (g t) -> p g t", g=8),
                   convd.rearrange("(g p) t -> p g t", p=128)[
                       :, 32:40, c * CL:(c + 1) * CL],
                   slot=f"lb{cs}", deps=bfm_war.pop(cs, []))
        lc = k.dma("sync", cfmb_r(cs).rearrange("p (g t) -> p g t", g=8),
                   convd.rearrange("(g p) t -> p g t", p=128)[
                       :, 40:48, c * CL:(c + 1) * CL],
                   slot=f"lc{cs}", deps=cfm_war.pop(cs, []))
        lg = k.dma("sync", gate_r(cs).rearrange("p (g t) -> p g t", g=32),
                   gate_d.rearrange("(g p) t -> p g t", p=128)[
                       :, :, c * CL:(c + 1) * CL],
                   slot=f"lg{cs}", deps=gld_war.pop(cs, []) + last_gate_sp)
        cb_toks = []
        cb_mm_last = None
        for i in range(16):
            g, sti = i // 2, i % 2
            cbb = (6, 7)[i % 2]
            cps = k.bank_cols(cbb, CL)
            mmc2 = k.mm_group(
                [(cps, bfmb_r(cs)[:, g * CL + sti * 128:
                                  g * CL + sti * 128 + 128],
                  cfmb_r(cs)[:, g * CL:(g + 1) * CL], [lb, lc])],
                banks=[cbb])
            cb_mm_last = mmc2
            ct = k.v(lambda e, g=g, sti=sti, cps=cps: e.tensor_tensor(
                out=cbt8[:, (g * 2 + sti) * CL:(g * 2 + sti + 1) * CL],
                in0=cps, in1=trimask[:, sti * CL:(sti + 1) * CL],
                op=ALU.mult), deps=[mmc2, c_tri] + state["cb_war"],
                rb=[cbb])
            cb_toks.append(ct)
        state["cb_war"] = []
        bfm_war[cs] = [cb_mm_last]
        yev_toks = []
        mmy_list = []
        ch_last = None
        for chx in range(32):
            xs = chx % 2
            xb = xs * 256
            lx = k.dma("sync", cvout[:, 2 * xs * CL:(2 * xs + 1) * CL],
                       convd[chx * 128:(chx + 1) * 128,
                             c * CL:(c + 1) * CL],
                       slot=f"lx{xs}", deps=cv_war.pop(2 * xs, []))
            tx = k.dma("sync",
                       xtm[:, xb:xb + 256].rearrange("p (j q) -> p j q",
                                                     j=2),
                       cvout[:, 2 * xs * CL:(2 * xs + 1) * CL].rearrange(
                           "p (j q) -> p j q", j=2),
                       transpose=True, slot=f"xtp{xs}",
                       deps=[lx] + xtm_war2.pop(xs, []))
            cv_war[2 * xs] = [tx]
            xw = []
            for hl2 in range(2):
                hh = 2 * chx + hl2
                g = hh // 8
                p = hh % 2
                hs = hc_ctr[0] % 3
                h4 = hc_ctr[0] % 4
                hc_ctr[0] += 1
                rs = rowsl[hs]
                rA, rB, rex = rs["A"], rs["Bb"], rs["rex"]
                rw = rows_war.pop(hs, [])
                b_x = (6, 7)[p]
                bce = k.bank_cols(b_x, CL)
                at = state["acs_tok"][c // 2]
                d1 = k.dma("sync", rA[1:2, 0:CL],
                           acs[hh:hh + 1, c * CL:(c + 1) * CL],
                           slot=f"rx{hs}", deps=[at] + rw)
                if c == 0:
                    d2 = k.dma("sync", rB[0:1, 1:257],
                               acs[hh:hh + 1, 0:CL], slot=f"ry{hs}",
                               deps=[at] + rw)
                    q3 = k.s(lambda e, rB=rB, rex=rex: e.activation(
                        out=rex, in_=rB[0:1, 1:1 + CL],
                        func=AF.Exp), deps=[d2] + rw)
                    q3a = q3
                else:
                    d2 = k.dma("sync", rB[0:1, 0:257],
                               acs[hh:hh + 1, c * CL - 1:(c + 1) * CL],
                               slot=f"ry{hs}", deps=[at] + rw)
                    q3a = k.v(lambda e, rB=rB, rex=rex:
                              e.tensor_scalar(
                                  out=rex, in0=rB[0:1, 1:1 + CL],
                                  scalar1=rB[0:1, 0:1], scalar2=None,
                                  op0=ALU.subtract), deps=[d2] + rw)
                    q3 = k.s(lambda e, rex=rex: e.activation(
                        out=rex, in_=rex, func=AF.Exp), deps=[q3a])
                mme = k.mm_group([(bce, k.onesr[:, :], rex, [q3])],
                                 banks=[b_x], extra_deps=k.consts["ones"])
                chv = chs_r(h4)
                ch1 = k.v(lambda e, g=g, bce=bce, chv=chv, cs=cs:
                          e.tensor_tensor(
                              out=chv, in0=cfmb_r(cs)[:, g * CL:
                                                      (g + 1) * CL],
                              in1=bce, op=ALU.mult),
                          deps=[mme, lc] + chs_war.pop(h4, []), rb=[b_x])
                ch_last = ch1
                lv = lcbt_r(h4)
                lw = lcbt_war.pop(h4, [])
                lts = []
                for sti in range(2):
                    bidx = (0, 2)[p] + sti
                    dps = k.bank_cols(bidx, CL)
                    mmf = k.mm_group(
                        [(dps, rA[0:2, sti * 128:sti * 128 + 128],
                          rB[0:2, 1:1 + CL],
                          [d1, d2] + rowc[hs])],
                        banks=[bidx])
                    z1 = k.v(lambda e, dps=dps: e.tensor_scalar(
                        out=dps, in0=dps, scalar1=0.0, scalar2=None,
                        op0=ALU.min), deps=[mmf], rb=[bidx], wb=[bidx])
                    z2 = k.s(lambda e, sti=sti, dps=dps, lv=lv:
                             e.activation(
                                 out=lv[:, sti * CL:(sti + 1) * CL],
                                 in_=dps, func=AF.Exp),
                             deps=[z1] + lw, rb=[bidx])
                    z3 = k.v(lambda e, sti=sti, g=g, lv=lv: e.tensor_tensor(
                        out=lv[:, sti * CL:(sti + 1) * CL],
                        in0=lv[:, sti * CL:(sti + 1) * CL],
                        in1=cbt8[:, (g * 2 + sti) * CL:
                                 (g * 2 + sti + 1) * CL],
                        op=ALU.mult), deps=[z2] + cb_toks)
                    lts.append(z3)
                    mmf_last = mmf
                rows_war[hs] = [mmf_last, q3a]
                b_y = 4 + p
                yps = (k.ps[0:64, b_y * 512:b_y * 512 + CL] if p == 0
                       else k.ps[64:128, b_y * 512:b_y * 512 + CL])
                parts = [(yps, xtm[:, xb + tt * 128 + hl2 * 64:
                                   xb + tt * 128 + hl2 * 64 + 64],
                          lv[:, tt * CL:(tt + 1) * CL], [tx] + lts)
                         for tt in range(2)]
                parts.append((yps, hb_bf[:, hh * 64:(hh + 1) * 64],
                              chv, hb_rd + [ch1]))
                mmy = k.mm_group(parts, banks=[b_y])
                mmy_list.append(mmy)
                chs_war[h4] = [mmy]
                lcbt_war[h4] = [mmy]
                ydst = yfm[hl2 * 64:hl2 * 64 + 64,
                           (hh // 2) * CL:(hh // 2 + 1) * CL]
                yv = k.s(lambda e, ydst=ydst, yps=yps: e.copy(
                    out=ydst, in_=yps),
                    deps=[mmy] + state["yfm_war"][hh // 2], rb=[b_y])
                yev_toks.append(yv)
                state["yfm_war"][hh // 2] = []
                xw.append(mmy)
            xtm_war2[xs] = xw + [tx]
        cfm_war[cs] = [cb_mm_last, ch_last]
        sq_ps2 = k.bank_cols(2, CL, rows=1)
        for grpi in range(4):
            parts = []
            for chy in range(grpi * 8, grpi * 8 + 8):
                sl = chy % 8
                st = k.s(lambda e, chy=chy, sl=sl: e.activation(
                    out=sqs[sl], in_=yfm[:, chy * CL:(chy + 1) * CL],
                    func=AF.Square), deps=yev_toks[-1:] + sq_war[sl])
                parts.append((sq_ps2, k.onesc[:, 1:2], sqs[sl], [st]))
            mmt2 = k.mm_group(parts, banks=[2], extra_deps=k.consts["ones"],
                              cont=(grpi > 0), open_end=(grpi < 3))
            for sl in range(8):
                sq_war[sl] = [mmt2]
        rr2 = rrow[0:1, 4 * CL:5 * CL]
        n1 = k.v(lambda e: e.tensor_scalar(
            out=rr2, in0=sq_ps2, scalar1=1.0 / DI, scalar2=1e-6,
            op0=ALU.mult, op1=ALU.add), deps=[mmt2] + state["rrow_war"],
            rb=[2])
        n2 = k.s(lambda e: e.activation(out=rr2, in_=rr2, func=AF.Ln),
                 deps=[n1])
        n3 = k.v(lambda e: e.tensor_scalar(out=rr2, in0=rr2, scalar1=-0.5,
                                           scalar2=None, op0=ALU.mult),
                 deps=[n2])
        n4 = k.s(lambda e: e.activation(out=rr2, in_=rr2, func=AF.Exp),
                 deps=[n3])
        bps2 = k.ps[:, 2 * 512:2 * 512 + CL]
        mmb2 = k.mm_group([(bps2, k.onesr[:, :], rr2, [n4])], banks=[2],
                          extra_deps=k.consts["ones"])
        b2 = k.s(lambda e: e.copy(out=bcv, in_=bps2),
                 deps=[mmb2] + state["bcv_war"], rb=[2])
        state["rrow_war"] = [mmb2]
        yg_toks = []
        bcv_r = []
        g2_last = None
        for chy in range(32):
            yv2 = yfm[:, chy * CL:(chy + 1) * CL]
            ts = chy % 2
            tv = tmp_r[ts]
            g1 = k.v(lambda e, yv2=yv2, chy=chy, tv=tv:
                     e.scalar_tensor_tensor(
                         out=tv, in0=yv2,
                         scalar=cwcols[:, 288 + chy:289 + chy],
                         in1=bcv, op0=ALU.mult, op1=ALU.mult),
                     deps=[b2, c_cw, yev_toks[2 * chy + 1]]
                     + tmp_war_r.pop(ts, []))
            gsl = 1 + 2 * (chy % 2)
            g2 = k.s(lambda e, chy=chy, gsl=gsl, cs=cs: e.activation(
                out=cvout[:, gsl * CL:(gsl + 1) * CL],
                in_=gate_r(cs)[:, chy * CL:(chy + 1) * CL],
                func=AF.Silu), deps=[lg] + cv_war.pop(gsl, []))
            g2_last = g2
            g3 = k.v(lambda e, yv2=yv2, gsl=gsl, tv=tv: e.tensor_tensor(
                out=yv2, in0=tv, in1=cvout[:, gsl * CL:(gsl + 1) * CL],
                op=ALU.mult), deps=[g1, g2])
            tmp_war_r[ts] = [g3]
            cv_war[gsl] = [g3]
            state["yfm_war"][chy] = [g3]
            bcv_r.append(g3)
            yg_toks.append(g3)
        state["bcv_war"] = bcv_r[-1:]
        gld_war[cs] = [g2_last]
        sp = k.dma("sync",
                   ygd.rearrange("(g p) t -> p g t", p=128)[
                       :, :, c * CL:(c + 1) * CL],
                   yfm.rearrange("p (g t) -> p g t", g=32),
                   slot="ygsp", deps=yg_toks)
        ygd_sp.append(sp)
        for chy in range(32):
            state["yfm_war"][chy].append(sp)
        return mmy_list

    hb_bf = wbuf[:, 0:4096]
    hbf0 = k.v(lambda e: e.tensor_copy(out=hb_bf, in_=hbA[:, :]),
               deps=[hse] + list(xn_free))
    rd_f = [hse]
    rd_bf = [hbf0]
    prev_mmy = []
    for c in range(NC4):
        hr = hbA if c % 2 == 0 else hbB
        hw = hbB if c % 2 == 0 else hbA
        mmy_toks = yield_pass2(c, rd_bf)
        if c < NC4 - 1:
            war_w = prev_mmy if c > 0 else list(xn_free)
            rd_f, rd_bf = rec_chunk(c, hr, hw, rd_f, war_w,
                                    bf_war=mmy_toks)
        prev_mmy = mmy_toks

    xrl = k.dma("sync", xres[:, :].rearrange("p (ch t) -> p ch t", ch=NCH),
                xres_d.rearrange("(ch p) t -> p ch t", p=128),
                slot="xrl",
                deps=[Tok("c_tensor", em.counts["c_tensor"]),
                      Tok("c_vector", em.counts["c_vector"]),
                      Tok("c_scalar", em.counts["c_scalar"])])
    upd = []
    yglb = conv_in[:, 0:32 * 512]
    yld_war = [Tok("c_scalar", em.counts["c_scalar"]),
               Tok("c_vector", em.counts["c_vector"])]
    op_war = {0: [Tok("c_tensor", em.counts["c_tensor"])],
              1: [Tok("c_tensor", em.counts["c_tensor"])]}
    for tg in range(2):
        yl = k.dma("sync", yglb.rearrange("p (g t) -> p g t", g=32),
                   ygd.rearrange("(g p) t -> p g t", p=128)[
                       :, :, tg * 512:(tg + 1) * 512],
                   slot="ygl", deps=ygd_sp + yld_war)
        yld_war = []
        for dch in range(NCH):
            parts = []
            lts = []
            for hhalf in range(2):
                sl = (dch * 2 + hhalf) % 2
                wv = wbuf[:, sl * NCH * 128:(sl + 1) * NCH * 128]
                lt = k.dma("gpsimd" if sl == 0 else "sync", wv,
                           W["outproj"][dch][:, hhalf * 16:(hhalf + 1) * 16]
                           .rearrange("p ic j -> p (ic j)"),
                           slot=f"w{sl}", deps=op_war.pop(sl, []))
                lts.append(lt)
                parts += [(k.bank_cols(dch % 2, 512),
                           wv[:, ic * 128:(ic + 1) * 128],
                           yglb[:, (hhalf * 16 + ic) * 512:
                                (hhalf * 16 + ic + 1) * 512],
                           [yl, lt]) for ic in range(16)]
            pb = dch % 2
            ps = k.bank_cols(pb, 512)
            mmo = k.mm_group(parts, banks=[pb])
            for lt in lts:
                k.slot_use(lt, mmo)
            yld_war.append(mmo)
            xs = xres[:, dch * TPC + tg * 512: dch * TPC + tg * 512 + 512]
            ut = k.v(lambda e, xs=xs, ps=ps: e.tensor_tensor(
                out=xs, in0=xs, in1=ps, op=ALU.add),
                deps=[mmo, xrl], rb=[pb])
            upd.append(ut)
    return upd


def host_prep_mamba(inputs):
    ipw = inputs["L0_attn_b_in_proj_w"]
    X_, Adt, B_, C_, G_ = (ipw[0:4096], ipw[4096:4160], ipw[4160:5184],
                           ipw[5184:6208], ipw[6208:10304])
    reord = np.concatenate(
        [X_, B_, C_, G_, Adt, np.zeros((64, D), np.float32)], axis=0)
    inproj = prep_fm(np.ascontiguousarray(reord.T)).astype(BF)
    outproj = prep_fm(np.ascontiguousarray(
        inputs["L0_attn_b_out_proj_w"].T)).astype(BF)
    cw = inputs["L0_attn_b_conv_w"][:, 0, :]
    cwcols = np.zeros((128, 336), np.float32)
    for ch in range(48):
        for dd in range(4):
            cwcols[:, ch * 4 + dd] = cw[ch * 128:(ch + 1) * 128, dd]
        cwcols[:, 192 + ch] = inputs["L0_attn_b_conv_b"][ch * 128:(ch + 1) * 128]
        cwcols[:, 240 + ch] = inputs["L0_attn_b_conv_norm_w"][
            ch * 128:(ch + 1) * 128]
    for ch in range(32):
        cwcols[:, 288 + ch] = inputs["L0_attn_b_out_norm_w"][
            ch * 128:(ch + 1) * 128]
    for ch in range(16):
        cwcols[:, 320 + ch] = inputs["L0_attn_b_norm_w"][
            ch * 128:(ch + 1) * 128]
    dtw = np.ascontiguousarray(inputs["L0_attn_b_dt_w"].T)
    acols = np.zeros((128, 3), np.float32)
    acols[0:64, 0] = inputs["L0_attn_b_dt_b"]
    acols[0:64, 1] = inputs["L0_attn_b_A_log"]
    tri = np.zeros((128, 512), np.float32)
    for sti in range(2):
        for i in range(128):
            tri[i, sti * 256:(sti + 1) * 256] = (
                (sti * 128 + i) <= np.arange(256)).astype(np.float32)
    shared = dict(inproj=inproj, outproj=outproj, cwcols=cwcols, dtw=dtw,
                  acols=acols, tri=tri)
    percore = [dict(flag=np.zeros((128, 1), np.float32)),
               dict(flag=np.ones((128, 1), np.float32))]
    return shared, percore


def build_mamba_nc():
    nc = bass.Bass("TRN2", target_bir_lowering=False, debug=False)
    xr = nc.dram_tensor("xr", [D, TPC], F32, kind="ExternalInput").ap()
    yr = nc.dram_tensor("yr", [D, TPC], F32, kind="ExternalOutput").ap()
    ident = nc.dram_tensor("ident", [128, 128], F32, kind="ExternalInput").ap()
    W = {
        "inproj": nc.dram_tensor("w_inproj", [81, 128, NCH, 128], F32,
                                 kind="ExternalInput").ap(),
        "outproj": nc.dram_tensor("w_outproj", [NCH, 128, 32, 128], F32,
                                  kind="ExternalInput").ap(),
        "cwcols": nc.dram_tensor("w_cwcols", [128, 336], F32,
                                 kind="ExternalInput").ap(),
        "dtw": nc.dram_tensor("w_dtw", [64, 64], F32,
                              kind="ExternalInput").ap(),
        "acols": nc.dram_tensor("w_acols", [128, 3], F32,
                                kind="ExternalInput").ap(),
        "tri": nc.dram_tensor("w_tri", [128, 512], F32,
                              kind="ExternalInput").ap(),
        "flag": nc.dram_tensor("w_flag", [128, 1], F32,
                               kind="ExternalInput").ap(),
    }
    k = K(nc)
    k.setup_consts(ident)
    xres = k.sb_raw("xres", [128, NCH * TPC], F32).ap()
    xn = k.sb_raw("xn", [128, NCH * TPC], BF16).ap()
    scratch = k.sb_raw("lnscr", [128, 8 * 512], F32).ap()
    rows = scratch[0:1, 0:2560]
    xt = emit_load_res_fm(k, xr, xres)
    xnt, scrfree = emit_ln(k, xres, xt, xn, scratch, rows)
    upd = emit_mamba(k, "m", xn, xnt, xres, xt, W, scratch, sf_deps=scrfree)
    st = emit_store_res_fm(k, xres, yr, upd)
    k.em.finalize([st])
    return nc


def run_mamba_launch(inputs, xres_list, mode="hw", sim_cores=2):
    shared, percore = host_prep_mamba(inputs)
    nc = build_mamba_nc()
    ncores = 8 if mode == "hw" else sim_cores
    in_maps = []
    for c in range(ncores):
        m = {"xr": xres_list[c], "ident": _IDENT}
        for kk, v in shared.items():
            m["w_" + kk] = v
        for kk, v in percore[c % 2].items():
            m["w_" + kk] = v
        in_maps.append(m)
    res = run_bass_kernel_spmd(nc, in_maps, core_ids=list(range(8)))
    return [res.results[c]["yr"] for c in range(8)], nc


def build_fused_nc(cfg0, cfg1, cfg2):
    nc = bass.Bass("TRN2", target_bir_lowering=False, debug=False)
    xr = nc.dram_tensor("xr", [D, TPC], F32, kind="ExternalInput").ap()
    yr = nc.dram_tensor("yr", [D, TPC], F32, kind="ExternalOutput").ap()
    ident = nc.dram_tensor("ident", [128, 128], F32, kind="ExternalInput").ap()

    def attn_w(pfx, cfg):
        W = {}
        W["wq"] = nc.dram_tensor(pfx + "wq", [cfg["nqh"], 128, NCH, 128],
                                 BF16, kind="ExternalInput").ap()
        W["wk"] = nc.dram_tensor(pfx + "wk", [cfg["nkvh"], 128, NCH, 128],
                                 BF16, kind="ExternalInput").ap()
        W["wv"] = nc.dram_tensor(pfx + "wv", [cfg["nkvh"], 128, NCH, 128],
                                 BF16, kind="ExternalInput").ap()
        W["wo"] = nc.dram_tensor(pfx + "wo", [NCH, 128, cfg["nqh"], 128],
                                 BF16, kind="ExternalInput").ap()
        if cfg["rope"]:
            W["tabq"] = nc.dram_tensor(pfx + "tabq", [128, 2 * TPC], BF16,
                                       kind="ExternalInput").ap()
            W["tabk"] = nc.dram_tensor(
                pfx + "tabk", [128, 2 * cfg.get("kvw", 2 * TPC)], BF16,
                kind="ExternalInput").ap()
            W["rot"] = nc.dram_tensor(pfx + "rot", [128, 128], BF16,
                                      kind="ExternalInput").ap()
        if cfg["kw"] is not None:
            W["masks"] = nc.dram_tensor(pfx + "masks", [8, 128, cfg["kw"]],
                                        BF16, kind="ExternalInput").ap()
        if cfg.get("bias"):
            W["bcols"] = nc.dram_tensor(pfx + "bcols", [128, 64], F32,
                                        kind="ExternalInput").ap()
        if cfg.get("qknorm"):
            W["nscales"] = nc.dram_tensor(pfx + "nscales", [128, 2], F32,
                                          kind="ExternalInput").ap()
        return W

    def ffn_w(pfx, nI, w3):
        W = {"router": nc.dram_tensor(pfx + "router", [NCH, 128, 8], F32,
                                      kind="ExternalInput").ap(),
             "w2": nc.dram_tensor(pfx + "w2", [3, NCH, 3, 128, nI, 128],
                                  BF16, kind="ExternalInput").ap()}
        if w3:
            W["w1"] = nc.dram_tensor(pfx + "w1", [9, nI, 2, 128, NCH, 128],
                                     BF16, kind="ExternalInput").ap()
            W["w3"] = W["w1"]
        else:
            W["w1"] = nc.dram_tensor(pfx + "w1", [9, nI, 128, NCH, 128],
                                     BF16, kind="ExternalInput").ap()
        return W

    W0a = attn_w("a0_", cfg0)
    Wf0 = ffn_w("f0_", 8, True)
    Wm = {
        "inproj": nc.dram_tensor("m_inproj", [81, 128, NCH, 128], BF16,
                                 kind="ExternalInput").ap(),
        "outproj": nc.dram_tensor("m_outproj", [NCH, 128, 32, 128], BF16,
                                  kind="ExternalInput").ap(),
        "cwcols": nc.dram_tensor("m_cwcols", [128, 336], F32,
                                 kind="ExternalInput").ap(),
        "dtw": nc.dram_tensor("m_dtw", [64, 64], F32,
                              kind="ExternalInput").ap(),
        "acols": nc.dram_tensor("m_acols", [128, 3], F32,
                                kind="ExternalInput").ap(),
        "tri": nc.dram_tensor("m_tri", [128, 512], F32,
                              kind="ExternalInput").ap(),
        "flag": nc.dram_tensor("m_flag", [128, 1], F32,
                               kind="ExternalInput").ap(),
    }
    W1a = attn_w("a1_", cfg1)
    Wf1 = ffn_w("f1_", 11, False)
    W2a = attn_w("a2_", cfg2)

    k = K(nc)
    k.setup_consts(ident)
    xres = k.sb_raw("xres", [128, NCH * TPC], F32).ap()
    xn = k.sb_raw("xn", [128, NCH * TPC], BF16).ap()
    scratch = k.sb_raw("lnscr", [128, 8 * 512], F32).ap()
    rows = scratch[0:1, 0:2560]
    k.mk_arenas(46720, 768)
    xt = emit_load_res_fm(k, xr, xres)
    upd = xt

    def ln():
        return emit_ln(k, xres, upd, xn, scratch, rows)

    xnt, sf = ln()
    upd = emit_attn(k, "a0", cfg0, xn, xnt, xres, upd, W0a,
                    spare_bf=xn, spare_f32=scratch, sf_deps=sf)
    k.phase_barrier()
    xnt, sf = ln()
    upd = emit_ffn(k, "f0a", xn, xnt, xres, upd, Wf0, 1024, False,
                   scratch, sf_deps=sf)
    k.phase_barrier()
    xnt, sf = ln()
    upd = emit_mamba(k, "m", xn, xnt, xres, upd, Wm, scratch, sf_deps=sf)
    k.phase_barrier()
    xnt, sf = ln()
    upd = emit_ffn(k, "f0b", xn, xnt, xres, upd, Wf0, 1024, False,
                   scratch, sf_deps=sf)
    k.phase_barrier()
    xnt, sf = ln()
    upd = emit_attn(k, "a1", cfg1, xn, xnt, xres, upd, W1a,
                    spare_bf=xn, spare_f32=scratch, sf_deps=sf)
    k.phase_barrier()
    xnt, sf = ln()
    upd = emit_ffn(k, "f1a", xn, xnt, xres, upd, Wf1, 1408, True,
                   scratch, sf_deps=sf)
    k.phase_barrier()
    xnt, sf = ln()
    upd = emit_attn(k, "a2", cfg2, xn, xnt, xres, upd, W2a,
                    spare_bf=xn, spare_f32=scratch, sf_deps=sf)
    k.phase_barrier()
    xnt, sf = ln()
    upd = emit_ffn(k, "f1b", xn, xnt, xres, upd, Wf1, 1408, True,
                   scratch, sf_deps=sf)
    k.phase_barrier()
    st = emit_store_res_fm(k, xres, yr, upd)
    k.em.finalize([st])
    return nc


def kernel(**inputs):
    xres = _to_fm_shards(np.ascontiguousarray(inputs["L0_attn_a_x"]))
    cfg0, sh0, pc0 = host_prep_attn("l0a", inputs)
    cfg1, sh1, pc1 = host_prep_attn("l1a", inputs)
    cfg2, sh2, pc2 = host_prep_attn("l1b", inputs)
    Wf0 = host_prep_ffn(0, inputs)
    Wf1 = host_prep_ffn(1, inputs)
    shm, pcm = host_prep_mamba(inputs)
    nc = build_fused_nc(cfg0, cfg1, cfg2)
    in_maps = []
    for c in range(8):
        m = {"xr": xres[c], "ident": _IDENT}
        for kk, v in sh0.items():
            m["a0_" + kk] = v
        for kk, v in pc0[c % 2].items():
            m["a0_" + kk] = v
        for kk, v in sh1.items():
            m["a1_" + kk] = v
        for kk, v in sh2.items():
            m["a2_" + kk] = v
        for kk, v in pc2[c % 2].items():
            m["a2_" + kk] = v
        m["f0_router"] = Wf0["router"]
        m["f0_w1"] = Wf0["w1"]
        m["f0_w2"] = Wf0["w2"]
        m["f1_router"] = Wf1["router"]
        m["f1_w1"] = Wf1["w1"]
        m["f1_w2"] = Wf1["w2"]
        for kk, v in shm.items():
            m["m_" + kk] = v
        for kk, v in pcm[c % 2].items():
            m["m_" + kk] = v
        in_maps.append(m)
    res = run_bass_kernel_spmd(nc, in_maps, core_ids=list(range(8)))
    return _from_fm_shards([res.results[c]["yr"] for c in range(8)])


# version 27
# speedup vs baseline: 1.1763x; 1.0417x over previous
import contextlib
import numpy as np
import ml_dtypes

import concourse.bass as bass
import concourse.mybir as mybir
from concourse.bass_utils import run_bass_kernel_spmd

F32 = mybir.dt.float32
BF16 = mybir.dt.bfloat16
AF = mybir.ActivationFunctionType
ALU = mybir.AluOpType
AX = mybir.AxisListType

B, S, D = 4, 2048, 2048
TPC = 1024
NCH = D // 128
NEG = -1.0e30

BF = ml_dtypes.bfloat16


class Tok:
    __slots__ = ("sem", "val")

    def __init__(self, sem, val):
        self.sem, self.val = sem, val


class Emitter:

    ENGINES = ("tensor", "vector", "scalar", "gpsimd", "sync")

    def __init__(self, nc):
        self.nc = nc
        self.ctx = contextlib.ExitStack()
        self.items = {e: [] for e in self.ENGINES}
        self.sems = {}
        self.counts = {}
        self.hw = {e: {} for e in self.ENGINES}
        for e in ("tensor", "vector", "scalar", "gpsimd"):
            self._mksem("c_" + e)
        self.bank_w = [None] * 8
        self.bank_r = [[] for _ in range(8)]

    def _mksem(self, name):
        h = self.ctx.enter_context(self.nc.semaphore(name))
        self.sems[name] = h
        self.counts[name] = 0
        return name

    def dma_slot(self, name):
        if name not in self.sems:
            self._mksem(name)
        return name

    def emit(self, eng, fn, deps=(), inc="auto", inc_by=1):
        waits = []
        hw = self.hw[eng]
        for t in deps:
            if t is None:
                continue
            if hw.get(t.sem, 0) < t.val:
                waits.append((t.sem, t.val))
                hw[t.sem] = t.val
        semname = "c_" + eng if inc == "auto" else inc
        if semname is not None:
            self.counts[semname] += inc_by
            tok = Tok(semname, self.counts[semname])
            inc_pair = (semname, inc_by)
        else:
            tok, inc_pair = None, None
        self.items[eng].append((waits, fn, inc_pair))
        return tok

    def wait(self, eng, deps):
        hw = self.hw[eng]
        for t in deps:
            if t is None:
                continue
            if hw.get(t.sem, 0) < t.val:
                self.items[eng].append(([(t.sem, t.val)], None, None))
                hw[t.sem] = t.val

    def bank_write_deps(self, banks):
        deps = []
        for b in banks:
            deps.extend(self.bank_r[b])
            if self.bank_w[b] is not None:
                deps.append(self.bank_w[b])
        return deps

    def bank_set_writer(self, banks, tok):
        for b in banks:
            self.bank_w[b] = tok
            self.bank_r[b] = []

    def bank_read(self, banks, tok):
        for b in banks:
            self.bank_r[b].append(tok)

    def finalize(self, final_deps):
        self.wait("sync", list(final_deps))
        nc = self.nc
        sems = self.sems
        with nc.Block() as blk:
            for engname in self.ENGINES:
                items = self.items[engname]
                if not items:
                    continue
                section = getattr(blk, engname)

                def body(e, items=items):
                    for waits, fn, inc_pair in items:
                        for semname, val in waits:
                            e.wait_ge(sems[semname], val)
                        if fn is not None:
                            ins = fn(e)
                            if inc_pair is not None:
                                ins.then_inc(sems[inc_pair[0]], inc_pair[1])

                section(body)
        self.ctx.close()


class K:
    def __init__(self, nc):
        self.nc = nc
        self.em = Emitter(nc)
        self.ctx = self.em.ctx
        self._names = set()
        self.ps = self.sb_raw("psum32", [128, 4096], F32, psum=True)
        self._dma_ctr = 0
        self._slot_last_use = {}
        self.consts = {}

    def sb_raw(self, name, shape, dtype, psum=False):
        assert name not in self._names, name
        self._names.add(name)
        f = self.nc.psum_tensor if psum else self.nc.sbuf_tensor
        return self.ctx.enter_context(f(name, shape, dtype))

    def mk_arenas(self, bf_cols, f32_cols):
        self.abf = self.sb_raw("arena_bf", [128, bf_cols], BF16).ap()
        self.af32 = self.sb_raw("arena_f32", [128, f32_cols], F32).ap()
        self.fused = True
        self._aoff = {"bf": 0, "f32": 0}
        self._amax = {"bf": 0, "f32": 0}

    def phase_reset(self):
        for kk in self._aoff:
            self._amax[kk] = max(self._amax[kk], self._aoff[kk])
            self._aoff[kk] = 0

    def phase_barrier(self):
        em = self.em
        toks = [Tok(name, em.counts[name]) for name in em.sems
                if em.counts[name] > 0]
        for eng in Emitter.ENGINES:
            em.wait(eng, toks)
        em.bank_w = [None] * 8
        em.bank_r = [[] for _ in range(8)]
        self.phase_reset()

    def sb(self, name, shape, dtype):
        if not getattr(self, "fused", False):
            return self.sb_raw(name, shape, dtype).ap()
        akey = "bf" if dtype == BF16 else "f32"
        arena = self.abf if akey == "bf" else self.af32
        off = self._aoff[akey]
        n = shape[1]
        na = (n + 15) & ~15
        self._aoff[akey] = off + na
        assert off + na <= arena.shape[-1], (name, akey, off + na)
        return arena[0:shape[0], off:off + n]

    def dram(self, name, shape, dtype, kind="Internal"):
        assert name not in self._names, name
        self._names.add(name)
        return self.nc.dram_tensor(name, shape, dtype, kind=kind).ap()

    def bank(self, i, n=1, rows=128):
        return self.ps[0:rows, i * 512:(i + n) * 512]

    def bank_cols(self, i, cols, rows=128):
        return self.ps[0:rows, i * 512:i * 512 + cols]

    def dma(self, queue, out, in_, slot=None, deps=(), **kw):
        if slot is None:
            self._dma_ctr += 1
            slot = f"d{queue[0]}{self._dma_ctr}"
        slot = self.em.dma_slot(slot)
        deps = list(deps) + self._slot_last_use.pop(slot, [])
        prev = self.em.counts[slot]
        if prev:
            deps.append(Tok(slot, prev))
        return self.em.emit(queue, lambda e: e.dma_start(out=out, in_=in_, **kw),
                            deps=deps, inc=slot, inc_by=16)

    def slot_use(self, slot_tok, consumer_tok):
        self._slot_last_use.setdefault(slot_tok.sem, []).append(consumer_tok)

    def mm_group(self, parts, banks, extra_deps=(), cont=False,
                 open_end=False):
        n = len(parts)
        tok = None
        bdeps = (self.em.bank_write_deps(banks) if not cont else []) \
            + list(extra_deps)
        for i, (out, lhsT, rhs, deps) in enumerate(parts):
            d = list(deps) + (bdeps if i == 0 else [])
            start = (i == 0 and not cont)
            stop = (i == n - 1) and not open_end
            last = i == n - 1
            tok = self.em.emit(
                "tensor",
                (lambda e, o=out, l=lhsT, r=rhs, st=start, sp=stop:
                 e.matmul(o, l, r, start=st, stop=sp)),
                deps=d,
                inc="auto" if last else None)
        self.em.bank_set_writer(banks, tok)
        return tok

    def transpose(self, out, in_, identity, banks, deps=()):
        deps = list(deps) + self.em.bank_write_deps(banks)
        tok = self.em.emit(
            "tensor", lambda e: e.transpose(out, in_, identity), deps=deps)
        self.em.bank_set_writer(banks, tok)
        return tok

    def v(self, fn, deps=(), rb=(), wb=()):
        return self._comp("vector", fn, deps, rb, wb)

    def s(self, fn, deps=(), rb=(), wb=()):
        return self._comp("scalar", fn, deps, rb, wb)

    def g(self, fn, deps=(), rb=(), wb=()):
        return self._comp("gpsimd", fn, deps, rb, wb)

    def _comp(self, eng, fn, deps, rb, wb):
        deps = list(deps)
        for b in rb:
            if self.em.bank_w[b] is not None:
                deps.append(self.em.bank_w[b])
        if wb:
            deps += self.em.bank_write_deps(wb)
        tok = self.em.emit(eng, fn, deps=deps)
        self.em.bank_read(rb, tok)
        if wb:
            self.em.bank_set_writer(wb, tok)
        return tok

    def setup_consts(self, ident32_ap):
        c = {}
        self.onesc = self.sb_raw("ones_col", [128, 3], F32).ap()
        self.onesr = self.sb_raw("ones_row", [1, 128], F32).ap()
        self.onesrb = self.sb_raw("ones_rowb", [1, 128], BF16).ap()
        self.id32 = self.sb_raw("ident32", [128, 128], F32).ap()
        t1 = self.v(lambda e: e.memset(self.onesc[:, 0:1], 1.0 / D))
        t2 = self.v(lambda e: e.memset(self.onesc[:, 1:2], 1.0))
        t2b = self.v(lambda e: e.memset(self.onesc[:, 2:3], 1.0 / 128.0))
        t3 = self.v(lambda e: e.memset(self.onesr[:, :], 1.0))
        t3b = self.v(lambda e: e.memset(self.onesrb[:, :], 1.0))
        t4 = self.dma("gpsimd", self.id32[:, :], ident32_ap, slot="ident")
        c["ones"] = [t1, t2, t2b, t3, t3b]
        c["ident"] = [t4]
        self.consts = c


def prep_fm(Wc):
    KD, FO = Wc.shape
    assert KD % 128 == 0 and FO % 128 == 0
    return np.ascontiguousarray(
        Wc.reshape(KD // 128, 128, FO // 128, 128).transpose(2, 1, 0, 3))


def prep_rhs(Wc):
    KD, FO = Wc.shape
    assert KD % 128 == 0
    return np.ascontiguousarray(Wc.reshape(KD // 128, 128, FO))


def emit_ln(k: K, xres, xres_toks, xn, scratch, rows):
    toks = []
    sq_tok = [None] * 16
    rows_war = []
    for tg in range(2):
        c0 = tg * 512
        mu_ps = k.bank_cols(0, 512, rows=1)
        sq_ps = k.bank_cols(1, 512, rows=1)
        k.mm_group(
            [(mu_ps, k.onesc[:, 0:1], xres[:, ch * TPC + c0: ch * TPC + c0 + 512],
              list(xres_toks)) for ch in range(NCH)],
            banks=[0], extra_deps=k.consts["ones"])
        for half in range(2):
            parts = []
            for ch in range(half * 8, half * 8 + 8):
                sl = ch % 8
                st = k.s(lambda e, ch=ch, sl=sl, c0=c0: e.activation(
                    out=scratch[:, sl * 512:(sl + 1) * 512],
                    in_=xres[:, ch * TPC + c0: ch * TPC + c0 + 512],
                    func=AF.Square),
                    deps=list(xres_toks) + rows_war
                    + ([sq_tok[sl]] if sq_tok[sl] else []))
                parts.append((sq_ps, k.onesc[:, 0:1],
                              scratch[:, sl * 512:(sl + 1) * 512], [st]))
            mmt = k.mm_group(parts, banks=[1], extra_deps=k.consts["ones"],
                             cont=(half == 1), open_end=(half == 0))
            for sl in range(8):
                sq_tok[sl] = mmt
        r_mu, r_msq, r_t, r_A, r_B = (rows[0:1, i * 512:(i + 1) * 512]
                                      for i in range(5))
        t1 = k.v(lambda e: e.tensor_copy(out=r_mu, in_=mu_ps), rb=[0],
                 deps=rows_war + [sq_tok[0]])
        t2 = k.v(lambda e: e.tensor_copy(out=r_msq, in_=sq_ps), rb=[1])
        t3 = k.v(lambda e: e.tensor_tensor(out=r_t, in0=r_mu, in1=r_mu,
                                           op=ALU.mult), deps=[t1])
        t4 = k.v(lambda e: e.scalar_tensor_tensor(
            out=r_t, in0=r_msq, scalar=1.0e-5, in1=r_t,
            op0=ALU.add, op1=ALU.subtract), deps=[t2, t3])
        t5 = k.s(lambda e: e.activation(out=r_t, in_=r_t, func=AF.Ln),
                 deps=[t4])
        t5b = k.v(lambda e: e.tensor_scalar(out=r_t, in0=r_t, scalar1=-0.5,
                                            scalar2=None, op0=ALU.mult),
                  deps=[t5])
        t6 = k.s(lambda e: e.activation(out=r_A, in_=r_t, func=AF.Exp),
                 deps=[t5b])
        t7 = k.v(lambda e: e.scalar_tensor_tensor(
            out=r_B, in0=r_mu, scalar=-1.0, in1=r_A,
            op0=ALU.mult, op1=ALU.mult), deps=[t3, t6])
        bcA = k.bank_cols(2, 512)
        bcB = k.bank_cols(3, 512)
        mmA = k.mm_group([(bcA, k.onesr[:, :], r_A, [t6])], banks=[2],
                         extra_deps=k.consts["ones"])
        mmB = k.mm_group([(bcB, k.onesr[:, :], r_B, [t7])], banks=[3],
                         extra_deps=k.consts["ones"])
        rows_war = [mmA, mmB, t6, t7]
        for ch in range(NCH):
            sl = ch % 8
            tmp = scratch[:, sl * 512:(sl + 1) * 512]
            ta = k.v(lambda e, ch=ch, tmp=tmp, c0=c0: e.tensor_tensor(
                out=tmp, in0=xres[:, ch * TPC + c0: ch * TPC + c0 + 512],
                in1=bcA, op=ALU.mult),
                deps=list(xres_toks) + [sq_tok[sl]] + rows_war, rb=[2])
            tb = k.v(lambda e, ch=ch, tmp=tmp, c0=c0: e.tensor_tensor(
                out=xn[:, ch * TPC + c0: ch * TPC + c0 + 512], in0=tmp,
                in1=bcB, op=ALU.add), deps=[ta], rb=[3])
            sq_tok[sl] = tb
            toks.append(tb)
    free = [Tok(sn, k.em.counts[sn])
            for sn in ("c_tensor", "c_vector", "c_scalar")]
    return toks, free


def emit_ffn(k: K, tag, xn, xn_toks, xres, xres_toks, W, I, renorm,
             scratch=None, sf_deps=()):
    nI = I // 128
    has_w3 = "w3" in W
    em = k.em

    rw = k.sb(tag + "_rw", [128, NCH * 8], BF16)
    scores = k.sb(tag + "_scores", [128, 8], F32)
    srows = k.sb(tag + "_srows", [128, 16], F32)
    comb = k.sb(tag + "_comb", [128, 16], F32)
    combf = k.sb(tag + "_combf", [16, TPC], BF16)
    combrow = k.sb(tag + "_combrow", [1, 2 * 512], BF16)
    comb_d = k.dram(tag + "_comb_d", [16, TPC], BF16)
    probs = k.sb(tag + "_probs", [128, 8], F32)
    wsl_cols = (2 if has_w3 else 1) * NCH * 128
    w1buf = k.sb(tag + "_w1buf", [128, 2 * wsl_cols], BF16)
    if has_w3:
        w2h = [(0, nI)]
        h1s = k.sb(tag + "_h1s", [128, 4 * 512], BF16)
    else:
        icA = (nI + 1) // 2
        w2h = [(0, icA), (icA, nI - icA)]
        h1s = None
    w2cols = 3 * w2h[0][1] * 128
    w2buf = k.sb(tag + "_w2buf", [128, 2 * w2cols], BF16)
    g_all = k.sb(tag + "_gall", [128, 3 * nI * 2 * 512], BF16)
    cbc = scratch[:, 0:1024]

    def gsl_of(el, fi, tg):
        o = ((el * nI + fi) * 2 + tg) * 512
        return g_all[:, o:o + 512]

    rwt = k.dma("gpsimd", rw[:, :].rearrange("p (kc e) -> p kc e", kc=NCH),
                W["router"].rearrange("kc p e -> p kc e"), slot="rw")

    combf_toks = []
    iter_war = []
    for tt in range(8):
        ps_r = k.bank_cols(4, 8)
        mmt = k.mm_group(
            [(ps_r, xn[:, kc * TPC + tt * 128:kc * TPC + (tt + 1) * 128],
              rw[:, kc * 8:(kc + 1) * 8], list(xn_toks) + [rwt])
             for kc in range(NCH)], banks=[4])
        m1n = srows[:, 0:1]
        sums = srows[:, 1:2]
        rec = srows[:, 2:3]
        m1 = srows[:, 3:4]
        m2 = srows[:, 4:5]
        den = srows[:, 5:6]
        t0 = k.v(lambda e: e.reduce_max(out=m1n, in_=ps_r, axis=AX.X,
                                        negate=True), rb=[4], deps=iter_war)
        t1 = k.s(lambda e: e.activation(out=scores, in_=ps_r, func=AF.Exp,
                                        bias=m1n, accum_out=sums),
                 deps=[t0] + iter_war, rb=[4])
        t2 = k.v(lambda e: e.reciprocal(out=rec, in_=sums), deps=[t1])
        t3 = k.v(lambda e: e.tensor_scalar(out=probs, in0=scores, scalar1=rec,
                                           scalar2=None, op0=ALU.mult),
                 deps=[t1, t2])
        t4 = k.v(lambda e: e.reduce_max(out=m1, in_=probs, axis=AX.X), deps=[t3])
        t6 = k.v(lambda e: e.scalar_tensor_tensor(
            out=comb[:, 8:16], in0=probs, scalar=m1, in1=probs,
            op0=ALU.is_ge, op1=ALU.bypass), deps=[t4])
        t7 = k.v(lambda e: e.scalar_tensor_tensor(
            out=comb[:, 8:16], in0=comb[:, 8:16], scalar=-1.0e9, in1=probs,
            op0=ALU.mult, op1=ALU.add), deps=[t6, t3])
        t8 = k.v(lambda e: e.reduce_max(out=m2, in_=comb[:, 8:16], axis=AX.X),
                 deps=[t7])
        t9 = k.v(lambda e: e.tensor_scalar(out=comb[:, 0:8], in0=probs,
                                           scalar1=m2, scalar2=None,
                                           op0=ALU.is_ge), deps=[t8])
        t10 = k.v(lambda e: e.tensor_tensor(out=comb[:, 0:8], in0=comb[:, 0:8],
                                            in1=probs, op=ALU.mult), deps=[t9])
        last = t10
        if renorm:
            t11 = k.v(lambda e: e.tensor_tensor(out=den, in0=m1, in1=m2,
                                                op=ALU.add), deps=[t4, t8])
            t12 = k.v(lambda e: e.reciprocal(out=den, in_=den), deps=[t11])
            last = k.v(lambda e: e.tensor_scalar(out=comb[:, 0:8],
                                                 in0=comb[:, 0:8], scalar1=den,
                                                 scalar2=None, op0=ALU.mult),
                       deps=[t10, t12])
        tz = k.v(lambda e: e.memset(comb[:, 8:9], 1.0), deps=[last])
        tz2 = k.v(lambda e: e.memset(comb[:, 9:16], 0.0), deps=[tz])
        ctp = k.ps[0:16, 5 * 512:5 * 512 + 128]
        tt_t = k.transpose(ctp, comb[:, 0:16], k.id32[:, :], banks=[5],
                           deps=[last, tz, tz2] + k.consts["ident"])
        cv = k.v(lambda e, tt=tt: e.tensor_copy(
            out=combf[:, tt * 128:(tt + 1) * 128], in_=ctp), deps=[tt_t], rb=[5])
        combf_toks.append(cv)
        iter_war = [tt_t, cv, t1]
    cd = k.dma("sync", comb_d[:, :], combf[:, :], slot="cd",
               deps=combf_toks)

    groups = [(0, 1, 2), (3, 4, 5), (6, 7, 8)]
    upd_toks = []
    g_war = []
    h_war = {}
    cbc_war = list(sf_deps)
    ya_tok = {}
    w1ctr = [0]
    for gi, grp in enumerate(groups):
        gdeps = []
        for el, e_idx in enumerate(grp):
            cbts = []
            for tg in range(2):
                crv = combrow[0:1, tg * 512:(tg + 1) * 512]
                crt = k.dma("sync", crv,
                            comb_d[e_idx:e_idx + 1, tg * 512:(tg + 1) * 512],
                            slot=f"cr{tg}", deps=[cd])
                bc = k.bank_cols(6, 512)
                bmt = k.mm_group([(bc, k.onesrb[:, :], crv, [crt])],
                                 banks=[6], extra_deps=k.consts["ones"])
                k.slot_use(crt, bmt)
                cbt = k.s(lambda e, tg=tg, bc=bc: e.copy(
                    out=cbc[:, tg * 512:(tg + 1) * 512], in_=bc),
                    rb=[6], deps=cbc_war)
                cbc_war = []
                cbts.append(cbt)
            for fi in range(nI):
                sl = w1ctr[0] % 2
                w1q = "gpsimd" if sl == 0 else "sync"
                w1ctr[0] += 1
                wslab = w1buf[:, sl * wsl_cols:(sl + 1) * wsl_cols]
                w1v = wslab[:, 0:NCH * 128]
                if has_w3:
                    lt = k.dma(w1q,
                               wslab.rearrange("p (w kc j) -> p w kc j",
                                               w=2, kc=NCH),
                               W["w1"][e_idx, fi].rearrange(
                                   "w p kc j -> p w kc j"),
                               slot=f"w1{sl}")
                else:
                    lt = k.dma(w1q, w1v,
                               W["w1"][e_idx, fi].rearrange(
                                   "p kc j -> p (kc j)"),
                               slot=f"w1{sl}")
                for tg in range(2):
                    c0 = tg * 512
                    b1 = tg
                    ps_h1 = k.bank_cols(b1, 512)
                    mm1 = k.mm_group(
                        [(ps_h1, w1v[:, kc * 128:(kc + 1) * 128],
                          xn[:, kc * TPC + c0:kc * TPC + c0 + 512],
                          list(xn_toks) + [lt]) for kc in range(NCH)],
                        banks=[b1])
                    k.slot_use(lt, mm1)
                    gsl = gsl_of(el, fi, tg)
                    if has_w3:
                        hsl = h1s[:, (sl * 2 + tg) * 512:
                                  (sl * 2 + tg + 1) * 512]
                        sil = k.s(lambda e, hsl=hsl, ps_h1=ps_h1:
                                  e.activation(out=hsl, in_=ps_h1,
                                               func=AF.Silu),
                                  deps=[mm1] + h_war.pop((sl, tg), []),
                                  rb=[b1])
                        w3v = wslab[:, NCH * 128:2 * NCH * 128]
                        b3 = 2 + tg
                        ps_h3 = k.bank_cols(b3, 512)
                        mm3 = k.mm_group(
                            [(ps_h3, w3v[:, kc * 128:(kc + 1) * 128],
                              xn[:, kc * TPC + c0:kc * TPC + c0 + 512],
                              list(xn_toks) + [lt]) for kc in range(NCH)],
                            banks=[b3])
                        k.slot_use(lt, mm3)
                        hc = k.v(lambda e, gsl=gsl, ps_h3=ps_h3, tg=tg:
                                 e.tensor_tensor(
                                     out=gsl, in0=ps_h3,
                                     in1=cbc[:, tg * 512:(tg + 1) * 512],
                                     op=ALU.mult),
                                 deps=[mm3, cbts[tg]] + g_war, rb=[b3])
                        gt = k.v(lambda e, gsl=gsl, hsl=hsl: e.tensor_tensor(
                            out=gsl, in0=gsl, in1=hsl, op=ALU.mult),
                            deps=[sil, hc])
                        cbc_war.append(hc)
                        h_war[(sl, tg)] = [gt]
                    else:
                        sil = k.s(lambda e, gsl=gsl, ps_h1=ps_h1:
                                  e.activation(out=gsl, in_=ps_h1,
                                               func=AF.Silu),
                                  deps=[mm1] + g_war + h_war.pop((0, tg), []),
                                  rb=[b1])
                        gt = k.v(lambda e, gsl=gsl, tg=tg: e.tensor_tensor(
                            out=gsl, in0=gsl,
                            in1=cbc[:, tg * 512:(tg + 1) * 512],
                            op=ALU.mult), deps=[sil, cbts[tg]])
                        cbc_war.append(gt)
                        h_war[(0, tg)] = [gt]
                    gdeps.append(gt)
            g_war = []
        for dch in range(NCH):
            lts = []
            wvs = []
            for hi, (ic0, icn) in enumerate(w2h):
                sl = (dch * len(w2h) + hi) % 2
                w2q = "sync" if sl == 0 else "gpsimd"
                wv = w2buf[:, sl * w2cols: sl * w2cols + 3 * icn * 128]
                lt = k.dma(w2q,
                           wv.rearrange("p (el ic j) -> p el ic j",
                                        el=3, ic=icn),
                           W["w2"][gi, dch][:, :, ic0:ic0 + icn, :]
                           .rearrange("el p ic j -> p el ic j"),
                           slot=f"w2{sl}")
                lts.append(lt)
                wvs.append((wv, ic0, icn))
            for tg in range(2):
                b_y = 4 + tg
                ps_y = k.bank_cols(b_y, 512)
                parts = []
                for hi, (wv, ic0, icn) in enumerate(wvs):
                    for el in range(3):
                        for ici in range(icn):
                            parts.append(
                                (ps_y,
                                 wv[:, (el * icn + ici) * 128:
                                    (el * icn + ici + 1) * 128],
                                 gsl_of(el, ic0 + ici, tg),
                                 gdeps + [lts[hi]]))
                mmy = k.mm_group(parts, banks=[b_y])
                for lt in lts:
                    k.slot_use(lt, mmy)
                g_war.append(mmy)
                xs = xres[:, dch * TPC + tg * 512:dch * TPC + tg * 512 + 512]
                yd = [ya_tok[(dch, tg)]] if (dch, tg) in ya_tok else []
                ut = k.v(lambda e, xs=xs, ps_y=ps_y: e.tensor_tensor(
                    out=xs, in0=xs, in1=ps_y, op=ALU.add),
                    deps=[mmy] + yd + list(xres_toks), rb=[b_y])
                ya_tok[(dch, tg)] = ut
                if gi == 2:
                    upd_toks.append(ut)
    return upd_toks


def emit_load_res_fm(k: K, xr_ap, xres):
    t = k.dma("sync", xres[:, :].rearrange("p (ch t) -> p ch t", ch=NCH),
              xr_ap.rearrange("(ch p) t -> p ch t", p=128), slot="xload")
    return [t]


def emit_store_res_fm(k: K, xres, yr_ap, deps):
    t = k.dma("sync", yr_ap.rearrange("(ch p) t -> p ch t", p=128),
              xres[:, :].rearrange("p (ch t) -> p ch t", ch=NCH),
              slot="xstore", deps=deps)
    return t


def host_prep_ffn(L, inputs):
    if L == 0:
        gw = inputs["L0_ffn_gate_w"]
        w1 = inputs["L0_ffn_w1"]
        w2 = inputs["L0_ffn_w2"]
        w3 = inputs["L0_ffn_w3"]
        ws1, ws2, ws3 = (inputs["L0_ffn_ws1"], inputs["L0_ffn_ws2"],
                         inputs["L0_ffn_ws3"])
        I = 1024
        W1 = np.stack([prep_fm(w1[e].T) for e in range(8)] + [prep_fm(ws1.T)])
        W3 = np.stack([prep_fm(w3[e].T) for e in range(8)] + [prep_fm(ws3.T)])
        W13 = np.ascontiguousarray(np.stack([W1, W3], axis=2)).astype(BF)
        W2 = np.stack([prep_fm(w2[e].T) for e in range(8)] + [prep_fm(ws2.T)])
        W2g = np.ascontiguousarray(
            W2.reshape(3, 3, NCH, 128, I // 128, 128)
            .transpose(0, 2, 1, 3, 4, 5)).astype(BF)
        router = prep_rhs(gw.T)
        return dict(router=router, w1=W13, w2=W2g, I=I, renorm=False)
    else:
        w1 = inputs["L1_ffn_w1"]
        w2 = inputs["L1_ffn_w2"]
        ws1, ws2 = inputs["L1_ffn_w_shared1"], inputs["L1_ffn_w_shared2"]
        router = prep_rhs(inputs["L1_ffn_w_router"])
        I = 1408
        W1 = np.stack([prep_fm(w1[e]) for e in range(8)]
                      + [prep_fm(ws1)]).astype(BF)
        W2 = np.stack([prep_fm(w2[e]) for e in range(8)] + [prep_fm(ws2)])
        W2g = np.ascontiguousarray(
            W2.reshape(3, 3, NCH, 128, I // 128, 128)
            .transpose(0, 2, 1, 3, 4, 5)).astype(BF)
        return dict(router=router, w1=W1, w2=W2g, I=I, renorm=True)


def build_ffn_nc(L, I, renorm, has_w3):
    nc = bass.Bass("TRN2", target_bir_lowering=False, debug=False)
    nI = I // 128
    xr = nc.dram_tensor("xr", [D, TPC], F32, kind="ExternalInput").ap()
    yr = nc.dram_tensor("yr", [D, TPC], F32, kind="ExternalOutput").ap()
    ident = nc.dram_tensor("ident", [128, 128], F32, kind="ExternalInput").ap()
    router = nc.dram_tensor("w_router", [NCH, 128, 8], F32,
                            kind="ExternalInput").ap()
    w2 = nc.dram_tensor("w_w2", [3, NCH, 3, 128, nI, 128], F32,
                        kind="ExternalInput").ap()
    if has_w3:
        w1 = nc.dram_tensor("w_w1", [9, nI, 2, 128, NCH, 128], F32,
                            kind="ExternalInput").ap()
        W = dict(router=router, w1=w1, w2=w2, w3=w1)
    else:
        w1 = nc.dram_tensor("w_w1", [9, nI, 128, NCH, 128], F32,
                            kind="ExternalInput").ap()
        W = dict(router=router, w1=w1, w2=w2)
    k = K(nc)
    k.setup_consts(ident)
    xres = k.sb_raw("xres", [128, NCH * TPC], F32).ap()
    xn = k.sb_raw("xn", [128, NCH * TPC], BF16).ap()
    scratch = k.sb_raw("lnscr", [128, 8 * 512], F32).ap()
    rows = scratch[0:1, 0:2560]
    xt = emit_load_res_fm(k, xr, xres)
    xnt, _scrfree = emit_ln(k, xres, xt, xn, scratch, rows)
    upd = emit_ffn(k, "f", xn, xnt, xres, xt, W, I, renorm, scratch,
                   sf_deps=_scrfree)
    st = emit_store_res_fm(k, xres, yr, upd)
    k.em.finalize([st])
    return nc


_IDENT = np.eye(128, dtype=np.float32)


def run_ffn_launch(L, inputs, xres_list):
    Wp = host_prep_ffn(L, inputs)
    nc = build_ffn_nc(L, Wp["I"], Wp["renorm"], L == 0)
    in_maps = []
    for c in range(8):
        m = {"xr": xres_list[c], "ident": _IDENT, "w_router": Wp["router"],
             "w_w1": Wp["w1"], "w_w2": Wp["w2"]}
        in_maps.append(m)
    res = run_bass_kernel_spmd(nc, in_maps, core_ids=list(range(8)))
    return [res.results[c]["yr"] for c in range(8)]


def _to_fm_shards(x):
    out = []
    for c in range(8):
        b, h = c // 2, c % 2
        out.append(np.ascontiguousarray(x[b, h * TPC:(h + 1) * TPC, :].T))
    return out


def _from_fm_shards(shards):
    x = np.empty((B, S, D), np.float32)
    for c in range(8):
        b, h = c // 2, c % 2
        x[b, h * TPC:(h + 1) * TPC, :] = shards[c].T
    return x


def kernel_unfused(**inputs):
    xres = _to_fm_shards(np.ascontiguousarray(inputs["L0_attn_a_x"]))
    xres, _ = run_attn_launch("l0a", inputs, xres)
    xres = run_ffn_launch(0, inputs, xres)
    xres, _ = run_mamba_launch(inputs, xres)
    xres = run_ffn_launch(0, inputs, xres)
    xres, _ = run_attn_launch("l1a", inputs, xres)
    xres = run_ffn_launch(1, inputs, xres)
    xres, _ = run_attn_launch("l1b", inputs, xres)
    xres = run_ffn_launch(1, inputs, xres)
    return _from_fm_shards(xres)


RG_PAIRS = [[0, 1], [2, 3], [4, 5], [6, 7]]


def emit_allgather(k: K, tag, src, dst, deps):
    sem = k.em.dma_slot("cc")
    return k.em.emit(
        "gpsimd",
        lambda e: e.collective_compute(
            "AllGather", ALU.bypass, replica_groups=RG_PAIRS,
            ins=[src.opt()], outs=[dst.opt()]),
        deps=deps, inc=sem, inc_by=1)


def emit_attn(k: K, tag, cfg, xn, xn_toks, xres, xres_toks, W,
              spare_bf=None, spare_f32=None, sf_deps=()):
    nqh, nkvh = cfg["nqh"], cfg["nkvh"]
    gq, gkv = cfg["gq"], cfg["gkv"]
    ngrp = nqh // gq
    KV = cfg.get("kvw", 2 * TPC)
    full = cfg["kw"] is None
    KW = 2048 if full else cfg["kw"]
    NKC = KW // 128
    rope = cfg["rope"]
    qknorm = cfg.get("qknorm", False)
    bias = cfg.get("bias", False)
    L0SCALE = 0.08838834764831845

    xnw = k.sb(tag + "_xnw", [128, NCH * KV], BF16)
    kf = k.sb(tag + "_kf", [128, gkv * KV], BF16)
    wbuf = k.sb(tag + "_wbuf", [128, 2 * NCH * 128], BF16)
    wobuf = k.sb(tag + "_wobuf", [128, 2 * 2 * gq * 128], BF16)
    srows = k.sb(tag + "_srows", [128, 16], F32)
    ob = [0]

    def carve_bf(n):
        v = spare_bf[:, ob[0]:ob[0] + n]
        ob[0] += n
        return v

    of = [0]

    def carve_f(n):
        v = spare_f32[:, of[0]:of[0] + n]
        of[0] += n
        return v

    if full:
        pr = carve_bf(2 * KW)
        ptg = carve_bf(16 * 256)
        scrr = spare_f32[:, 0:2 * 2048]
    elif KW <= 1024:
        pr = carve_bf(2 * 2 * KW)
        ptg = carve_bf(2 * 2 * KW)
        scrr = None
    else:
        pr = carve_bf(2 * KW)
        ptg = carve_bf(2 * KW)
        scrr = None
    vfm = carve_bf(KV)
    wo_every = cfg.get("wo_every", 2)
    outp = carve_bf(wo_every * gq * TPC)
    qf = carve_bf(gq * TPC)
    spare_total = NCH * TPC
    mcols = (2 * KW) if (not full) else 0
    if spare_total - ob[0] - mcols >= gkv * KV:
        vt = carve_bf(gkv * KV)
    else:
        vt = k.sb(tag + "_vt", [128, gkv * KV], BF16)
    if rope:
        tabq = k.sb(tag + "_tabq", [128, 2048], BF16)
        tabk = tabq
        ptm = k.sb(tag + "_ptm", [128, 128], BF16)
        rsc = carve_f(2048)
    if not full:
        msk = carve_bf(2 * KW)
    if qknorm:
        nsc = carve_f(1024)
        ncols = k.sb(tag + "_ncols", [128, 6], F32)
        nrows = nsc[0:1, 0:1024]
        bcsb = carve_f(1024)
    if bias:
        bcols = k.sb(tag + "_bcols", [128, 4 * NCH], F32)
    assert ob[0] <= spare_bf.shape[-1], ob[0]
    assert spare_f32 is None or of[0] <= spare_f32.shape[-1], of[0]

    ccs = k.dram(tag + "_ccs", [D, TPC], BF16)
    ccd = k.dram(tag + "_ccd", [2 * D, TPC], BF16)
    t_src = k.dma("sync", ccs.rearrange("(ch p) t -> p ch t", p=128),
                  xn.rearrange("p (ch t) -> p ch t", ch=NCH),
                  slot="src", deps=list(xn_toks))
    t_cc = emit_allgather(k, tag, ccs, ccd, [t_src])
    xnw_toks = []
    xnw_own = []
    PREV = KV - TPC
    for ch in range(NCH):
        tp = k.dma("sync", xnw[:, ch * KV: ch * KV + PREV],
                   ccd[ch * 128:(ch + 1) * 128, TPC - PREV:TPC],
                   slot=f"xw{ch % 4}", deps=[t_cc])
        to = k.v(lambda e, ch=ch: e.tensor_copy(
            out=xnw[:, ch * KV + PREV: (ch + 1) * KV],
            in_=xn[:, ch * TPC:(ch + 1) * TPC]), deps=list(xn_toks))
        xnw_toks += [tp, to]
        xnw_own.append(to)

    consts = []
    ptt = None
    sfw = list(sf_deps)
    if rope:
        ptt = k.dma("sync", ptm[:, :], W["rot"], slot="pt")
        consts.append(ptt)
    if bias:
        consts.append(k.dma("gpsimd", bcols[:, :], W["bcols"], slot="bb"))
    nconst_toks = []
    if qknorm:
        tq = k.dma("gpsimd", ncols[:, 0:2], W["nscales"], slot="ns")
        t1 = k.v(lambda e: e.tensor_scalar(
            out=ncols[:, 0:1], in0=ncols[:, 0:1], scalar1=L0SCALE,
            scalar2=None, op0=ALU.mult), deps=[tq])
        nconst_toks = [t1]

    rope_war = {}
    qn_war = []

    def rope_head(buf, c0, length, wtab, ring, ring_name, deps):
        toks = list(deps)
        out_toks = []
        blocks = [(i * 1024, 1024) for i in range(length // 1024)]
        if length % 1024:
            blocks.append((length - length % 1024, length % 1024))
        for off, wd in blocks:
            war = rope_war.pop(ring_name, [])
            rw2 = rope_war.pop("rsc", [])
            rslot = ring[:, 0:2 * wd]
            tl = k.dma("sync", rslot, wtab[:, 2 * off: 2 * off + 2 * wd],
                       slot="tb" + ring_name, deps=sfw)
            mms = []
            for sub in range(wd // 512):
                mm_ = k.mm_group(
                    [(k.bank_cols(2 + sub, 512), ptm[:, :],
                      buf[:, c0 + off + sub * 512:
                          c0 + off + sub * 512 + 512],
                      toks + [ptt])], banks=[2 + sub])
                mms.append(mm_)
            r1 = rsc[:, 0:wd]
            r2 = rsc[:, 1024:1024 + wd]
            v1 = k.v(lambda e, rslot=rslot, wd=wd, r1=r1: e.tensor_tensor(
                out=r1, in0=k.ps[:, 2 * 512: 2 * 512 + wd],
                in1=rslot[:, wd:2 * wd], op=ALU.mult),
                deps=mms + [tl] + war + rw2,
                rb=list(range(2, 2 + wd // 512)))
            g1 = k.g(lambda e, off=off, wd=wd, rslot=rslot, r2=r2:
                     e.tensor_tensor(
                         out=r2, in0=buf[:, c0 + off: c0 + off + wd],
                         in1=rslot[:, 0:wd], op=ALU.mult),
                     deps=toks + [tl] + war + rw2)
            v2 = k.v(lambda e, off=off, wd=wd, r1=r1, r2=r2: e.tensor_tensor(
                out=buf[:, c0 + off: c0 + off + wd],
                in0=r1, in1=r2, op=ALU.add), deps=[v1, g1])
            rope_war[ring_name] = [v1, g1]
            rope_war["rsc"] = [v2]
            k.slot_use(tl, v1)
            k.slot_use(tl, g1)
            out_toks.append(v2)
        return out_toks

    def qk_norm_head(buf, c0, length, is_q, deps):
        nonlocal qn_war
        scol = ncols[:, 0:1] if is_q else ncols[:, 1:2]
        out_toks = []
        for b in range(length // 1024):
            bb = c0 + b * 1024
            war = qn_war
            st = k.s(lambda e, bb=bb: e.activation(
                out=nsc[:, 0:1024], in_=buf[:, bb:bb + 1024],
                func=AF.Square), deps=list(deps) + nconst_toks + war)
            rr = nrows[0:1, 0:1024]
            mms = []
            for hh2 in range(2):
                rp = k.ps[0:1, (2 + hh2) * 512:(2 + hh2) * 512 + 512]
                mm = k.mm_group(
                    [(rp, k.onesc[:, 2:3], nsc[:, hh2 * 512:(hh2 + 1) * 512],
                      [st])], banks=[2 + hh2], extra_deps=k.consts["ones"])
                mms.append((mm, rp, hh2))
            c1a = k.v(lambda e: e.tensor_scalar(
                out=rr[0:1, 0:512], in0=mms[0][1], scalar1=1e-6, scalar2=None,
                op0=ALU.add), deps=[mms[0][0]] + war, rb=[2])
            c1b = k.v(lambda e: e.tensor_scalar(
                out=rr[0:1, 512:1024], in0=mms[1][1], scalar1=1e-6,
                scalar2=None, op0=ALU.add), deps=[mms[1][0]] + war, rb=[3])
            c2 = k.s(lambda e: e.activation(out=rr, in_=rr, func=AF.Ln),
                     deps=[c1a, c1b])
            c3 = k.v(lambda e: e.tensor_scalar(
                out=rr, in0=rr, scalar1=-0.5, scalar2=None, op0=ALU.mult),
                deps=[c2])
            c4 = k.s(lambda e: e.activation(out=rr, in_=rr, func=AF.Exp),
                     deps=[c3])
            bt = []
            for hh2 in range(2):
                bp = k.ps[:, (2 + hh2) * 512:(2 + hh2) * 512 + 512]
                mmb = k.mm_group(
                    [(bp, k.onesr[:, :], rr[0:1, hh2 * 512:(hh2 + 1) * 512],
                      [c4])], banks=[2 + hh2], extra_deps=k.consts["ones"])
                cp = k.s(lambda e, hh2=hh2, bp=bp: e.copy(
                    out=bcsb[:, hh2 * 512:(hh2 + 1) * 512], in_=bp),
                    deps=[mmb] + war, rb=[2 + hh2])
                bt.append(cp)
            c6 = k.v(lambda e, bb=bb: e.scalar_tensor_tensor(
                out=buf[:, bb:bb + 1024], in0=buf[:, bb:bb + 1024],
                scalar=scol, in1=bcsb[:, 0:1024], op0=ALU.mult,
                op1=ALU.mult), deps=bt + list(deps))
            qn_war = [c6, st]
            out_toks.append(c6)
        return out_toks

    slab_ctr = [0]

    def proj_head(wkey, fo, rhs_c0, rhs_len, dst, dst_c0, evac, war,
                  rdeps=None):
        rdeps = xnw_toks if rdeps is None else rdeps
        sl = slab_ctr[0] % 2
        slab_ctr[0] += 1
        wv = wbuf[:, sl * NCH * 128:(sl + 1) * NCH * 128]
        lt = k.dma("gpsimd", wv,
                   W[wkey][fo].rearrange("p kc j -> p (kc j)"),
                   slot=f"w{sl}")
        toks = []
        for tg in range(rhs_len // 512):
            pb = tg % 2
            ps = k.bank_cols(pb, 512)
            mm = k.mm_group(
                [(ps, wv[:, kc * 128:(kc + 1) * 128],
                  xnw[:, kc * KV + rhs_c0 + tg * 512:
                      kc * KV + rhs_c0 + tg * 512 + 512],
                  rdeps + [lt]) for kc in range(NCH)], banks=[pb])
            toks.append(evac(ps, dst, dst_c0 + tg * 512, 512, mm, pb,
                             tg % 2, war))
        k.slot_use(lt, toks[-1])
        return toks

    def evac_plain(ps, dst, c0, width, mm_tok, pbank, sl, war):
        return k.s(lambda e: e.copy(out=dst[:, c0:c0 + width], in_=ps),
                   deps=[mm_tok] + war, rb=[pbank])

    def mk_evac_bias(colidx, scale):
        def ev(ps, dst, c0, width, mm_tok, pbank, sl, war):
            col = bcols[:, colidx:colidx + 1]
            if scale is None:
                return k.v(lambda e: e.tensor_scalar(
                    out=dst[:, c0:c0 + width], in0=ps, scalar1=col,
                    scalar2=None, op0=ALU.add),
                    deps=[mm_tok] + consts + war, rb=[pbank])
            return k.v(lambda e: e.tensor_scalar(
                out=dst[:, c0:c0 + width], in0=ps, scalar1=col, scalar2=scale,
                op0=ALU.add, op1=ALU.mult),
                deps=[mm_tok] + consts + war, rb=[pbank])
        return ev

    def ev_qn_q(ps, dst, c0, width, mm_tok, pbank, sl, war):
        return qknorm_evac(ps, dst[:, c0:c0 + width], width, mm_tok, pbank,
                           True, sl, war)

    def ev_qn_k(ps, dst, c0, width, mm_tok, pbank, sl, war):
        return qknorm_evac(ps, dst[:, c0:c0 + width], width, mm_tok, pbank,
                           False, sl, war)

    def tensor_now():
        return Tok("c_tensor", k.em.counts["c_tensor"])

    donor = list(xn_toks) + xnw_toks + list(sf_deps)
    upd_toks = []
    wo_pair_deps = []
    outp_war = list(donor)
    msk_tok = {}
    msk_tt = {}
    msk_war = {0: list(donor), 1: list(donor)}
    pr_war = {0: list(donor), 1: list(donor)}
    scr_war = {0: list(donor) + list(sf_deps), 1: list(donor) + list(sf_deps)}
    ptT_war = {0: list(donor), 1: list(donor)}
    ptg_war = list(donor)
    nm_war = {}
    qf_war = [list(donor) for _ in range(gq)]
    kv_war = [[] for _ in range(gkv)]
    vt_war = [list(donor) for _ in range(gkv)]
    vfm_war = list(donor)
    bo_done = [False]
    pend = []

    def flush_pend():
        if not pend:
            return
        p = pend.pop(0)
        hls = [p["one_hl"]] if "one_hl" in p else list(range(gq))
        for i, hl in enumerate(hls):
            hkl = hl // (gq // gkv)
            ps_av = k.ps[:, p["offs"][i]:p["offs"][i] + 128]
            avb = [p["offs"][i] // 512]
            parts = [(ps_av,
                      vt[:, hkl * KV + (p["kstart"] // 128 + j) * 128:
                         hkl * KV + (p["kstart"] // 128 + j + 1) * 128],
                      p["dstT"][:, (i * NKC + j) * 128:
                                (i * NKC + j + 1) * 128],
                      p["tT"] + list(vt_war[hkl]))
                     for j in range(NKC)]
            mav = k.mm_group(parts, banks=avb)
            oc = ((p["grp"] % wo_every) * gq + hl) * TPC + p["tt"] * 128
            ot = k.s(lambda e, oc=oc, ps_av=ps_av: e.copy(
                out=outp[:, oc:oc + 128], in_=ps_av),
                deps=[mav] + outp_war, rb=avb)
            wo_pair_deps.append(ot)
        ptT_war[p["ptkey"]] = \
            [Tok("c_tensor", k.em.counts["c_tensor"])]

    pend_f = [None]
    nzh0 = [None]

    def flush_pend_f():
        nonlocal ptg_war
        p = pend_f[0]
        if p is None:
            return
        pend_f[0] = None
        ps_av = k.ps[:, 0:256]
        parts = [(ps_av,
                  vt[:, p["hkl"] * KV + j * 128:
                     p["hkl"] * KV + (j + 1) * 128],
                  ptg[:, j * 256:(j + 1) * 256],
                  p["tT"] + list(vt_war[p["hkl"]]))
                 for j in range(16)]
        mav = k.mm_group(parts, banks=[0])
        ptg_war = [mav]
        oc = ((p["grp"] % wo_every) * gq + p["hl"]) * TPC \
            + (p["tt"] - 1) * 128
        ot = k.s(lambda e, oc=oc, ps_av=ps_av: e.copy(
            out=outp[:, oc:oc + 256], in_=ps_av),
            deps=[mav] + outp_war, rb=[0])
        wo_pair_deps.append(ot)

    for grp in range(ngrp):
        pair = grp // 2
        pl = grp % 2
        for hl in range(gq):
            h = grp * gq + hl
            dst_c0 = hl * TPC
            if bias:
                toks = proj_head("wq", h, KV - TPC, TPC, qf, dst_c0,
                                 mk_evac_bias(0 * NCH + h, L0SCALE),
                                 qf_war[hl], rdeps=xnw_own)
            else:
                toks = proj_head("wq", h, KV - TPC, TPC, qf, dst_c0,
                                 evac_plain, qf_war[hl], rdeps=xnw_own)
            if qknorm:
                toks = qk_norm_head(qf, dst_c0, TPC, True, toks)
            if rope:
                toks = rope_head(qf, dst_c0, TPC, W["tabq"], tabq, "t", toks)
            qf_war[hl] = toks
        for kl in range(gkv):
            hk = grp * gkv + kl
            if bias:
                ktoks = proj_head("wk", hk, 0, KV, kf, kl * KV,
                                  mk_evac_bias(1 * NCH + hk, None), kv_war[kl])
            else:
                ktoks = proj_head("wk", hk, 0, KV, kf, kl * KV, evac_plain,
                                  kv_war[kl])
            if qknorm:
                ktoks = qk_norm_head(kf, kl * KV, KV, False, ktoks)
            if rope:
                ktoks = rope_head(kf, kl * KV, KV, W["tabk"], tabk, "t",
                                  ktoks)
            kv_war[kl] = ktoks
            if bias:
                vtoks = proj_head("wv", hk, 0, KV, vfm, 0,
                                  mk_evac_bias(2 * NCH + hk, None),
                                  vfm_war + vt_war[kl])
            else:
                vtoks = proj_head("wv", hk, 0, KV, vfm, 0, evac_plain,
                                  vfm_war + vt_war[kl])
            tv = k.dma("sync",
                       vt[:, kl * KV:(kl + 1) * KV].rearrange(
                           "p (j q) -> p j q", j=KV // 128),
                       vfm[:, 0:KV].rearrange("p (j q) -> p j q",
                                              j=KV // 128),
                       transpose=True, slot="vtp", deps=vtoks)
            vfm_war = [tv]
            vt_war[kl] = [tv]
        if full:
            tile_iter = [(hl, tt) for hl in range(gq) for tt in range(8)]
        else:
            tile_iter = []
        tctr = 0
        if (not full) and KW > 1024:
            def ensure_mask2(tti):
                sl_ = tti % 2
                if msk_tt.get(sl_) != tti:
                    mt_ = k.dma("sync", msk[:, sl_ * KW:(sl_ + 1) * KW],
                                W["masks"][tti], slot=f"mk{sl_}",
                                deps=msk_war.pop(sl_, []))
                    msk_tok[sl_] = mt_
                    msk_tt[sl_] = tti
            for tt in range(8):
                mslot = tt % 2
                mv = msk[:, mslot * KW:(mslot + 1) * KW]
                ensure_mask2(tt)
                if tt + 1 < 8:
                    ensure_mask2(tt + 1)
                for hl in range(gq):
                    par = tctr % 2
                    tctr += 1
                    hkl = hl // (gq // gkv)
                    s_off = par * 1536
                    s_banks = (0, 1, 2) if par == 0 else (3, 4, 5)
                    kstart = tt * 128
                    qtile = qf[:, hl * TPC + tt * 128:
                               hl * TPC + (tt + 1) * 128]
                    qdeps = list(qf_war[hl]) + list(kv_war[hkl])
                    sc_toks = []
                    c0 = 0
                    while c0 < KW:
                        w = min(512, KW - c0)
                        ps_j = k.ps[:, s_off + c0: s_off + c0 + w]
                        mmt = k.mm_group(
                            [(ps_j, qtile,
                              kf[:, hkl * KV + kstart + c0:
                                 hkl * KV + kstart + c0 + w], qdeps)],
                            banks=list(s_banks) if c0 == 0 else [])
                        sc_toks.append(mmt)
                        c0 += w
                    base = k.ps[:, s_off:s_off + KW]
                    am = k.v(lambda e, base=base, mv=mv: e.tensor_tensor(
                        out=base, in0=base, in1=mv, op=ALU.add),
                        deps=sc_toks + [msk_tok[mslot]],
                        rb=list(s_banks), wb=list(s_banks))
                    msk_war.setdefault(mslot, []).append(am)
                    nmp = srows[:, par:par + 1]
                    smp = srows[:, 2 + par:3 + par]
                    rcp = srows[:, 4 + par:5 + par]
                    mx = k.v(lambda e, base=base, nmp=nmp: e.reduce_max(
                        out=nmp, in_=base, axis=AX.X, negate=True),
                        deps=[am] + nm_war.pop(par, []), rb=list(s_banks))
                    prv = pr[:, par * KW:(par + 1) * KW]
                    ex = k.s(lambda e, base=base, prv=prv, nmp=nmp, smp=smp:
                             e.activation(out=prv, in_=base, func=AF.Exp,
                                          bias=nmp, accum_out=smp),
                             deps=[mx] + pr_war.pop(par, []),
                             rb=list(s_banks))
                    nm_war[par] = [ex]
                    rc = k.v(lambda e, smp=smp, rcp=rcp: e.reciprocal(
                        out=rcp, in_=smp), deps=[ex])
                    nz = k.v(lambda e, prv=prv, rcp=rcp: e.tensor_scalar(
                        out=prv, in0=prv, scalar1=rcp, scalar2=None,
                        op0=ALU.mult), deps=[rc])
                    if len(pend) >= 2:
                        flush_pend()
                    dstT = ptg[:, par * KW:(par + 1) * KW]
                    ptw = ptT_war.pop(par, [])
                    tT = k.dma("sync",
                               dstT.rearrange("p (j q) -> p j q", j=NKC),
                               prv.rearrange("p (j q) -> p j q", j=NKC),
                               transpose=True, slot="pT", deps=[nz] + ptw)
                    pr_war[par] = [tT]
                    pend.append(dict(
                        tT=[tT], dstT=dstT, kstart=kstart, grp=grp, tt=tt,
                        offs=[3072 + par * 512], ptkey=par, one_hl=hl))
            while pend:
                flush_pend()
        if (not full) and KW <= 1024:
            for tt in range(8):
                par = tt % 2
                tctr += 1
                if KW <= 1024:
                    s_off = par * 2048
                    s_banks = (0, 1, 2) if par == 0 else (4, 5, 6)
                    kstart = tt * 128
                    use_ev = False
                else:
                    s_off = 0
                    s_banks = (0, 1, 2, 3, 4)
                    kstart = tt * 128
                    use_ev = True

                def ensure_mask(tti):
                    sl_ = tti % 2
                    if msk_tt.get(sl_) != tti:
                        mt_ = k.dma("sync", msk[:, sl_ * KW:(sl_ + 1) * KW],
                                    W["masks"][tti], slot=f"mk{sl_}",
                                    deps=msk_war.pop(sl_, []))
                        msk_tok[sl_] = mt_
                        msk_tt[sl_] = tti
                mslot = tt % 2
                mv = msk[:, mslot * KW:(mslot + 1) * KW]
                ensure_mask(tt)
                if tt + 1 < 8:
                    ensure_mask(tt + 1)
                sc_toks = []
                for hl in range(gq):
                    hkl = hl // (gq // gkv)
                    qtile = qf[:, hl * TPC + tt * 128:
                               hl * TPC + (tt + 1) * 128]
                    qdeps = list(qf_war[hl]) + list(kv_war[hkl])
                    p0 = s_off + hl * KW
                    c0 = 0
                    while c0 < KW:
                        w = min(512 - (p0 + c0) % 512, KW - c0)
                        ps_j = k.ps[:, p0 + c0: p0 + c0 + w]
                        mmt = k.mm_group(
                            [(ps_j, qtile,
                              kf[:, hkl * KV + kstart + c0:
                                 hkl * KV + kstart + c0 + w], qdeps)],
                            banks=list(s_banks) if (c0 == 0 and hl == 0)
                            else [])
                        sc_toks.append(mmt)
                        c0 += w
                if use_ev:
                    base = scrr[:, par * 2 * KW:(par + 1) * 2 * KW]
                    ev = k.s(lambda e, base=base: e.copy(
                        out=base, in_=k.ps[:, s_off:s_off + 2 * KW]),
                        deps=sc_toks + scr_war.pop(par, []),
                        rb=list(s_banks))
                    bdeps = [ev]
                else:
                    base = k.ps[:, s_off:s_off + 2 * KW]
                    bdeps = sc_toks
                am0 = k.v(lambda e, base=base, mv=mv: e.tensor_tensor(
                    out=base[:, 0:KW], in0=base[:, 0:KW], in1=mv,
                    op=ALU.add),
                    deps=bdeps + [msk_tok[mslot]],
                    **({} if use_ev else dict(rb=list(s_banks),
                                              wb=list(s_banks))))
                am1 = k.v(lambda e, base=base, mv=mv: e.tensor_tensor(
                    out=base[:, KW:2 * KW], in0=base[:, KW:2 * KW], in1=mv,
                    op=ALU.add), deps=bdeps + [msk_tok[mslot], am0])
                msk_war.setdefault(mslot, []).extend([am0, am1])
                nmp = srows[:, par * 2:(par + 1) * 2]
                smp = srows[:, 4 + par * 2: 6 + par * 2]
                rcp = srows[:, 8 + par * 2: 10 + par * 2]
                mx = k.v(lambda e, base=base, nmp=nmp: e.reduce_max(
                    out=nmp, in_=base.rearrange("p (h w) -> p h w", h=2),
                    axis=AX.X, negate=True),
                    deps=[am0, am1] + nm_war.pop(par, []),
                    **({} if use_ev else dict(rb=list(s_banks))))
                prv = pr[:, par * 2 * KW:(par + 1) * 2 * KW] if KW <= 1024 \
                    else pr[:, 0:2 * KW]
                prw = pr_war.pop(par if KW <= 1024 else 0, [])
                exs = []
                for hl in range(gq):
                    ex = k.s(lambda e, base=base, prv=prv, hl=hl, nmp=nmp,
                             smp=smp: e.activation(
                                 out=prv[:, hl * KW:(hl + 1) * KW],
                                 in_=base[:, hl * KW:(hl + 1) * KW],
                                 func=AF.Exp, bias=nmp[:, hl:hl + 1],
                                 accum_out=smp[:, hl:hl + 1]),
                             deps=[mx] + prw,
                             **({} if use_ev else dict(rb=list(s_banks))))
                    exs.append(ex)
                nm_war[par] = list(exs)
                if use_ev:
                    scr_war[par] = list(exs)
                rc = k.v(lambda e, smp=smp, rcp=rcp: e.reciprocal(
                    out=rcp, in_=smp), deps=exs)
                nzs = []
                for hl in range(gq):
                    nz = k.v(lambda e, prv=prv, hl=hl, rcp=rcp:
                             e.tensor_scalar(
                                 out=prv[:, hl * KW:(hl + 1) * KW],
                                 in0=prv[:, hl * KW:(hl + 1) * KW],
                                 scalar1=rcp[:, hl:hl + 1], scalar2=None,
                                 op0=ALU.mult), deps=[rc])
                    nzs.append(nz)
                def emit_tT(key):
                    ptw = ptT_war.pop(key, [])
                    t_ = k.dma("sync",
                               dstT.rearrange("p (j q) -> p j q", j=2 * NKC),
                               prv.rearrange("p (j q) -> p j q", j=2 * NKC),
                               transpose=True, slot="pT", deps=nzs + ptw)
                    pr_war[key] = [t_]
                    return t_
                if KW <= 1024:
                    if len(pend) >= 2:
                        flush_pend()
                    dstT = ptg[:, par * 2 * KW:(par + 1) * 2 * KW]
                    tT = emit_tT(par)
                    av_offs = [s_off + 1536 + hl * 128 for hl in range(gq)]
                else:
                    flush_pend()
                    dstT = ptg[:, 0:2 * KW]
                    tT = emit_tT(0)
                    av_offs = [2560 + par * 768 + hl * 128
                               for hl in range(gq)]
                pend.append(dict(tT=[tT], dstT=dstT, kstart=kstart, grp=grp,
                                 tt=tt, offs=av_offs,
                                 ptkey=(par if KW <= 1024 else 0)))
            while pend:
                flush_pend()
        for hl, tt in tile_iter:
            hkl = hl // (gq // gkv)
            par = tctr % 2
            tctr += 1
            s_off = par * 2048
            s_banks = (0, 1, 2, 3) if par == 0 else (4, 5, 6, 7)
            qtile = qf[:, hl * TPC + tt * 128: hl * TPC + (tt + 1) * 128]
            qdeps = list(qf_war[hl]) + list(kv_war[hkl])
            sc_toks = []
            for j in range(4):
                c0 = j * 512
                ps_j = k.ps[:, s_off + c0: s_off + c0 + 512]
                mmt = k.mm_group(
                    [(ps_j, qtile,
                      kf[:, hkl * KV + c0: hkl * KV + c0 + 512], qdeps)],
                    banks=list(s_banks) if j == 0 else [])
                sc_toks.append(mmt)
            ps_full = k.ps[:, s_off: s_off + KW]
            scrv = scrr[:, par * 2048:(par + 1) * 2048]
            ev = k.s(lambda e, ps_full=ps_full, scrv=scrv: e.copy(
                out=scrv, in_=ps_full),
                deps=sc_toks + scr_war.pop(par, []), rb=list(s_banks))
            nm = srows[:, par:par + 1]
            sm = srows[:, 2 + par:3 + par]
            mx = k.v(lambda e, scrv=scrv, nm=nm: e.reduce_max(
                out=nm, in_=scrv, axis=AX.X, negate=True),
                deps=[ev] + nm_war.pop(par, []))
            prv = pr[:, par * KW:(par + 1) * KW]
            ex = k.s(lambda e, scrv=scrv, prv=prv, nm=nm, sm=sm:
                     e.activation(out=prv, in_=scrv, func=AF.Exp,
                                  bias=nm, accum_out=sm),
                     deps=[mx] + pr_war.pop(par, []))
            nm_war[par] = [ex]
            scr_war[par] = [ex]
            rc = k.v(lambda e, sm=sm, par=par: e.reciprocal(
                out=srows[:, 4 + par:5 + par], in_=sm), deps=[ex])
            nz = k.v(lambda e, prv=prv, par=par: e.tensor_scalar(
                out=prv, in0=prv, scalar1=srows[:, 4 + par:5 + par],
                scalar2=None, op0=ALU.mult), deps=[rc])
            if tt % 2 == 0:
                nzh0[0] = (nz, prv)
            else:
                flush_pend_f()
                tTs = []
                for half, (nzv, prvv) in enumerate((nzh0[0], (nz, prv))):
                    dstT = ptg.rearrange("p (j q) -> p j q", j=16)[
                        :, :, half * 128:(half + 1) * 128]
                    t_ = k.dma("sync", dstT,
                               prvv.rearrange("p (j q) -> p j q", j=16),
                               transpose=True, slot="pT",
                               deps=[nzv] + ptg_war)
                    tTs.append(t_)
                pr_war[0] = [tTs[0]]
                pr_war[1] = [tTs[1]]
                pend_f[0] = dict(tT=tTs, hl=hl, hkl=hkl, grp=grp, tt=tt)
        if full:
            flush_pend_f()
        tnow = [tensor_now()]
        for hl in range(gq):
            qf_war[hl] = qf_war[hl] + tnow
        for kl in range(gkv):
            kv_war[kl] = kv_war[kl] + tnow
            vt_war[kl] = vt_war[kl] + tnow
        if grp % wo_every == wo_every - 1:
            nh = wo_every * gq
            h0 = (grp // wo_every) * nh
            outp_war = []
            for dch in range(NCH):
                wsl = dch % 2
                wv = wobuf[:, wsl * nh * 128:(wsl + 1) * nh * 128]
                lt = k.dma("gpsimd",
                           wv.rearrange("p (h j) -> p h j", h=nh),
                           W["wo"][dch][:, h0:h0 + nh, :],
                           slot=f"wo{wsl}")
                for tg in range(2):
                    pb = tg % 2
                    ps = k.bank_cols(pb, 512)
                    parts = [(ps, wv[:, hl4 * 128:(hl4 + 1) * 128],
                              outp[:, hl4 * TPC + tg * 512:
                                   hl4 * TPC + tg * 512 + 512],
                              wo_pair_deps + [lt]) for hl4 in range(nh)]
                    mmw = k.mm_group(parts, banks=[pb])
                    k.slot_use(lt, mmw)
                    xs = xres[:, dch * TPC + tg * 512:
                              dch * TPC + tg * 512 + 512]
                    if bias and not bo_done[0]:
                        ut = k.v(lambda e, xs=xs, ps=ps, dch=dch:
                                 e.scalar_tensor_tensor(
                                     out=xs, in0=ps,
                                     scalar=bcols[:, 3 * NCH + dch:
                                                  3 * NCH + dch + 1],
                                     in1=xs, op0=ALU.add, op1=ALU.add),
                                 deps=[mmw] + list(xres_toks) + consts, rb=[pb])
                    else:
                        ut = k.v(lambda e, xs=xs, ps=ps: e.tensor_tensor(
                            out=xs, in0=xs, in1=ps, op=ALU.add),
                            deps=[mmw] + list(xres_toks), rb=[pb])
                    outp_war.append(ut)
                    upd_toks.append(ut)
            bo_done[0] = True
            wo_pair_deps = []
    return upd_toks


def _rope_tables_il(pos, scale):
    hd = 128
    inv = 1.0 / (10000.0 ** (np.arange(0, hd, 2, dtype=np.float64) / hd))
    fr = pos[None, :].astype(np.float64) * inv[:, None]
    c = np.repeat(np.cos(fr), 2, axis=0) * scale
    s = np.repeat(np.sin(fr), 2, axis=0) * scale
    return c.astype(np.float32), s.astype(np.float32)


def _rope_tables_half(pos, scale):
    hd = 128
    frac = np.arange(0, hd, 2, dtype=np.float64) / hd
    ts = 10000.0 ** frac
    fr = pos[None, :].astype(np.float64) / ts[:, None]
    c = np.concatenate([np.cos(fr), np.cos(fr)], axis=0) * scale
    s = np.concatenate([np.sin(fr), np.sin(fr)], axis=0) * scale
    return c.astype(np.float32), s.astype(np.float32)


def _pack_tab(c, s_):
    T = c.shape[1]
    out = np.empty((128, 2 * T), np.float32)
    blocks = [(i * 1024, 1024) for i in range(T // 1024)]
    if T % 1024:
        blocks.append((T - T % 1024, T % 1024))
    for off, wd in blocks:
        out[:, 2 * off: 2 * off + wd] = c[:, off:off + wd]
        out[:, 2 * off + wd: 2 * off + 2 * wd] = s_[:, off:off + wd]
    return out.astype(BF)


def _rot_pt(mode):
    P = np.zeros((128, 128), np.float32)
    if mode == "il":
        for i in range(64):
            P[2 * i, 2 * i + 1] = -1.0
            P[2 * i + 1, 2 * i] = 1.0
    else:
        for i in range(64):
            P[i, i + 64] = -1.0
            P[i + 64, i] = 1.0
    return np.ascontiguousarray(P.T).astype(BF)


def host_prep_attn(which, inputs):
    if which == "l0a":
        wq = prep_fm(inputs["L0_attn_a_Wq"].T).astype(BF)
        wk = prep_fm(inputs["L0_attn_a_Wk"].T).astype(BF)
        wv = prep_fm(inputs["L0_attn_a_Wv"].T).astype(BF)
        wo = prep_fm(inputs["L0_attn_a_Wo"].T).astype(BF)
        KW, WIN = 640, 512
        cfg = dict(nqh=16, nkvh=16, gq=2, gkv=2, kw=KW, rope="il",
                   qknorm=False, bias=False, kvw=1536)
        shared = dict(wq=wq, wk=wk, wv=wv, wo=wo, rot=_rot_pt("il"))
        percore = []
        for h2 in range(2):
            pos_q = np.arange(TPC) + h2 * TPC
            pos_k = np.maximum(np.arange(1536) + h2 * TPC - 512, 0)
            scale = 1.0 / np.sqrt(128.0)
            cq, sq = _rope_tables_il(pos_q, scale)
            ck, sk = _rope_tables_il(pos_k, 1.0)
            tabq = _pack_tab(cq, sq)
            tabk = _pack_tab(ck, sk)
            masks = np.full((8, 128, KW), NEG, np.float32)
            for tt in range(8):
                for i in range(128):
                    p = h2 * TPC + tt * 128 + i
                    for j in range(KW):
                        kg = h2 * TPC + tt * 128 + 512 + j - TPC
                        if i < j <= i + WIN and kg >= 0:
                            masks[tt, i, j] = 0.0
            percore.append(dict(tabq=tabq, tabk=tabk,
                                masks=masks.astype(BF)))
        return cfg, shared, percore
    if which == "l1a":
        wq = prep_fm(inputs["L1_attn_a_wq"].T).astype(BF)
        wk = prep_fm(inputs["L1_attn_a_wk"].T).astype(BF)
        wv = prep_fm(inputs["L1_attn_a_wv"].T).astype(BF)
        wo = prep_fm(inputs["L1_attn_a_wo"].T).astype(BF)
        bcols = np.zeros((128, 64), np.float32)
        bcols[:, 0:16] = inputs["L1_attn_a_bq"].reshape(16, 128).T
        bcols[:, 16:32] = inputs["L1_attn_a_bk"].reshape(16, 128).T
        bcols[:, 32:48] = inputs["L1_attn_a_bv"].reshape(16, 128).T
        bcols[:, 48:64] = inputs["L1_attn_a_bo"].reshape(16, 128).T
        cfg = dict(nqh=16, nkvh=16, gq=2, gkv=2, kw=None, rope=None,
                   qknorm=False, bias=True)
        shared = dict(wq=wq, wk=wk, wv=wv, wo=wo, bcols=bcols)
        return cfg, shared, [dict(), dict()]
    if which == "l1b":
        qk = inputs["L1_attn_b_q_kernel"]
        kvk = inputs["L1_attn_b_kv_kernel"]
        ok = inputs["L1_attn_b_out_kernel"]
        wq = prep_fm(np.ascontiguousarray(
            qk.transpose(1, 0, 2).reshape(D, 16 * 128))).astype(BF)
        wk = prep_fm(np.ascontiguousarray(
            kvk[0].transpose(1, 0, 2).reshape(D, 8 * 128))).astype(BF)
        wv = prep_fm(np.ascontiguousarray(
            kvk[1].transpose(1, 0, 2).reshape(D, 8 * 128))).astype(BF)
        wo = prep_fm(np.ascontiguousarray(
            ok.reshape(16 * 128, D))).astype(BF)
        nsc = np.zeros((128, 2), np.float32)
        nsc[:, 0] = inputs["L1_attn_b_q_norm_scale"]
        nsc[:, 1] = inputs["L1_attn_b_k_norm_scale"]
        KW, WIN = 1152, 1024
        cfg = dict(nqh=16, nkvh=8, gq=2, gkv=1, kw=KW, rope="half",
                   qknorm=True, bias=False, wo_every=1)
        shared = dict(wq=wq, wk=wk, wv=wv, wo=wo, rot=_rot_pt("half"),
                      nscales=nsc)
        percore = []
        for h2 in range(2):
            pos_q = np.arange(TPC) + h2 * TPC
            pos_k = np.maximum(np.arange(2 * TPC) + h2 * TPC - TPC, 0)
            cq, sq = _rope_tables_half(pos_q, 1.0)
            ck, sk = _rope_tables_half(pos_k, 1.0)
            tabq = _pack_tab(cq, sq)
            tabk = _pack_tab(ck, sk)
            masks = np.full((8, 128, KW), NEG, np.float32)
            for tt in range(8):
                for i in range(128):
                    for j in range(KW):
                        kg = h2 * TPC + tt * 128 + j - TPC
                        if i < j <= i + WIN and kg >= 0:
                            masks[tt, i, j] = 0.0
            percore.append(dict(tabq=tabq, tabk=tabk,
                                masks=masks.astype(BF)))
        return cfg, shared, percore
    raise ValueError(which)


def build_attn_nc(which, cfg, shared, percore):
    nc = bass.Bass("TRN2", target_bir_lowering=False, debug=False)
    xr = nc.dram_tensor("xr", [D, TPC], F32, kind="ExternalInput").ap()
    yr = nc.dram_tensor("yr", [D, TPC], F32, kind="ExternalOutput").ap()
    ident = nc.dram_tensor("ident", [128, 128], F32, kind="ExternalInput").ap()
    W = {}
    nfo_q = cfg["nqh"]
    nfo_kv = cfg["nkvh"]
    W["wq"] = nc.dram_tensor("w_wq", [nfo_q, 128, NCH, 128], F32,
                             kind="ExternalInput").ap()
    W["wk"] = nc.dram_tensor("w_wk", [nfo_kv, 128, NCH, 128], F32,
                             kind="ExternalInput").ap()
    W["wv"] = nc.dram_tensor("w_wv", [nfo_kv, 128, NCH, 128], F32,
                             kind="ExternalInput").ap()
    W["wo"] = nc.dram_tensor("w_wo", [NCH, 128, nfo_q, 128], F32,
                             kind="ExternalInput").ap()
    if cfg["rope"]:
        W["tabq"] = nc.dram_tensor("w_tabq", [128, 2 * TPC], BF16,
                                   kind="ExternalInput").ap()
        W["tabk"] = nc.dram_tensor(
            "w_tabk", [128, 2 * cfg.get("kvw", 2 * TPC)], BF16,
            kind="ExternalInput").ap()
        W["rot"] = nc.dram_tensor("w_rot", [128, 128], BF16,
                                  kind="ExternalInput").ap()
    if cfg["kw"] is not None:
        W["masks"] = nc.dram_tensor("w_masks", [8, 128, cfg["kw"]], BF16,
                                    kind="ExternalInput").ap()
    if cfg.get("bias"):
        W["bcols"] = nc.dram_tensor("w_bcols", [128, 64], F32,
                                    kind="ExternalInput").ap()
    if cfg.get("qknorm"):
        W["nscales"] = nc.dram_tensor("w_nscales", [128, 2], F32,
                                      kind="ExternalInput").ap()
    k = K(nc)
    k.setup_consts(ident)
    xres = k.sb_raw("xres", [128, NCH * TPC], F32).ap()
    xn = k.sb_raw("xn", [128, NCH * TPC], BF16).ap()
    scratch = k.sb_raw("lnscr", [128, 8 * 512], F32).ap()
    rows = scratch[0:1, 0:2560]
    xt = emit_load_res_fm(k, xr, xres)
    xnt, scrfree = emit_ln(k, xres, xt, xn, scratch, rows)
    upd = emit_attn(k, "a", cfg, xn, xnt, xres, xt, W,
                    spare_bf=xn, spare_f32=scratch, sf_deps=scrfree)
    st = emit_store_res_fm(k, xres, yr, upd)
    k.em.finalize([st])
    return nc


def run_attn_launch(which, inputs, xres_list, mode="hw", sim_cores=2):
    cfg, shared, percore = host_prep_attn(which, inputs)
    nc = build_attn_nc(which, cfg, shared, percore)
    in_maps = []
    ncores = 8 if mode == "hw" else sim_cores
    for c in range(ncores):
        m = {"xr": xres_list[c], "ident": _IDENT}
        for kk, v in shared.items():
            m["w_" + kk] = v
        for kk, v in percore[c % 2].items():
            m["w_" + kk] = v
        in_maps.append(m)
    res = run_bass_kernel_spmd(nc, in_maps, core_ids=list(range(8)))
    return [res.results[c]["yr"] for c in range(8)], nc


def emit_mamba(k: K, tag, xn, xn_toks, xres, xres_toks, W, lnscr,
               sf_deps=()):
    em = k.em
    DI, CVD, NFO, HALO, NC4, CL = 4096, 6144, 81, 32, 4, 256

    xres_d = k.dram(tag + "_xres_d", [D, TPC], F32)
    ccs_m = k.dram(tag + "_ccs", [D, HALO], BF16)
    ccd_m = k.dram(tag + "_ccd", [2 * D, HALO], BF16)
    gate_d = k.dram(tag + "_gate_d", [DI, TPC], BF16)
    convd = k.dram(tag + "_convd", [CVD, TPC], BF16)
    states_d = k.dram(tag + "_states_d", [NC4, 128, DI], BF16)
    cdx_d = k.dram(tag + "_cdx_d", [NC4, 64, 64], F32)
    h_src = k.dram(tag + "_hsrc", [128, DI], F32)
    h_dst = k.dram(tag + "_hdst", [256, DI], F32)
    ygd = k.dram(tag + "_ygd", [DI, TPC], BF16)

    conv_in = k.sb(tag + "_cvin", [128, 48 * 544], BF16)
    wbuf = k.sb(tag + "_wbuf", [128, 2 * 2 * NCH * 128], BF16)
    halo_sb = k.sb(tag + "_halo", [128, NCH * HALO], BF16)
    cwcols = k.sb(tag + "_cw", [128, 336], F32)
    dtw = k.sb(tag + "_dtw", [64, 64], F32)
    acol = k.sb(tag + "_acol", [128, 4], F32)
    cvout = k.sb(tag + "_cvout", [128, 4 * CL], BF16)
    xtm = k.sb(tag + "_xtm", [128, 4 * 128], BF16)
    btm = k.sb(tag + "_btm", [128, 16 * 128], BF16)
    cbt8 = wbuf[:, 4096:8192]
    bdf = k.sb(tag + "_bdf", [128, 4 * 128], BF16)
    gstg = k.sb(tag + "_gstg", [128, 1024], BF16)
    dftm = k.sb(tag + "_dftm", [128, 2 * 64], F32)
    trimask = k.sb(tag + "_tri", [128, 2 * CL], BF16)
    rrow = xres[0:1, 48 * CL + 2560: 48 * CL + 2560 + 5 * CL]
    cdxt = k.sb(tag + "_cdxt", [64, 64], F32)
    ones64 = k.sb(tag + "_o64", [64, 64], F32)
    ststg = k.sb(tag + "_stst", [128, DI], BF16)

    sqs_defined_early = True
    conv_raw = xres[:, 0:48 * CL]
    adt = xres[:, 48 * CL:48 * CL + 1024]
    acs = xres[:, 48 * CL + 1024: 48 * CL + 2048]
    awrk = xres[:, 48 * CL + 2048: 48 * CL + 2560]
    hbuf = conv_raw[:, 0:DI]
    cdflat = conv_raw[0:1, DI:2 * DI]
    yfm = xn[:, 0:32 * CL]
    hprev = xn[:, 32 * CL:32 * CL + DI]

    addn = []
    for ch in range(NCH):
        t = k.v(lambda e, ch=ch: e.tensor_tensor(
            out=xres[:, ch * TPC:(ch + 1) * TPC],
            in0=xres[:, ch * TPC:(ch + 1) * TPC],
            in1=xn[:, ch * TPC:(ch + 1) * TPC], op=ALU.add),
            deps=list(xres_toks) + list(xn_toks))
        addn.append(t)
    xspill = k.dma("sync", xres_d.rearrange("(ch p) t -> p ch t", p=128),
                   xres[:, :].rearrange("p (ch t) -> p ch t", ch=NCH),
                   slot="xsp", deps=addn)

    c_cw = k.dma("gpsimd", cwcols[:, :], W["cwcols"], slot="ccw")
    sqs0 = [lnscr[:, i * 512:(i + 1) * 512] for i in range(8)]
    sqw0 = [list(sf_deps) for _ in range(8)]
    rmsrow = rrow[0:1, 0:512]
    xn2_toks = []
    for tg in range(2):
        c0 = tg * 512
        rps = k.bank_cols(3, 512, rows=1)
        for grpi in range(2):
            parts = []
            for ch in range(grpi * 8, grpi * 8 + 8):
                sl = ch % 8
                st = k.s(lambda e, ch=ch, sl=sl, c0=c0: e.activation(
                    out=sqs0[sl], in_=xn[:, ch * TPC + c0:ch * TPC + c0 + 512],
                    func=AF.Square), deps=list(xn_toks) + sqw0[sl] + addn)
                parts.append((rps, k.onesc[:, 1:2], sqs0[sl], [st]))
            mmr0 = k.mm_group(parts, banks=[3], extra_deps=k.consts["ones"],
                              cont=(grpi == 1), open_end=(grpi == 0))
            for sl in range(8):
                sqw0[sl] = [mmr0]
        w0 = xn2_toks[-1:]
        m1 = k.v(lambda e: e.tensor_scalar(
            out=rmsrow[0:1, 0:512], in0=rps, scalar1=1.0 / D, scalar2=1e-6,
            op0=ALU.mult, op1=ALU.add), deps=[mmr0, xspill] + w0, rb=[3])
        m2 = k.s(lambda e: e.activation(out=rmsrow[0:1, 0:512],
                                        in_=rmsrow[0:1, 0:512], func=AF.Ln),
                 deps=[m1])
        m3 = k.v(lambda e: e.tensor_scalar(
            out=rmsrow[0:1, 0:512], in0=rmsrow[0:1, 0:512], scalar1=-0.5,
            scalar2=None, op0=ALU.mult), deps=[m2])
        m4 = k.s(lambda e: e.activation(out=rmsrow[0:1, 0:512],
                                        in_=rmsrow[0:1, 0:512], func=AF.Exp),
                 deps=[m3])
        bps0 = k.bank_cols(2, 512)
        mmb0 = k.mm_group([(bps0, k.onesr[:, :], rmsrow[0:1, 0:512], [m4])],
                          banks=[2], extra_deps=k.consts["ones"])
        for ch in range(NCH):
            tt = k.v(lambda e, ch=ch, c0=c0, bps0=bps0: e.scalar_tensor_tensor(
                out=xn[:, ch * TPC + c0:ch * TPC + c0 + 512],
                in0=xn[:, ch * TPC + c0:ch * TPC + c0 + 512],
                scalar=cwcols[:, 320 + ch:321 + ch], in1=bps0,
                op0=ALU.mult, op1=ALU.mult),
                deps=[mmb0, c_cw] + sqw0[ch % 8], rb=[2])
            xn2_toks.append(tt)
    xn_toks = xn2_toks
    rms_rrow_war = [mmb0] + xn2_toks[-1:]

    t_h = k.dma("sync", ccs_m.rearrange("(ch p) t -> p ch t", p=128),
                xn.rearrange("p (ch t) -> p ch t", ch=NCH)[:, :, TPC - HALO:],
                slot="hsrc", deps=list(xn_toks))
    t_cc = emit_allgather(k, tag + "h", ccs_m, ccd_m, [t_h])
    th_l = k.dma("sync", halo_sb[:, :].rearrange("p (ch t) -> p ch t", ch=NCH),
                 ccd_m[0:D].rearrange("(ch p) t -> p ch t", p=128),
                 slot="hld", deps=[t_cc])
    c_dtw = k.dma("gpsimd", dtw[:, :], W["dtw"], slot="cdtw")
    c_ac = k.dma("gpsimd", acol[:, 0:3], W["acols"], slot="cac")
    c_fl = k.dma("gpsimd", acol[:, 3:4], W["flag"], slot="cfl")
    c_tri = k.dma("gpsimd", trimask[:, :], W["tri"], slot="ctri")
    t_o = k.v(lambda e: e.memset(ones64[:, :], 1.0))
    a1 = k.s(lambda e: e.activation(out=acol[:, 2:3], in_=acol[:, 1:2],
                                    func=AF.Exp), deps=[c_ac])
    a2 = k.v(lambda e: e.tensor_scalar(out=acol[:, 2:3], in0=acol[:, 2:3],
                                       scalar1=-1.0, scalar2=None,
                                       op0=ALU.mult), deps=[a1])
    th_f = k.v(lambda e: e.tensor_scalar(
        out=halo_sb[:, :], in0=halo_sb[:, :], scalar1=acol[:, 3:4],
        scalar2=None, op0=ALU.mult), deps=[th_l, c_fl])
    halo_toks = [th_f]

    slabc = [0]
    sq_war = [[] for _ in range(8)]
    sqs = [lnscr[:, i * CL:(i + 1) * CL] for i in range(8)]
    tmpf = lnscr[:, 8 * CL: 9 * CL]
    tmpf2 = lnscr[:, 9 * CL:10 * CL]
    bcv = awrk[:, 256:512]
    dfw = awrk[:, 0:256]

    state = dict(cvout_war=[[] for _ in range(8)], xtm_war={0: [], 1: []},
                 btm_war=[],
                 cb_war=[], lc_war=[], chs_war=[], bdf_war=[], dftm_war=[],
                 yfm_war=[[] for _ in range(32)], stst_war=[], cdxt_war=[],
                 rrow_war=rms_rrow_war, bcv_war=[xspill], tmp_war=[],
                 tmp2_war=[],
                 acs_tok=[None, None], dfz_war=[xspill], adt_done=[],
                 conv_done=[xspill], cdsp=[])

    def ablock(th):
        own0 = th * 512
        z = awrk[0:64, 0:512]
        z2 = acs[0:64, th * 512:(th + 1) * 512]
        ps = k.bank_cols(3, 512, rows=64)
        mmz = k.mm_group([(ps, dtw[:, :], adt[0:64, own0:own0 + 512],
                           state["adt_done"] + [c_dtw])], banks=[3])
        w0 = state["dfz_war"]
        e0 = k.v(lambda e: e.tensor_scalar(
            out=z, in0=ps, scalar1=acol[0:64, 0:1], scalar2=None,
            op0=ALU.add), deps=[mmz, c_ac] + w0, rb=[3])
        e1a = k.v(lambda e: e.tensor_scalar(out=z2, in0=z, scalar1=-1.0,
                                            scalar2=None, op0=ALU.mult),
                  deps=[e0] + w0)
        e1 = k.v(lambda e: e.tensor_tensor(out=z2, in0=z2, in1=z,
                                           op=ALU.min), deps=[e1a])
        e2 = k.s(lambda e: e.activation(out=z2, in_=z2, func=AF.Exp),
                 deps=[e1])
        e3 = k.v(lambda e: e.tensor_scalar(out=z2, in0=z2, scalar1=1.0,
                                           scalar2=None, op0=ALU.add),
                 deps=[e2])
        e4 = k.s(lambda e: e.activation(out=z2, in_=z2, func=AF.Ln),
                 deps=[e3])
        e5 = k.v(lambda e: e.tensor_scalar(out=z, in0=z, scalar1=0.0,
                                           scalar2=None, op0=ALU.max),
                 deps=[e4])
        e6 = k.v(lambda e: e.tensor_tensor(out=z, in0=z, in1=z2, op=ALU.add),
                 deps=[e5])
        e7 = k.v(lambda e: e.tensor_scalar(out=z, in0=z,
                                           scalar1=acol[0:64, 2:3],
                                           scalar2=None, op0=ALU.mult),
                 deps=[e6, a2])
        if th == 0:
            e8 = k.v(lambda e: e.tensor_tensor_scan(
                out=acs[0:64, 0:512], data0=z, data1=z, initial=0.0,
                op0=ALU.add, op1=ALU.bypass),
                deps=[e7] + ([state["acs_tok"][0]] if state["acs_tok"][0]
                             else []))
        else:
            e8 = k.v(lambda e: e.tensor_tensor_scan(
                out=acs[0:64, 512:1024], data0=z, data1=z,
                initial=acs[0:64, 511:512], op0=ALU.add, op1=ALU.bypass),
                deps=[e7, state["acs_tok"][0]])
        state["acs_tok"][th] = e8
        state["dfz_war"] = [e8]
        return e8

    def cdx(c):
        at = state["acs_tok"][c // 2]
        e_i = (c + 1) * CL - 1
        w = state["cdxt_war"]
        col = cdxt[:, 0:1]
        if c == 0:
            t1 = k.v(lambda e: e.tensor_copy(out=col, in_=acs[0:64, 255:256]),
                     deps=[at] + w)
        else:
            t1 = k.v(lambda e, e_i=e_i, c=c: e.tensor_scalar(
                out=col, in0=acs[0:64, e_i:e_i + 1],
                scalar1=acs[0:64, c * CL - 1:c * CL], op0=ALU.subtract,
                scalar2=None), deps=[at] + w)
        t2 = k.s(lambda e: e.activation(out=col, in_=col, func=AF.Exp),
                 deps=[t1])
        t3 = k.v(lambda e: e.tensor_scalar(out=cdxt[:, :], in0=ones64[:, :],
                                           scalar1=col, scalar2=None,
                                           op0=ALU.mult), deps=[t2, t_o] + w)
        sp = k.dma("sync", cdx_d[c], cdxt[0:64, :], slot="cdsp",
                   deps=[t3])
        state["cdxt_war"] = [sp]
        state.setdefault("cdsp", []).append(sp)
        return sp

    def pass1(c):
        th = c // 2
        c0 = HALO + (c % 2) * CL
        at = state["acs_tok"][th]
        sq_ps = k.bank_cols(2, CL, rows=1)
        first = True
        cr_toks = []
        for grpi in range(6):
            parts = []
            for chi in range(grpi * 8, grpi * 8 + 8):
                sl = chi % 8
                cr = conv_raw[:, chi * CL:(chi + 1) * CL]
                ci = conv_in[:, chi * 544:(chi + 1) * 544]
                t0 = k.v(lambda e, ci=ci, cr=cr, c0=c0, chi=chi:
                         e.tensor_scalar(
                             out=cr, in0=ci[:, c0 - 3:c0 - 3 + CL],
                             scalar1=cwcols[:, chi * 4:chi * 4 + 1],
                             scalar2=cwcols[:, 192 + chi:193 + chi],
                             op0=ALU.mult, op1=ALU.add),
                         deps=state["conv_evt"][chi] + [c_cw]
                         + state.setdefault("crw", {}).pop(chi, []))
                for dd in (1, 2, 3):
                    t0 = k.v(lambda e, ci=ci, cr=cr, c0=c0, chi=chi, dd=dd:
                             e.scalar_tensor_tensor(
                                 out=cr, in0=ci[:, c0 - 3 + dd:
                                                c0 - 3 + dd + CL],
                                 scalar=cwcols[:, chi * 4 + dd:
                                               chi * 4 + dd + 1],
                                 in1=cr, op0=ALU.mult, op1=ALU.add),
                             deps=[t0])
                cr_toks.append(t0)
                st = k.s(lambda e, cr=cr, sl=sl: e.activation(
                    out=sqs[sl], in_=cr, func=AF.Square),
                    deps=[t0] + sq_war[sl])
                state["crw"][chi] = [st]
                parts.append((sq_ps, k.onesc[:, 1:2], sqs[sl], [st]))
            mmt = k.mm_group(parts, banks=[2], extra_deps=k.consts["ones"],
                             cont=(grpi > 0), open_end=(grpi < 5))
            for sl in range(8):
                sq_war[sl] = [mmt]
        rr = rrow[0:1, 4 * CL:5 * CL]
        r1 = k.v(lambda e: e.tensor_scalar(
            out=rr, in0=sq_ps, scalar1=1.0 / CVD, scalar2=1e-6,
            op0=ALU.mult, op1=ALU.add), deps=[mmt] + state["rrow_war"],
            rb=[2])
        r2 = k.s(lambda e: e.activation(out=rr, in_=rr, func=AF.Ln), deps=[r1])
        r3 = k.v(lambda e: e.tensor_scalar(out=rr, in0=rr, scalar1=-0.5,
                                           scalar2=None, op0=ALU.mult),
                 deps=[r2])
        r4 = k.s(lambda e: e.activation(out=rr, in_=rr, func=AF.Exp),
                 deps=[r3])
        bps = k.ps[:, 2 * 512: 2 * 512 + CL]
        mmb = k.mm_group([(bps, k.onesr[:, :], rr, [r4])], banks=[2],
                         extra_deps=k.consts["ones"])
        b1 = k.s(lambda e: e.copy(out=bcv, in_=bps),
                 deps=[mmb] + state["bcv_war"], rb=[2])
        state["rrow_war"] = [mmb]
        w = state["dfz_war"]
        d1 = k.v(lambda e, c=c: e.tensor_scalar(
            out=dfw[0:64, :], in0=acs[0:64, c * CL:(c + 1) * CL],
            scalar1=acs[0:64, (c + 1) * CL - 1:(c + 1) * CL],
            scalar2=-1.0, op0=ALU.subtract, op1=ALU.mult),
            deps=[at] + w)
        d2 = k.s(lambda e: e.activation(out=dfw[0:64, :], in_=dfw[0:64, :],
                                        func=AF.Exp), deps=[d1])
        state["dfz_war"] = [d2]
        dtts = []
        for tt in range(2):
            tps = k.ps[:, 3 * 512:3 * 512 + 64]
            mt = k.transpose(tps, dfw[0:64, tt * 128:(tt + 1) * 128],
                             k.id32[0:64, 0:64], banks=[3],
                             deps=[d2] + k.consts["ident"])
            ev = k.v(lambda e, tt=tt, tps=tps: e.tensor_copy(
                out=dftm[:, tt * 64:(tt + 1) * 64], in_=tps),
                deps=[mt] + state["dftm_war"], rb=[3])
            dtts.append(ev)
        state["dftm_war"] = dtts
        state["dfz_war"] = dtts
        cdx(c)
        stst_toks = []
        order = list(range(32, 40)) + list(range(40, 48)) + list(range(32))
        u1_last = None
        for chi in order:
            sl = chi % 4
            ts1 = chi % 2
            twk = "tmp_war" if ts1 == 0 else "tmp2_war"
            tv = (tmpf, tmpf2)[ts1]
            cr = conv_raw[:, chi * CL:(chi + 1) * CL]
            u1 = k.v(lambda e, cr=cr, chi=chi, tv=tv: e.scalar_tensor_tensor(
                out=tv, in0=cr, scalar=cwcols[:, 240 + chi:241 + chi],
                in1=bcv, op0=ALU.mult, op1=ALU.mult),
                deps=cr_toks[-1:] + [b1, c_cw] + state[twk])
            u1_last = u1
            cv = cvout[:, sl * CL:(sl + 1) * CL]
            u2 = k.s(lambda e, cv=cv, tv=tv: e.activation(out=cv, in_=tv,
                                                          func=AF.Silu),
                     deps=[u1] + state["cvout_war"][sl])
            state[twk] = [u2]
            sp = k.dma("sync", convd[chi * 128:(chi + 1) * 128,
                                     c * CL:(c + 1) * CL], cv,
                       slot=f"cvsp{sl % 2}", deps=[u2])
            state.setdefault("cvsp_last", {})[sl % 2] = sp
            war_new = [sp]
            if 32 <= chi < 40:
                g = chi - 32
                tb = k.dma("sync",
                           btm[:, g * 256:(g + 1) * 256].rearrange(
                               "p (j q) -> p j q", j=2),
                           cv.rearrange("p (j q) -> p j q", j=2),
                           transpose=True, slot="btp",
                           deps=[u2] + state["btm_war"])
                state["btm_war"] = []
                war_new.append(tb)
                state["btm_tok"] = state.get("btm_tok", []) + [tb]
            elif chi < 32:
                xs1 = chi % 2
                xb1 = xs1 * 256
                tx = k.dma("sync",
                           xtm[:, xb1:xb1 + 256].rearrange(
                               "p (j q) -> p j q", j=2),
                           cv.rearrange("p (j q) -> p j q", j=2),
                           transpose=True, slot=f"xtp{xs1}",
                           deps=[u2] + state["xtm_war"][xs1])
                war_new.append(tx)
                xw = []
                for hh in (2 * chi, 2 * chi + 1):
                    bo1 = (hh % 2) * 256
                    bdfw = state.setdefault("bdfw", {})
                    bdts = []
                    for tt in range(2):
                        bb = k.v(lambda e, hh=hh, tt=tt, bo1=bo1:
                                 e.tensor_scalar(
                                     out=bdf[:, bo1 + tt * 128:
                                             bo1 + (tt + 1) * 128],
                                     in0=btm[:, ((hh // 8) * 2 + tt) * 128:
                                             ((hh // 8) * 2 + tt + 1) * 128],
                                     scalar1=dftm[:, tt * 64 + hh:
                                                  tt * 64 + hh + 1],
                                     scalar2=None, op0=ALU.mult),
                                 deps=state["btm_tok"] + dtts
                                 + bdfw.pop(hh % 2, []))
                        bdts.append(bb)
                    sps = k.bank_cols(4 + (hh % 2), 64)
                    mmS = k.mm_group(
                        [(sps, bdf[:, bo1 + tt * 128:bo1 + (tt + 1) * 128],
                          xtm[:, xb1 + tt * 128 + (hh % 2) * 64:
                              xb1 + tt * 128 + (hh % 2) * 64 + 64],
                          [tx] + bdts) for tt in range(2)],
                        banks=[4 + (hh % 2)])
                    bdfw[hh % 2] = [mmS]
                    ev = k.v(lambda e, hh=hh, sps=sps: e.tensor_copy(
                        out=ststg[:, hh * 64:(hh + 1) * 64], in_=sps),
                        deps=[mmS] + state["stst_war"], rb=[4 + (hh % 2)])
                    stst_toks.append(ev)
                    xw.append(mmS)
                state["xtm_war"][xs1] = xw + [tx]
            state["cvout_war"][sl] = war_new
        ssp = k.dma("sync", states_d[c], ststg[:, :], slot="stsp",
                    deps=stst_toks)
        state["stst_war"] = [ssp]
        state["bcv_war"] = [u1_last]
        state["btm_war"] = [Tok("c_vector", em.counts["c_vector"])]
        state["btm_tok"] = []
        state["conv_done"] = [Tok("c_tensor", em.counts["c_tensor"]),
                              Tok("c_vector", em.counts["c_vector"]),
                              Tok("c_scalar", em.counts["c_scalar"])]

    for sl in range(8):
        sq_war[sl] = list(sqw0[sl])
    proj_evac_war = {("a",): [xspill]}
    gate_sp_d = {0: [], 1: []}
    for th in range(2):
        own0 = th * 512
        pair_lt = {}
        for fo in range(NFO):
            is_gate = 48 <= fo < 80
            is_adt = fo == 80
            fop = fo & ~1
            if fo == fop:
                sl = slabc[0] % 2
                slabc[0] += 1
                n_fo = 2 if fop + 1 < NFO else 1
                wpair = wbuf[:, sl * 4096: sl * 4096 + n_fo * 2048]
                lt = k.dma("gpsimd",
                           wpair.rearrange("p (f kc j) -> p f kc j",
                                           f=n_fo, kc=NCH),
                           W["inproj"][fop:fop + n_fo].rearrange(
                               "f p kc j -> p f kc j"),
                           slot=f"w{sl}")
                pair_lt[fop] = (lt, wpair)
            lt, wpair = pair_lt[fop]
            wv = wpair[:, (fo - fop) * 2048:(fo - fop + 1) * 2048]
            last_use = None
            if not (is_gate or is_adt):
                ps = k.bank_cols(2, HALO)
                if th == 0:
                    parts = [(ps, wv[:, kc * 128:(kc + 1) * 128],
                              halo_sb[:, kc * HALO:(kc + 1) * HALO],
                              halo_toks + [lt]) for kc in range(NCH)]
                else:
                    parts = [(ps, wv[:, kc * 128:(kc + 1) * 128],
                              xn[:, kc * TPC + 480: kc * TPC + 512],
                              list(xn_toks) + [lt]) for kc in range(NCH)]
                mmh = k.mm_group(parts, banks=[2])
                dst = conv_in[:, fo * 544: fo * 544 + HALO]
                e1 = k.s(lambda e, dst=dst, ps=ps: e.copy(out=dst, in_=ps),
                         deps=[mmh] + proj_evac_war.pop(("h", fo), []),
                         rb=[2])
                proj_evac_war[("h", fo)] = []
                last_use = e1
            pb = fo % 2
            ps = k.bank_cols(pb, 512)
            parts = [(ps, wv[:, kc * 128:(kc + 1) * 128],
                      xn[:, kc * TPC + own0: kc * TPC + own0 + 512],
                      list(xn_toks) + [lt]) for kc in range(NCH)]
            mmo = k.mm_group(parts, banks=[pb])
            if is_adt:
                ev = k.v(lambda e, ps=ps, own0=own0: e.tensor_copy(
                    out=adt[:, own0:own0 + 512], in_=ps),
                    deps=[mmo] + proj_evac_war.pop(("a",), []), rb=[pb])
                proj_evac_war[("a",)] = []
                state["adt_done"].append(ev)
            elif is_gate:
                gsl = (fo - 48) % 2
                gs = gstg[:, gsl * 512:(gsl + 1) * 512]
                ev = k.v(lambda e, ps=ps, gs=gs: e.tensor_copy(
                    out=gs, in_=ps),
                    deps=[mmo] + proj_evac_war.pop(("g", gsl), []), rb=[pb])
                sp = k.dma("sync",
                           gate_d[(fo - 48) * 128:(fo - 47) * 128,
                                  own0:own0 + 512], gs,
                           slot=f"gsp{gsl}", deps=[ev])
                proj_evac_war[("g", gsl)] = [sp]
                gate_sp_d[gsl] = [sp]
            else:
                dst = conv_in[:, fo * 544 + HALO: fo * 544 + 544]
                ev = k.s(lambda e, dst=dst, ps=ps: e.copy(out=dst, in_=ps),
                         deps=[mmo] + proj_evac_war.pop(("o", fo), []),
                         rb=[pb])
                proj_evac_war[("o", fo)] = []
            k.slot_use(lt, ev if last_use is None else last_use)
            k.slot_use(lt, ev)
            if not (is_gate or is_adt):
                state["conv_done"].append(ev)
                state.setdefault("conv_evt", {})[fo] = [ev] + (
                    [last_use] if last_use is not None else [])
                if last_use is not None:
                    state["conv_done"].append(last_use)
        ablock(th)
        for c in (2 * th, 2 * th + 1):
            pass1(c)
            state["conv_done"] = state["conv_done"][-3:]
        proj_evac_war = {("o", fo): list(state["conv_done"])
                         for fo in range(48)}
        proj_evac_war.update({("h", fo): list(state["conv_done"])
                              for fo in range(48)})
        proj_evac_war[("a",)] = []
        proj_evac_war[("g", 0)] = list(gate_sp_d[0])
        proj_evac_war[("g", 1)] = list(gate_sp_d[1])
        state["conv_done"] = []
    last_gate_sp = gate_sp_d[0] + gate_sp_d[1]

    hbA = hbuf
    hbB = hprev
    hz = k.v(lambda e: e.memset(hbA[:, :], 0.0),
             deps=state["tmp_war"] + sq_war[0] + [Tok("c_scalar",
                                                      em.counts["c_scalar"]),
                                                  Tok("c_tensor",
                                                      em.counts["c_tensor"])])

    def rec_chunk(c, hr, hw, rd_toks, wr_war, bf_war=None):
        ld1 = k.dma("sync", ststg[:, :], states_d[c], slot="stld",
                    deps=state["stst_war"])
        ld2 = k.dma("sync", cdflat,
                    cdx_d[c:c + 1].rearrange("c h d -> c (h d)"),
                    slot="cdld",
                    deps=state.get("cdflat_war", []) + state["cdsp"])
        nt = []
        nbf = []
        for piece in range(8):
            pb = piece % 2
            bp = k.bank_cols(pb, 512)
            mmc = k.mm_group(
                [(bp, k.onesr[:, :], cdflat[0:1, piece * 512:
                                            (piece + 1) * 512], [ld2])],
                banks=[pb], extra_deps=k.consts["ones"])
            v1 = k.v(lambda e, piece=piece, bp=bp, hr=hr, hw=hw:
                     e.tensor_tensor(
                         out=hw[:, piece * 512:(piece + 1) * 512],
                         in0=hr[:, piece * 512:(piece + 1) * 512],
                         in1=bp, op=ALU.mult),
                     deps=[mmc] + rd_toks + wr_war, rb=[pb])
            v2 = k.v(lambda e, piece=piece, hw=hw: e.tensor_tensor(
                out=hw[:, piece * 512:(piece + 1) * 512],
                in0=hw[:, piece * 512:(piece + 1) * 512],
                in1=ststg[:, piece * 512:(piece + 1) * 512],
                op=ALU.add), deps=[v1, ld1])
            nt.append(v2)
            if bf_war is not None:
                v3 = k.v(lambda e, piece=piece, hw=hw: e.tensor_copy(
                    out=hb_bf[:, piece * 512:(piece + 1) * 512],
                    in_=hw[:, piece * 512:(piece + 1) * 512]),
                    deps=[v2] + bf_war)
                nbf.append(v3)
        state["stst_war"] = nt
        state["cdflat_war"] = [Tok("c_tensor", em.counts["c_tensor"])]
        return nt, nbf

    state["cdflat_war"] = list(state["conv_done"]) + sq_war[0] \
        + state["tmp_war"] + [Tok("c_tensor", em.counts["c_tensor"])]
    rd = [hz]
    for c in range(NC4):
        rd, _ = rec_chunk(c, hbA, hbA, rd, [])
    hsp = k.dma("sync", h_src[:, :], hbA[:, :], slot="hsp", deps=rd)
    hcc = emit_allgather(k, tag + "s", h_src, h_dst, [hsp])
    hld = k.dma("sync", hbA[:, :], h_dst[0:128, :], slot="hseed",
                deps=[hcc, hsp])
    hse = k.v(lambda e: e.tensor_scalar(out=hbA[:, :], in0=hbA[:, :],
                                        scalar1=acol[:, 3:4], scalar2=None,
                                        op0=ALU.mult), deps=[hld, c_fl])
    state["stst_war"] = [hse]

    GC = 32 * CL
    xn_free = [Tok("c_tensor", em.counts["c_tensor"]), t_h, xspill] \
        + list(state.get("cvsp_last", {}).values())

    def gate_r(s):
        return conv_in[:, (s % 2) * GC:(s % 2) * GC + GC]

    def bfmb_r(s):
        return conv_in[:, 2 * GC + (s % 2) * 8 * CL:
                       2 * GC + (s % 2 + 1) * 8 * CL]

    def cfmb_r(s):
        return conv_in[:, 2 * GC + 16 * CL + (s % 2) * 8 * CL:
                       2 * GC + 16 * CL + (s % 2 + 1) * 8 * CL]

    def lcbt_r(s):
        return btm[:, (s % 4) * 2 * CL:(s % 4 + 1) * 2 * CL]

    def chs_r(s):
        return xn[:, 32 * CL + DI + (s % 4) * CL:
                  32 * CL + DI + (s % 4 + 1) * CL]

    rowsl = [
        dict(A=lnscr[:, 2560:2816], Bb=lnscr[:, 2816:3073],
             rex=lnscr[0:1, 3073:3329]),
        dict(A=lnscr[:, 3329:3585], Bb=lnscr[:, 3585:3842],
             rex=xres[0:1, 15616:15872]),
        dict(A=xres[:, 14848:15104], Bb=xres[:, 15104:15361],
             rex=xres[0:1, 16128:16384]),
    ]
    p2i = [Tok("c_vector", em.counts["c_vector"]),
           Tok("c_scalar", em.counts["c_scalar"])]
    rows_war = {0: sqw0[5] + sqw0[6] + [xspill],
                1: sqw0[6] + sqw0[7] + [xspill],
                2: list(rms_rrow_war) + [xspill]}
    lcbt_war = {s: [Tok("c_vector", em.counts["c_vector"]),
                    Tok("c_tensor", em.counts["c_tensor"])]
                for s in range(4)}
    chs_war = {s: list(xn_free) for s in range(4)}
    cvsp = list(state.get("cvsp_last", {}).values())
    bfm_war = {0: p2i + cvsp, 1: p2i + cvsp}
    cfm_war = {0: p2i + cvsp, 1: p2i + cvsp}
    gld_war = {0: list(p2i), 1: list(p2i)}
    xtm_war2 = state["xtm_war"]
    cv_war = {sl: list(state["cvout_war"][sl]) for sl in range(4)}
    tmp_r = (tmpf, tmpf2)
    tmp_war_r = {0: list(state["tmp_war"]), 1: list(state["tmp2_war"])}
    for chy in range(32):
        state["yfm_war"][chy] = list(xn_free)
    ygd_sp = []
    hc_ctr = [0]
    riw = [xspill] + sqw0[5] + sqw0[6] + sqw0[7] + list(rms_rrow_war)
    rowc = []
    for s, rs in enumerate(rowsl):
        m = k.v(lambda e, rs=rs: e.memset(rs["A"][0:1, :], 1.0), deps=riw)
        rowc.append([m])
    mB = k.v(lambda e: e.memset(rowsl[0]["Bb"][0:1, :], -1.0), deps=riw)
    dprev = mB
    for s, rs in enumerate(rowsl):
        d = k.dma("sync", rs["Bb"][1:2, 0:257],
                  rowsl[0]["Bb"][0:1, 0:257], slot="rci", deps=[dprev])
        rowc[s].append(d)
        dprev = d
    rows_war[0] = rows_war[0] + [dprev]

    def yield_pass2(c, hb_rd):
        cs = c % 2
        lb = k.dma("sync", bfmb_r(cs).rearrange("p (g t) -> p g t", g=8),
                   convd.rearrange("(g p) t -> p g t", p=128)[
                       :, 32:40, c * CL:(c + 1) * CL],
                   slot=f"lb{cs}", deps=bfm_war.pop(cs, []))
        lc = k.dma("sync", cfmb_r(cs).rearrange("p (g t) -> p g t", g=8),
                   convd.rearrange("(g p) t -> p g t", p=128)[
                       :, 40:48, c * CL:(c + 1) * CL],
                   slot=f"lc{cs}", deps=cfm_war.pop(cs, []))
        lg = k.dma("sync", gate_r(cs).rearrange("p (g t) -> p g t", g=32),
                   gate_d.rearrange("(g p) t -> p g t", p=128)[
                       :, :, c * CL:(c + 1) * CL],
                   slot=f"lg{cs}", deps=gld_war.pop(cs, []) + last_gate_sp)
        cb_toks = []
        cb_mm_last = None
        for i in range(16):
            g, sti = i // 2, i % 2
            cbb = (6, 7)[i % 2]
            cps = k.bank_cols(cbb, CL)
            mmc2 = k.mm_group(
                [(cps, bfmb_r(cs)[:, g * CL + sti * 128:
                                  g * CL + sti * 128 + 128],
                  cfmb_r(cs)[:, g * CL:(g + 1) * CL], [lb, lc])],
                banks=[cbb])
            cb_mm_last = mmc2
            ct = k.v(lambda e, g=g, sti=sti, cps=cps: e.tensor_tensor(
                out=cbt8[:, (g * 2 + sti) * CL:(g * 2 + sti + 1) * CL],
                in0=cps, in1=trimask[:, sti * CL:(sti + 1) * CL],
                op=ALU.mult), deps=[mmc2, c_tri] + state["cb_war"],
                rb=[cbb])
            cb_toks.append(ct)
        state["cb_war"] = []
        bfm_war[cs] = [cb_mm_last]
        yev_toks = []
        mmy_list = []
        ch_last = None
        for chx in range(32):
            xs = chx % 2
            xb = xs * 256
            lx = k.dma("sync", cvout[:, 2 * xs * CL:(2 * xs + 1) * CL],
                       convd[chx * 128:(chx + 1) * 128,
                             c * CL:(c + 1) * CL],
                       slot=f"lx{xs}", deps=cv_war.pop(2 * xs, []))
            tx = k.dma("sync",
                       xtm[:, xb:xb + 256].rearrange("p (j q) -> p j q",
                                                     j=2),
                       cvout[:, 2 * xs * CL:(2 * xs + 1) * CL].rearrange(
                           "p (j q) -> p j q", j=2),
                       transpose=True, slot=f"xtp{xs}",
                       deps=[lx] + xtm_war2.pop(xs, []))
            cv_war[2 * xs] = [tx]
            xw = []
            for hl2 in range(2):
                hh = 2 * chx + hl2
                g = hh // 8
                p = hh % 2
                hs = hc_ctr[0] % 3
                h4 = hc_ctr[0] % 4
                hc_ctr[0] += 1
                rs = rowsl[hs]
                rA, rB, rex = rs["A"], rs["Bb"], rs["rex"]
                rw = rows_war.pop(hs, [])
                b_x = (6, 7)[p]
                bce = k.bank_cols(b_x, CL)
                at = state["acs_tok"][c // 2]
                d1 = k.dma("sync", rA[1:2, 0:CL],
                           acs[hh:hh + 1, c * CL:(c + 1) * CL],
                           slot=f"rx{hs}", deps=[at] + rw)
                if c == 0:
                    d2 = k.dma("sync", rB[0:1, 1:257],
                               acs[hh:hh + 1, 0:CL], slot=f"ry{hs}",
                               deps=[at] + rw)
                    q3 = k.s(lambda e, rB=rB, rex=rex: e.activation(
                        out=rex, in_=rB[0:1, 1:1 + CL],
                        func=AF.Exp), deps=[d2] + rw)
                    q3a = q3
                else:
                    d2 = k.dma("sync", rB[0:1, 0:257],
                               acs[hh:hh + 1, c * CL - 1:(c + 1) * CL],
                               slot=f"ry{hs}", deps=[at] + rw)
                    q3a = k.v(lambda e, rB=rB, rex=rex:
                              e.tensor_scalar(
                                  out=rex, in0=rB[0:1, 1:1 + CL],
                                  scalar1=rB[0:1, 0:1], scalar2=None,
                                  op0=ALU.subtract), deps=[d2] + rw)
                    q3 = k.s(lambda e, rex=rex: e.activation(
                        out=rex, in_=rex, func=AF.Exp), deps=[q3a])
                mme = k.mm_group([(bce, k.onesr[:, :], rex, [q3])],
                                 banks=[b_x], extra_deps=k.consts["ones"])
                chv = chs_r(h4)
                ch1 = k.v(lambda e, g=g, bce=bce, chv=chv, cs=cs:
                          e.tensor_tensor(
                              out=chv, in0=cfmb_r(cs)[:, g * CL:
                                                      (g + 1) * CL],
                              in1=bce, op=ALU.mult),
                          deps=[mme, lc] + chs_war.pop(h4, []), rb=[b_x])
                ch_last = ch1
                lv = lcbt_r(h4)
                lw = lcbt_war.pop(h4, [])
                lts = []
                for sti in range(2):
                    bidx = (0, 2)[p] + sti
                    dps = k.bank_cols(bidx, CL)
                    mmf = k.mm_group(
                        [(dps, rA[0:2, sti * 128:sti * 128 + 128],
                          rB[0:2, 1:1 + CL],
                          [d1, d2] + rowc[hs])],
                        banks=[bidx])
                    z1 = k.v(lambda e, dps=dps: e.tensor_scalar(
                        out=dps, in0=dps, scalar1=0.0, scalar2=None,
                        op0=ALU.min), deps=[mmf], rb=[bidx], wb=[bidx])
                    z2 = k.s(lambda e, sti=sti, dps=dps, lv=lv:
                             e.activation(
                                 out=lv[:, sti * CL:(sti + 1) * CL],
                                 in_=dps, func=AF.Exp),
                             deps=[z1] + lw, rb=[bidx])
                    z3 = k.v(lambda e, sti=sti, g=g, lv=lv: e.tensor_tensor(
                        out=lv[:, sti * CL:(sti + 1) * CL],
                        in0=lv[:, sti * CL:(sti + 1) * CL],
                        in1=cbt8[:, (g * 2 + sti) * CL:
                                 (g * 2 + sti + 1) * CL],
                        op=ALU.mult), deps=[z2] + cb_toks)
                    lts.append(z3)
                    mmf_last = mmf
                rows_war[hs] = [mmf_last, q3a]
                b_y = 4 + p
                yps = (k.ps[0:64, b_y * 512:b_y * 512 + CL] if p == 0
                       else k.ps[64:128, b_y * 512:b_y * 512 + CL])
                parts = [(yps, xtm[:, xb + tt * 128 + hl2 * 64:
                                   xb + tt * 128 + hl2 * 64 + 64],
                          lv[:, tt * CL:(tt + 1) * CL], [tx] + lts)
                         for tt in range(2)]
                parts.append((yps, hb_bf[:, hh * 64:(hh + 1) * 64],
                              chv, hb_rd + [ch1]))
                mmy = k.mm_group(parts, banks=[b_y])
                mmy_list.append(mmy)
                chs_war[h4] = [mmy]
                lcbt_war[h4] = [mmy]
                ydst = yfm[hl2 * 64:hl2 * 64 + 64,
                           (hh // 2) * CL:(hh // 2 + 1) * CL]
                yv = k.s(lambda e, ydst=ydst, yps=yps: e.copy(
                    out=ydst, in_=yps),
                    deps=[mmy] + state["yfm_war"][hh // 2], rb=[b_y])
                yev_toks.append(yv)
                state["yfm_war"][hh // 2] = []
                xw.append(mmy)
            xtm_war2[xs] = xw + [tx]
        cfm_war[cs] = [cb_mm_last, ch_last]
        sq_ps2 = k.bank_cols(2, CL, rows=1)
        for grpi in range(4):
            parts = []
            for chy in range(grpi * 8, grpi * 8 + 8):
                sl = chy % 8
                st = k.s(lambda e, chy=chy, sl=sl: e.activation(
                    out=sqs[sl], in_=yfm[:, chy * CL:(chy + 1) * CL],
                    func=AF.Square), deps=yev_toks[-1:] + sq_war[sl])
                parts.append((sq_ps2, k.onesc[:, 1:2], sqs[sl], [st]))
            mmt2 = k.mm_group(parts, banks=[2], extra_deps=k.consts["ones"],
                              cont=(grpi > 0), open_end=(grpi < 3))
            for sl in range(8):
                sq_war[sl] = [mmt2]
        rr2 = rrow[0:1, 4 * CL:5 * CL]
        n1 = k.v(lambda e: e.tensor_scalar(
            out=rr2, in0=sq_ps2, scalar1=1.0 / DI, scalar2=1e-6,
            op0=ALU.mult, op1=ALU.add), deps=[mmt2] + state["rrow_war"],
            rb=[2])
        n2 = k.s(lambda e: e.activation(out=rr2, in_=rr2, func=AF.Ln),
                 deps=[n1])
        n3 = k.v(lambda e: e.tensor_scalar(out=rr2, in0=rr2, scalar1=-0.5,
                                           scalar2=None, op0=ALU.mult),
                 deps=[n2])
        n4 = k.s(lambda e: e.activation(out=rr2, in_=rr2, func=AF.Exp),
                 deps=[n3])
        bps2 = k.ps[:, 2 * 512:2 * 512 + CL]
        mmb2 = k.mm_group([(bps2, k.onesr[:, :], rr2, [n4])], banks=[2],
                          extra_deps=k.consts["ones"])
        b2 = k.s(lambda e: e.copy(out=bcv, in_=bps2),
                 deps=[mmb2] + state["bcv_war"], rb=[2])
        state["rrow_war"] = [mmb2]
        yg_toks = []
        bcv_r = []
        g2_last = None
        for chy in range(32):
            yv2 = yfm[:, chy * CL:(chy + 1) * CL]
            ts = chy % 2
            tv = tmp_r[ts]
            g1 = k.v(lambda e, yv2=yv2, chy=chy, tv=tv:
                     e.scalar_tensor_tensor(
                         out=tv, in0=yv2,
                         scalar=cwcols[:, 288 + chy:289 + chy],
                         in1=bcv, op0=ALU.mult, op1=ALU.mult),
                     deps=[b2, c_cw, yev_toks[2 * chy + 1]]
                     + tmp_war_r.pop(ts, []))
            gsl = 1 + 2 * (chy % 2)
            g2 = k.s(lambda e, chy=chy, gsl=gsl, cs=cs: e.activation(
                out=cvout[:, gsl * CL:(gsl + 1) * CL],
                in_=gate_r(cs)[:, chy * CL:(chy + 1) * CL],
                func=AF.Silu), deps=[lg] + cv_war.pop(gsl, []))
            g2_last = g2
            g3 = k.v(lambda e, yv2=yv2, gsl=gsl, tv=tv: e.tensor_tensor(
                out=yv2, in0=tv, in1=cvout[:, gsl * CL:(gsl + 1) * CL],
                op=ALU.mult), deps=[g1, g2])
            tmp_war_r[ts] = [g3]
            cv_war[gsl] = [g3]
            state["yfm_war"][chy] = [g3]
            bcv_r.append(g3)
            yg_toks.append(g3)
        state["bcv_war"] = bcv_r[-1:]
        gld_war[cs] = [g2_last]
        sp = k.dma("sync",
                   ygd.rearrange("(g p) t -> p g t", p=128)[
                       :, :, c * CL:(c + 1) * CL],
                   yfm.rearrange("p (g t) -> p g t", g=32),
                   slot="ygsp", deps=yg_toks)
        ygd_sp.append(sp)
        for chy in range(32):
            state["yfm_war"][chy].append(sp)
        return mmy_list

    hb_bf = wbuf[:, 0:4096]
    hbf0 = k.v(lambda e: e.tensor_copy(out=hb_bf, in_=hbA[:, :]),
               deps=[hse] + list(xn_free))
    rd_f = [hse]
    rd_bf = [hbf0]
    prev_mmy = []
    for c in range(NC4):
        hr = hbA if c % 2 == 0 else hbB
        hw = hbB if c % 2 == 0 else hbA
        mmy_toks = yield_pass2(c, rd_bf)
        if c < NC4 - 1:
            war_w = prev_mmy if c > 0 else list(xn_free)
            rd_f, rd_bf = rec_chunk(c, hr, hw, rd_f, war_w,
                                    bf_war=mmy_toks)
        prev_mmy = mmy_toks

    xrl = k.dma("sync", xres[:, :].rearrange("p (ch t) -> p ch t", ch=NCH),
                xres_d.rearrange("(ch p) t -> p ch t", p=128),
                slot="xrl",
                deps=[Tok("c_tensor", em.counts["c_tensor"]),
                      Tok("c_vector", em.counts["c_vector"]),
                      Tok("c_scalar", em.counts["c_scalar"])])
    upd = []
    yglb = conv_in[:, 0:32 * 512]
    yld_war = [Tok("c_scalar", em.counts["c_scalar"]),
               Tok("c_vector", em.counts["c_vector"])]
    op_war = {0: [Tok("c_tensor", em.counts["c_tensor"])],
              1: [Tok("c_tensor", em.counts["c_tensor"])]}
    for tg in range(2):
        yl = k.dma("sync", yglb.rearrange("p (g t) -> p g t", g=32),
                   ygd.rearrange("(g p) t -> p g t", p=128)[
                       :, :, tg * 512:(tg + 1) * 512],
                   slot="ygl", deps=ygd_sp + yld_war)
        yld_war = []
        for dch in range(NCH):
            parts = []
            lts = []
            for hhalf in range(2):
                sl = (dch * 2 + hhalf) % 2
                wv = wbuf[:, sl * NCH * 128:(sl + 1) * NCH * 128]
                lt = k.dma("gpsimd", wv,
                           W["outproj"][dch][:, hhalf * 16:(hhalf + 1) * 16]
                           .rearrange("p ic j -> p (ic j)"),
                           slot=f"w{sl}", deps=op_war.pop(sl, []))
                lts.append(lt)
                parts += [(k.bank_cols(dch % 2, 512),
                           wv[:, ic * 128:(ic + 1) * 128],
                           yglb[:, (hhalf * 16 + ic) * 512:
                                (hhalf * 16 + ic + 1) * 512],
                           [yl, lt]) for ic in range(16)]
            pb = dch % 2
            ps = k.bank_cols(pb, 512)
            mmo = k.mm_group(parts, banks=[pb])
            for lt in lts:
                k.slot_use(lt, mmo)
            yld_war.append(mmo)
            xs = xres[:, dch * TPC + tg * 512: dch * TPC + tg * 512 + 512]
            ut = k.v(lambda e, xs=xs, ps=ps: e.tensor_tensor(
                out=xs, in0=xs, in1=ps, op=ALU.add),
                deps=[mmo, xrl], rb=[pb])
            upd.append(ut)
    return upd


def host_prep_mamba(inputs):
    ipw = inputs["L0_attn_b_in_proj_w"]
    X_, Adt, B_, C_, G_ = (ipw[0:4096], ipw[4096:4160], ipw[4160:5184],
                           ipw[5184:6208], ipw[6208:10304])
    reord = np.concatenate(
        [X_, B_, C_, G_, Adt, np.zeros((64, D), np.float32)], axis=0)
    inproj = prep_fm(np.ascontiguousarray(reord.T)).astype(BF)
    outproj = prep_fm(np.ascontiguousarray(
        inputs["L0_attn_b_out_proj_w"].T)).astype(BF)
    cw = inputs["L0_attn_b_conv_w"][:, 0, :]
    cwcols = np.zeros((128, 336), np.float32)
    for ch in range(48):
        for dd in range(4):
            cwcols[:, ch * 4 + dd] = cw[ch * 128:(ch + 1) * 128, dd]
        cwcols[:, 192 + ch] = inputs["L0_attn_b_conv_b"][ch * 128:(ch + 1) * 128]
        cwcols[:, 240 + ch] = inputs["L0_attn_b_conv_norm_w"][
            ch * 128:(ch + 1) * 128]
    for ch in range(32):
        cwcols[:, 288 + ch] = inputs["L0_attn_b_out_norm_w"][
            ch * 128:(ch + 1) * 128]
    for ch in range(16):
        cwcols[:, 320 + ch] = inputs["L0_attn_b_norm_w"][
            ch * 128:(ch + 1) * 128]
    dtw = np.ascontiguousarray(inputs["L0_attn_b_dt_w"].T)
    acols = np.zeros((128, 3), np.float32)
    acols[0:64, 0] = inputs["L0_attn_b_dt_b"]
    acols[0:64, 1] = inputs["L0_attn_b_A_log"]
    tri = np.zeros((128, 512), np.float32)
    for sti in range(2):
        for i in range(128):
            tri[i, sti * 256:(sti + 1) * 256] = (
                (sti * 128 + i) <= np.arange(256)).astype(np.float32)
    shared = dict(inproj=inproj, outproj=outproj, cwcols=cwcols, dtw=dtw,
                  acols=acols, tri=tri)
    percore = [dict(flag=np.zeros((128, 1), np.float32)),
               dict(flag=np.ones((128, 1), np.float32))]
    return shared, percore


def build_mamba_nc():
    nc = bass.Bass("TRN2", target_bir_lowering=False, debug=False)
    xr = nc.dram_tensor("xr", [D, TPC], F32, kind="ExternalInput").ap()
    yr = nc.dram_tensor("yr", [D, TPC], F32, kind="ExternalOutput").ap()
    ident = nc.dram_tensor("ident", [128, 128], F32, kind="ExternalInput").ap()
    W = {
        "inproj": nc.dram_tensor("w_inproj", [81, 128, NCH, 128], F32,
                                 kind="ExternalInput").ap(),
        "outproj": nc.dram_tensor("w_outproj", [NCH, 128, 32, 128], F32,
                                  kind="ExternalInput").ap(),
        "cwcols": nc.dram_tensor("w_cwcols", [128, 336], F32,
                                 kind="ExternalInput").ap(),
        "dtw": nc.dram_tensor("w_dtw", [64, 64], F32,
                              kind="ExternalInput").ap(),
        "acols": nc.dram_tensor("w_acols", [128, 3], F32,
                                kind="ExternalInput").ap(),
        "tri": nc.dram_tensor("w_tri", [128, 512], F32,
                              kind="ExternalInput").ap(),
        "flag": nc.dram_tensor("w_flag", [128, 1], F32,
                               kind="ExternalInput").ap(),
    }
    k = K(nc)
    k.setup_consts(ident)
    xres = k.sb_raw("xres", [128, NCH * TPC], F32).ap()
    xn = k.sb_raw("xn", [128, NCH * TPC], BF16).ap()
    scratch = k.sb_raw("lnscr", [128, 8 * 512], F32).ap()
    rows = scratch[0:1, 0:2560]
    xt = emit_load_res_fm(k, xr, xres)
    xnt, scrfree = emit_ln(k, xres, xt, xn, scratch, rows)
    upd = emit_mamba(k, "m", xn, xnt, xres, xt, W, scratch, sf_deps=scrfree)
    st = emit_store_res_fm(k, xres, yr, upd)
    k.em.finalize([st])
    return nc


def run_mamba_launch(inputs, xres_list, mode="hw", sim_cores=2):
    shared, percore = host_prep_mamba(inputs)
    nc = build_mamba_nc()
    ncores = 8 if mode == "hw" else sim_cores
    in_maps = []
    for c in range(ncores):
        m = {"xr": xres_list[c], "ident": _IDENT}
        for kk, v in shared.items():
            m["w_" + kk] = v
        for kk, v in percore[c % 2].items():
            m["w_" + kk] = v
        in_maps.append(m)
    res = run_bass_kernel_spmd(nc, in_maps, core_ids=list(range(8)))
    return [res.results[c]["yr"] for c in range(8)], nc


def build_fused_nc(cfg0, cfg1, cfg2):
    nc = bass.Bass("TRN2", target_bir_lowering=False, debug=False)
    xr = nc.dram_tensor("xr", [D, TPC], F32, kind="ExternalInput").ap()
    yr = nc.dram_tensor("yr", [D, TPC], F32, kind="ExternalOutput").ap()
    ident = nc.dram_tensor("ident", [128, 128], F32, kind="ExternalInput").ap()

    def attn_w(pfx, cfg):
        W = {}
        W["wq"] = nc.dram_tensor(pfx + "wq", [cfg["nqh"], 128, NCH, 128],
                                 BF16, kind="ExternalInput").ap()
        W["wk"] = nc.dram_tensor(pfx + "wk", [cfg["nkvh"], 128, NCH, 128],
                                 BF16, kind="ExternalInput").ap()
        W["wv"] = nc.dram_tensor(pfx + "wv", [cfg["nkvh"], 128, NCH, 128],
                                 BF16, kind="ExternalInput").ap()
        W["wo"] = nc.dram_tensor(pfx + "wo", [NCH, 128, cfg["nqh"], 128],
                                 BF16, kind="ExternalInput").ap()
        if cfg["rope"]:
            W["tabq"] = nc.dram_tensor(pfx + "tabq", [128, 2 * TPC], BF16,
                                       kind="ExternalInput").ap()
            W["tabk"] = nc.dram_tensor(
                pfx + "tabk", [128, 2 * cfg.get("kvw", 2 * TPC)], BF16,
                kind="ExternalInput").ap()
            W["rot"] = nc.dram_tensor(pfx + "rot", [128, 128], BF16,
                                      kind="ExternalInput").ap()
        if cfg["kw"] is not None:
            W["masks"] = nc.dram_tensor(pfx + "masks", [8, 128, cfg["kw"]],
                                        BF16, kind="ExternalInput").ap()
        if cfg.get("bias"):
            W["bcols"] = nc.dram_tensor(pfx + "bcols", [128, 64], F32,
                                        kind="ExternalInput").ap()
        if cfg.get("qknorm"):
            W["nscales"] = nc.dram_tensor(pfx + "nscales", [128, 2], F32,
                                          kind="ExternalInput").ap()
        return W

    def ffn_w(pfx, nI, w3):
        W = {"router": nc.dram_tensor(pfx + "router", [NCH, 128, 8], F32,
                                      kind="ExternalInput").ap(),
             "w2": nc.dram_tensor(pfx + "w2", [3, NCH, 3, 128, nI, 128],
                                  BF16, kind="ExternalInput").ap()}
        if w3:
            W["w1"] = nc.dram_tensor(pfx + "w1", [9, nI, 2, 128, NCH, 128],
                                     BF16, kind="ExternalInput").ap()
            W["w3"] = W["w1"]
        else:
            W["w1"] = nc.dram_tensor(pfx + "w1", [9, nI, 128, NCH, 128],
                                     BF16, kind="ExternalInput").ap()
        return W

    W0a = attn_w("a0_", cfg0)
    Wf0 = ffn_w("f0_", 8, True)
    Wm = {
        "inproj": nc.dram_tensor("m_inproj", [81, 128, NCH, 128], BF16,
                                 kind="ExternalInput").ap(),
        "outproj": nc.dram_tensor("m_outproj", [NCH, 128, 32, 128], BF16,
                                  kind="ExternalInput").ap(),
        "cwcols": nc.dram_tensor("m_cwcols", [128, 336], F32,
                                 kind="ExternalInput").ap(),
        "dtw": nc.dram_tensor("m_dtw", [64, 64], F32,
                              kind="ExternalInput").ap(),
        "acols": nc.dram_tensor("m_acols", [128, 3], F32,
                                kind="ExternalInput").ap(),
        "tri": nc.dram_tensor("m_tri", [128, 512], F32,
                              kind="ExternalInput").ap(),
        "flag": nc.dram_tensor("m_flag", [128, 1], F32,
                               kind="ExternalInput").ap(),
    }
    W1a = attn_w("a1_", cfg1)
    Wf1 = ffn_w("f1_", 11, False)
    W2a = attn_w("a2_", cfg2)

    k = K(nc)
    k.setup_consts(ident)
    xres = k.sb_raw("xres", [128, NCH * TPC], F32).ap()
    xn = k.sb_raw("xn", [128, NCH * TPC], BF16).ap()
    scratch = k.sb_raw("lnscr", [128, 8 * 512], F32).ap()
    rows = scratch[0:1, 0:2560]
    k.mk_arenas(46720, 768)
    xt = emit_load_res_fm(k, xr, xres)
    upd = xt

    def ln():
        return emit_ln(k, xres, upd, xn, scratch, rows)

    xnt, sf = ln()
    upd = emit_attn(k, "a0", cfg0, xn, xnt, xres, upd, W0a,
                    spare_bf=xn, spare_f32=scratch, sf_deps=sf)
    k.phase_barrier()
    xnt, sf = ln()
    upd = emit_ffn(k, "f0a", xn, xnt, xres, upd, Wf0, 1024, False,
                   scratch, sf_deps=sf)
    k.phase_barrier()
    xnt, sf = ln()
    upd = emit_mamba(k, "m", xn, xnt, xres, upd, Wm, scratch, sf_deps=sf)
    k.phase_barrier()
    xnt, sf = ln()
    upd = emit_ffn(k, "f0b", xn, xnt, xres, upd, Wf0, 1024, False,
                   scratch, sf_deps=sf)
    k.phase_barrier()
    xnt, sf = ln()
    upd = emit_attn(k, "a1", cfg1, xn, xnt, xres, upd, W1a,
                    spare_bf=xn, spare_f32=scratch, sf_deps=sf)
    k.phase_barrier()
    xnt, sf = ln()
    upd = emit_ffn(k, "f1a", xn, xnt, xres, upd, Wf1, 1408, True,
                   scratch, sf_deps=sf)
    k.phase_barrier()
    xnt, sf = ln()
    upd = emit_attn(k, "a2", cfg2, xn, xnt, xres, upd, W2a,
                    spare_bf=xn, spare_f32=scratch, sf_deps=sf)
    k.phase_barrier()
    xnt, sf = ln()
    upd = emit_ffn(k, "f1b", xn, xnt, xres, upd, Wf1, 1408, True,
                   scratch, sf_deps=sf)
    k.phase_barrier()
    st = emit_store_res_fm(k, xres, yr, upd)
    k.em.finalize([st])
    return nc


def kernel(**inputs):
    xres = _to_fm_shards(np.ascontiguousarray(inputs["L0_attn_a_x"]))
    cfg0, sh0, pc0 = host_prep_attn("l0a", inputs)
    cfg1, sh1, pc1 = host_prep_attn("l1a", inputs)
    cfg2, sh2, pc2 = host_prep_attn("l1b", inputs)
    Wf0 = host_prep_ffn(0, inputs)
    Wf1 = host_prep_ffn(1, inputs)
    shm, pcm = host_prep_mamba(inputs)
    nc = build_fused_nc(cfg0, cfg1, cfg2)
    in_maps = []
    for c in range(8):
        m = {"xr": xres[c], "ident": _IDENT}
        for kk, v in sh0.items():
            m["a0_" + kk] = v
        for kk, v in pc0[c % 2].items():
            m["a0_" + kk] = v
        for kk, v in sh1.items():
            m["a1_" + kk] = v
        for kk, v in sh2.items():
            m["a2_" + kk] = v
        for kk, v in pc2[c % 2].items():
            m["a2_" + kk] = v
        m["f0_router"] = Wf0["router"]
        m["f0_w1"] = Wf0["w1"]
        m["f0_w2"] = Wf0["w2"]
        m["f1_router"] = Wf1["router"]
        m["f1_w1"] = Wf1["w1"]
        m["f1_w2"] = Wf1["w2"]
        for kk, v in shm.items():
            m["m_" + kk] = v
        for kk, v in pcm[c % 2].items():
            m["m_" + kk] = v
        in_maps.append(m)
    res = run_bass_kernel_spmd(nc, in_maps, core_ids=list(range(8)))
    return _from_fm_shards([res.results[c]["yr"] for c in range(8)])
